# version 01
# baseline (speedup 1.0000x reference)
import numpy as np
import ml_dtypes
import concourse.bass as bass
import concourse.mybir as mybir
from concourse.bass_utils import run_bass_kernel_spmd
from concourse.bass_core_rust import IndirectOffsetOnAxis
from contextlib import ExitStack

F32 = mybir.dt.float32
BF16 = mybir.dt.bfloat16
I32 = mybir.dt.int32
AF = mybir.ActivationFunctionType
ALU = mybir.AluOpType
AX = mybir.AxisListType

NB, NT, D, H, DH, NE, KS, HID = 2, 2048, 1024, 16, 64, 8, 512, 4096
TTC, DCC = NT // 128, D // 128
NEG = -1000.0
BIGF = 65536.0
LN_EPS = 1e-5
SCALE = 8.0
HEADROOM = 60.0
NITER = 48
EVDBG = False
ATTN_DBG = False
ATTN_SERIAL = False
NEGC_ZERO = False
ATTN_QG = 4
ATTN_H = 16
ATTN_WOUT = True
ATTN_NORM = True
ATTN_EXP = True
EXP_FUNC = [AF.Exp]
ATTN_PV = True


class Sem:
    __slots__ = ("h", "n")
    def __init__(self, h):
        self.h = h
        self.n = 0


def build_program(stage=9):
    nc = bass.Bass("TRN2", target_bir_lowering=False, detect_race_conditions=False)

    x_d = nc.dram_tensor("x", [NB, NT, D], F32, kind="ExternalInput").ap()
    gamma_d = nc.dram_tensor("gamma", [1, D], F32, kind="ExternalInput").ap()
    wq_d = nc.dram_tensor("Wq", [D, D], F32, kind="ExternalInput").ap()
    wkv_d = nc.dram_tensor("Wkv", [D, 2 * DH], F32, kind="ExternalInput").ap()
    qs_d = nc.dram_tensor("qsc", [1, DH], F32, kind="ExternalInput").ap()
    ks_d = nc.dram_tensor("ksc", [1, DH], F32, kind="ExternalInput").ap()
    nkv_d = nc.dram_tensor("nullkv", [2, 2, DH], F32, kind="ExternalInput").ap()
    wo_d = nc.dram_tensor("Wout", [D, D], F32, kind="ExternalInput").ap()
    gw_d = nc.dram_tensor("gate_w", [D, NE], F32, kind="ExternalInput").ap()
    w1_d = nc.dram_tensor("w1", [NE, D, HID], F32, kind="ExternalInput").ap()
    w2_d = nc.dram_tensor("w2", [NE, HID, D], F32, kind="ExternalInput").ap()
    out_d = nc.dram_tensor("y", [NB, NT, D], F32, kind="ExternalOutput").ap()
    dbg_d = nc.dram_tensor("dbg", [128, 2048], F32, kind="ExternalOutput").ap()
    xn_s = nc.dram_tensor("xn_scr", [NB, NT, D], BF16, kind="Internal").ap()
    xl_s = nc.dram_tensor("xl_scr", [NT, D], BF16, kind="Internal").ap()
    x2_s = nc.dram_tensor("x2_scr", [NT, D], BF16, kind="Internal").ap()
    qn_s = nc.dram_tensor("qn_scr", [NT, D], BF16, kind="Internal").ap()

    ctx = ExitStack()
    with ctx:
        BFC = 58700
        AB = ctx.enter_context(nc.sbuf_tensor("AB", [128, BFC], BF16))
        bo = [0]
        def bfa(n):
            c = bo[0]; bo[0] += n
            assert bo[0] <= BFC
            return c
        R1 = bfa(16384); R2 = bfa(16384); R3 = bfa(4096); R4 = bfa(8192)
        W8 = bfa(8192); RM = bfa(1280); RON = bfa(128); RQK = bfa(128)
        RPK = bfa(64); RPH = bfa(512); RKV = bfa(1024); RGW = bfa(128)
        RLB = bfa(512); RSQ = bfa(512); RGM = bfa(1024); RKB = bfa(64)

        bvv = lambda c, n: AB[:, c:c + n]
        xnT = bvv(R1, 16384).rearrange("p (c t) -> p c t", c=DCC)
        QT = xnT
        xn2T = bvv(R2, 16384).rearrange("p (c t) -> p c t", c=DCC)
        OT = xn2T
        xnTlo_w = bvv(R3, 4096).rearrange("p (r t) -> p r t", r=2)
        w1f = bvv(R1, 16384).rearrange("p (c f) -> p c f", c=DCC)
        w2g_v = bvv(R2, 8192).rearrange("p (s c d) -> p s c d", s=2, c=4)
        hT_v = bvv(R2 + 8192, 8192).rearrange("p (c s) -> p c s", c=16)
        ringxn = bvv(R4, 2048).rearrange("p (r c) -> p r c", r=2)
        ringqn = ringxn
        ringx2 = bvv(R4 + 2048, 2048).rearrange("p (r c) -> p r c", r=2)
        ringxl = bvv(R4 + 4096, 2048).rearrange("p (r c) -> p r c", r=2)
        XgT_v = bvv(R4, 4096).rearrange("p (c s) -> p c s", c=DCC)
        Msel_v = bvv(R4 + 4096, 2048).rearrange("p (r s) -> p r s", r=4)
        Xg_v = bvv(R4 + 6144, 2048).rearrange("p (s d) -> p s d", s=2)
        Wq_b = bvv(W8, 8192).rearrange("p (c e) -> p c e", c=DCC)
        kT = bvv(W8, 2176)
        Vaug = bvv(W8 + 2176, 17 * 65).rearrange("p (j v) -> p j v", j=17)
        Ev = bvv(W8 + 3296, 2048).rearrange("p (s c) -> p s c", s=2)
        Wo_h = bvv(W8 + 4096, 4096).rearrange("p (c e) -> p c e", c=DCC)
        cm_stdT = bvv(RM, 128)
        cm_q0T = bvv(RM + 128, 128)
        I_bf = bvv(RM + 256, 128)
        cm_padT = bvv(RM + 384, 512)
        cm_full = bvv(RM + 896, 384)
        ones_r = bvv(RON, 128)
        qs_bf = bvv(RQK, 64)
        ks_bf = bvv(RQK + 64, 64)
        pack_v = bvv(RPK, 64).rearrange("p (c k) -> p c k", c=16)
        phi_v = [bvv(RPH, 128).rearrange("p (c e) -> p c e", c=16),
                 bvv(RPH + 256, 128).rearrange("p (c e) -> p c e", c=16)]
        plo_v = [bvv(RPH + 128, 128).rearrange("p (c e) -> p c e", c=16),
                 bvv(RPH + 384, 128).rearrange("p (c e) -> p c e", c=16)]
        Wkv_b = bvv(RKV, 1024).rearrange("p (c e) -> p c e", c=DCC)
        gw_hi = bvv(RGW, 64).rearrange("p (c e) -> p c e", c=DCC)
        gw_lo = bvv(RGW + 64, 64).rearrange("p (c e) -> p c e", c=DCC)
        linv_bf = bvv(RLB, 512)
        sq_bf = bvv(RSQ, 512)
        gamma_bc = bvv(RGM, 1024)
        kb_t = bvv(RKB, 64)

        FC = 10700
        AFp = ctx.enter_context(nc.sbuf_tensor("AFp", [128, FC], F32))
        fo = [0]
        def fa(n):
            c = fo[0]; fo[0] += n
            assert fo[0] <= FC
            return c
        fvv = lambda c, n: AFp[:, c:c + n]
        FXIN = fa(2048); FQ = fa(2048); FWRK = fa(1024); FXNF = fa(1024)
        FST = fa(256); FQS = fa(128); FIOT = fa(512); FCOL = fa(16)
        FDT = fa(256); FMG = fa(64); FNC = fa(16); FIF = fa(128)
        FATT = fa(2048); FOSB = fa(512); FLV = fa(512); FGW = fa(64)
        xin_v = fvv(FXIN, 2048).rearrange("p (r c) -> p r c", r=2)
        qf_v = fvv(FQ, 2048).rearrange("p (r c) -> p r c", r=2)
        gOut = fvv(FXIN, 4096).rearrange("p (s d) -> p s d", s=4)
        wrk = fvv(FWRK, 1024)
        probs_tok = fvv(FWRK, 128).rearrange("p (c e) -> p c e", c=16)
        ext_sb = fvv(FWRK + 128, 512)
        plo_stage = fvv(FWRK + 640, 128).rearrange("p (c e) -> p c e", c=16)
        xnf = fvv(FXNF, 1024)
        stats = fvv(FST, 256).rearrange("p (c k) -> p c k", c=16)
        qs_bc = fvv(FQS, 64)
        ks_bc = fvv(FQS + 64, 64)
        iota_s = fvv(FIOT, 512)
        p_col = fvv(FCOL, 1)
        ge_col = fvv(FCOL + 1, 1)
        tmod_col = fvv(FCOL + 2, 1)
        scr1 = fvv(FCOL + 3, 1)
        scr2 = fvv(FCOL + 4, 1)
        epscol = fvv(FCOL + 5, 1)
        dstT = [fvv(FDT, 128).rearrange("p (c e) -> p c e", c=16),
                fvv(FDT + 128, 128).rearrange("p (c e) -> p c e", c=16)]
        mg_v = fvv(FMG, 16).rearrange("p (s k) -> p s k", s=4)
        negC = fvv(FNC, 1)
        I_f = fvv(FIF, 128)
        att_v = fvv(FATT, 2048).rearrange("p (r c) -> p r c", r=2)
        osb = fvv(FOSB, 512)
        linv = fvv(FLV, 512)
        gwf_st = fvv(FGW, 64).rearrange("p (c e) -> p c e", c=DCC)

        pb = ctx.enter_context(nc.sbuf_tensor("pb", [8, 2 * NT], F32))
        pb_i = pb.ap().bitcast(I32)
        pb_f = pb.ap()
        bscr = ctx.enter_context(nc.sbuf_tensor("bscr", [8, 1024], F32))
        bch = ctx.enter_context(nc.sbuf_tensor("bch", [8, 768], F32))
        lohi = ctx.enter_context(nc.sbuf_tensor("lohi", [8, 16], F32))
        lohi_i = lohi.ap().bitcast(I32)
        theta = ctx.enter_context(nc.sbuf_tensor("theta", [8, 8], F32))
        mcol_i = [ctx.enter_context(nc.sbuf_tensor(f"mcol{k}", [128, 1], I32))
                  for k in range(8)]
        gco = ctx.enter_context(nc.sbuf_tensor("gco", [128, 24], F32))
        nkv_sb = ctx.enter_context(nc.sbuf_tensor("nkv_sb", [2, 4 * DH], F32))
        nkv2 = ctx.enter_context(nc.sbuf_tensor("nkv2", [2, 4 * DH], F32))
        nkb = ctx.enter_context(nc.sbuf_tensor("nkb", [2, 2 * DH], BF16))
        EvDbg = ctx.enter_context(nc.sbuf_tensor("EvDbg", [128, 2048], BF16))
        zbias = ctx.enter_context(nc.sbuf_tensor("zbias", [128, 1], F32))
        USE_EVDBG = [EVDBG]

        psA = ctx.enter_context(nc.psum_tensor("psA", [128, 2048], F32))
        psB = ctx.enter_context(nc.psum_tensor("psB", [128, 2048], F32))

        qSY = Sem(ctx.enter_context(nc.semaphore("qSY")))
        qGP = Sem(ctx.enter_context(nc.semaphore("qGP")))
        cPE = Sem(ctx.enter_context(nc.semaphore("cPE")))
        cDV = Sem(ctx.enter_context(nc.semaphore("cDV")))
        cAC = Sem(ctx.enter_context(nc.semaphore("cAC")))
        cGP = Sem(ctx.enter_context(nc.semaphore("cGP")))

        P = {k: [] for k in ("pe", "dv", "ac", "gp", "sy")}
        max_waited = {}
        last_qwait = {}
        def W(eng, sem, val):
            if val > 0:
                if id(sem) in max_waited:
                    max_waited[id(sem)] = max(max_waited[id(sem)], val)
                P[eng].append(lambda e, s=sem.h, v=val: e.wait_ge(s, v))
        auto_sem = {"dv": cDV, "ac": cAC, "gp": cGP}
        def E(eng, fn, sem=None, k=1):
            if sem is not None and k == 16:
                if id(sem) not in max_waited:
                    max_waited[id(sem)] = 0
                    last_qwait[id(sem)] = 0
                need = max_waited[id(sem)]
                if last_qwait[id(sem)] < need:
                    last_qwait[id(sem)] = need
                    P[eng].append(lambda e, s=sem.h, v=need: e.wait_ge(s, v))
            if sem is None and eng in auto_sem:
                sem = auto_sem[eng]
            if sem is None:
                P[eng].append(fn)
                return 0
            sem.n += k
            P[eng].append(lambda e, f=fn, s=sem.h, kk=k: f(e).then_inc(s, kk))
            if k == 1 and eng in auto_sem and sem is auto_sem[eng]:
                P[eng].append(lambda e, s=sem.h, v=sem.n: e.wait_ge(s, v))
            return sem.n

        ev_init = []
        for b in range(NB):
            E("sy", lambda e, b=b: e.dma_start(out=out_d[b], in_=x_d[b]), qSY, 16)
            ev_init.append(qSY.n)

        E("gp", lambda e: e.iota(iota_s[:], pattern=[[1, 512]], base=0,
                                 channel_multiplier=0,
                                 allow_small_or_imprecise_dtypes=True))
        E("gp", lambda e: e.iota(p_col[:], pattern=[[1, 1]], base=0,
                                 channel_multiplier=1,
                                 allow_small_or_imprecise_dtypes=True))
        E("gp", lambda e: e.memset(cm_full[:], NEG))
        E("gp", lambda e: e.memset(wrk[:, 0:512], 0.0), cGP)
        ev_z = cGP.n
        W("gp", cGP, ev_z)
        E("gp", lambda e: e.affine_select(cm_stdT[:], wrk[:, 0:128],
                                          pattern=[[1, 128]], compare_op=ALU.is_ge,
                                          fill=NEG, base=0, channel_multiplier=-1))
        E("gp", lambda e: e.affine_select(cm_padT[:], wrk[:, 0:512],
                                          pattern=[[0, 512]], compare_op=ALU.is_ge,
                                          fill=NEG, base=-126, channel_multiplier=1),
          cGP)
        ev_gpc = cGP.n

        E("gp", lambda e: e.dma_start(
            out=Wq_b[:], in_=wq_d.rearrange("(c p) e -> p c e", p=128)), qGP, 16)
        ev_wq = [qGP.n, 0]
        E("gp", lambda e: e.dma_start(
            out=Wkv_b[:], in_=wkv_d.rearrange("(c p) e -> p c e", p=128)), qGP, 16)
        ev_wkv = qGP.n
        E("gp", lambda e: e.dma_start(
            out=gw_hi[:], in_=gw_d.rearrange("(c p) e -> p c e", p=128)), qGP, 16)
        E("gp", lambda e: e.dma_start(
            out=gwf_st[:, :, :],
            in_=gw_d.rearrange("(c p) e -> p c e", p=128)), qGP, 16)
        ev_gwf = qGP.n

        E("sy", lambda e: e.dma_start(out=qs_bc[0:1, :], in_=qs_d), qSY, 16)
        E("sy", lambda e: e.dma_start(out=ks_bc[0:1, :], in_=ks_d), qSY, 16)
        E("sy", lambda e: e.dma_start(out=xnf[0:1, :], in_=gamma_d), qSY, 16)
        E("sy", lambda e: e.dma_start(
            out=nkv_sb[:, 0:128].rearrange("b (a d) -> b a d", a=2),
            in_=nkv_d.rearrange("a b d -> b a d")), qSY, 16)
        ev_smallw = qSY.n

        W("dv", qSY, ev_smallw)
        W("dv", cGP, ev_gpc)
        E("dv", lambda e: e.memset(epscol[:], LN_EPS))
        E("dv", lambda e: e.memset(zbias.ap()[:, :], 0.0))
        E("dv", lambda e: e.memset(ones_r[:], 1.0))
        E("dv", lambda e: e.tensor_scalar(I_f[:], iota_s[:, 0:128], p_col[:], None,
                                          ALU.is_equal))
        E("dv", lambda e: e.tensor_copy(I_bf[:], I_f[:]))
        E("dv", lambda e: e.tensor_copy(cm_q0T[:], cm_stdT[:]))
        E("dv", lambda e: e.memset(cm_q0T[0:64, 0:64], 0.0))
        E("dv", lambda e: e.tensor_scalar(ge_col[:], p_col[:], 64.0, None,
                                          ALU.is_ge))
        E("dv", lambda e: e.scalar_tensor_tensor(tmod_col[:], ge_col[:], -64.0,
                                                 p_col[:], ALU.mult, ALU.add))
        for tc in range(16):
            E("dv", lambda e, tc=tc: e.tensor_scalar(
                pack_v[:, tc, 0:1], ge_col[:], 1.0, 2.0 * tc, ALU.mult, ALU.add))
            E("dv", lambda e, tc=tc: e.tensor_copy(pack_v[:, tc, 1:2], tmod_col[:]))
        E("dv", lambda e: e.tensor_copy(qs_bf[0:1, :], qs_bc[0:1, :]))
        E("dv", lambda e: e.tensor_copy(ks_bf[0:1, :], ks_bc[0:1, :]))
        E("dv", lambda e: e.tensor_copy(linv_bf[0:1, :], xnf[0:1, 0:512]))
        E("dv", lambda e: e.tensor_copy(sq_bf[0:1, :], xnf[0:1, 512:1024]))
        E("dv", lambda e: e.memset(bch[:, 0:128], 0.0))
        E("dv", lambda e: e.memset(bch[:, 128:256], BIGF), cDV)
        ev_bcrows = cDV.n
        W("dv", qGP, ev_gwf)
        E("dv", lambda e: e.tensor_sub(
            wrk[:, 0:64].rearrange("p (c e) -> p c e", c=DCC), gwf_st, gw_hi[:]))
        E("dv", lambda e: e.tensor_copy(
            gw_lo[:], wrk[:, 0:64].rearrange("p (c e) -> p c e", c=DCC)), cDV)
        ev_gwlo = cDV.n

        W("pe", cDV, ev_bcrows)
        E("pe", lambda e: e.matmul(psA[:, 0:64], lhsT=ones_r[0:1, :],
                                   rhs=qs_bf[0:1, :], start=True, stop=True))
        E("pe", lambda e: e.matmul(psA[:, 64:128], lhsT=ones_r[0:1, :],
                                   rhs=ks_bf[0:1, :], start=True, stop=True))
        E("pe", lambda e: e.matmul(psA[:, 512:1024], lhsT=ones_r[0:1, :],
                                   rhs=linv_bf[0:1, :], start=True, stop=True))
        E("pe", lambda e: e.matmul(psA[:, 1024:1536], lhsT=ones_r[0:1, :],
                                   rhs=sq_bf[0:1, :], start=True, stop=True), cPE)
        ev_bcmm = cPE.n
        W("dv", cPE, ev_bcmm)
        E("dv", lambda e: e.tensor_copy(qs_bc[:], psA[:, 0:64]))
        E("dv", lambda e: e.tensor_copy(ks_bc[:], psA[:, 64:128]))
        E("dv", lambda e: e.tensor_copy(gamma_bc[:, 0:512], psA[:, 512:1024]))
        E("dv", lambda e: e.tensor_copy(gamma_bc[:, 512:1024], psA[:, 1024:1536]))
        E("dv", lambda e: e.tensor_reduce(scr1[:], qs_bc[:], AX.X, ALU.max,
                                          apply_absolute_value=True))
        E("dv", lambda e: e.tensor_reduce(scr2[:], ks_bc[:], AX.X, ALU.max,
                                          apply_absolute_value=True))
        E("dv", lambda e: e.tensor_mul(negC[:], scr1[:], scr2[:]))
        E("dv", lambda e: e.tensor_scalar(negC[:], negC[:], -SCALE, HEADROOM,
                                          ALU.mult, ALU.add))
        E("dv", lambda e: e.tensor_scalar_min(negC[:], negC[:], 0.0), cDV)
        ev_bcast = cDV.n

        E("dv", lambda e: e.tensor_mul(nkv2[0:2, 0:64], nkv_sb[0:2, 0:64],
                                       nkv_sb[0:2, 0:64]))
        E("dv", lambda e: e.reduce_sum(nkv2[0:2, 64:65], nkv2[0:2, 0:64], AX.X),
          cDV)
        ev_nk1 = cDV.n
        W("ac", cDV, ev_nk1)
        E("ac", lambda e: e.activation(nkv2[0:2, 65:66], nkv2[0:2, 64:65], AF.Ln))
        E("ac", lambda e: e.activation(nkv2[0:2, 66:67], nkv2[0:2, 65:66], AF.Exp,
                                       scale=-0.5), cAC)
        ev_nk2 = cAC.n
        W("dv", cAC, ev_nk2)
        E("dv", lambda e: e.tensor_scalar_min(nkv2[0:2, 66:67], nkv2[0:2, 66:67],
                                              1e12))
        E("dv", lambda e: e.tensor_scalar(nkv2[0:2, 0:64], nkv_sb[0:2, 0:64],
                                          nkv2[0:2, 66:67], None, ALU.mult))
        E("dv", lambda e: e.tensor_mul(nkb[0:2, 0:64], nkv2[0:2, 0:64],
                                       ks_bc[0:2, 0:64]))
        E("dv", lambda e: e.tensor_copy(nkb[0:2, 64:128], nkv_sb[0:2, 64:128]),
          cDV)
        ev_nkb = cDV.n

        ev_pb = [0, 0]
        ev_phi = [0, 0]
        ev_attn_pe = [0, 0]
        ev_wout_add = [0, 0]

        def p1(b, ev_prev_pe):
            st_hist = []
            for tt in range(TTC):
                r = tt % 2
                if len(st_hist) >= 2:
                    W("sy", qSY, st_hist[-2])
                    W("dv", qSY, st_hist[-2])
                E("sy", lambda e, b=b, tt=tt, r=r: e.dma_start(
                    out=xin_v[:, r, :], in_=x_d[b, tt * 128:(tt + 1) * 128, :]),
                  qSY, 16)
                evx = qSY.n
                W("dv", qSY, evx)
                E("dv", lambda e, tt=tt, r=r: e.reduce_sum(
                    stats[:, tt, 0:1], xin_v[:, r, :], AX.X, negate=True))
                E("dv", lambda e, tt=tt: e.tensor_scalar_mul(
                    stats[:, tt, 0:1], stats[:, tt, 0:1], 1.0 / D))
                E("dv", lambda e, tt=tt, r=r: e.tensor_scalar_add(
                    wrk[:], xin_v[:, r, :], stats[:, tt, 0:1]), cDV)
                ev_cen = cDV.n
                E("dv", lambda e, r=r: e.tensor_mul(xin_v[:, r, :], wrk[:],
                                                    wrk[:]))
                E("dv", lambda e, tt=tt, r=r: e.reduce_sum(
                    stats[:, tt, 1:2], xin_v[:, r, :], AX.X), cDV)
                ev_ss = cDV.n
                W("ac", cDV, ev_ss)
                E("ac", lambda e, tt=tt: e.activation(
                    stats[:, tt, 3:4], stats[:, tt, 1:2], AF.Ln, scale=1.0 / D,
                    bias=epscol[:]))
                E("ac", lambda e, tt=tt: e.activation(
                    stats[:, tt, 3:4], stats[:, tt, 3:4], AF.Exp, scale=-0.5), cAC)
                ev_r1 = cAC.n
                W("dv", cAC, ev_r1)
                E("dv", lambda e, tt=tt: e.tensor_scalar(
                    xnf[:], wrk[:], stats[:, tt, 3:4], None, ALU.mult))
                E("dv", lambda e, r=r: e.tensor_copy(ringxn[:, r, :], xnf[:]))
                E("dv", lambda e, r=r: e.tensor_sub(
                    ringxl[:, r, :], xnf[:], ringxn[:, r, :]))
                E("dv", lambda e, tt=tt: e.reduce_sum(
                    stats[:, tt, 4:5], xnf[:], AX.X, negate=True))
                E("dv", lambda e, tt=tt: e.tensor_scalar_mul(
                    stats[:, tt, 4:5], stats[:, tt, 4:5], 1.0 / D))
                E("dv", lambda e, tt=tt: e.tensor_scalar_add(
                    wrk[:], xnf[:], stats[:, tt, 4:5]), cDV)
                ev_cen2 = cDV.n
                E("dv", lambda e, r=r: e.tensor_mul(xin_v[:, r, :], wrk[:],
                                                    wrk[:]))
                E("dv", lambda e, tt=tt, r=r: e.reduce_sum(
                    stats[:, tt, 5:6], xin_v[:, r, :], AX.X), cDV)
                ev_ss2 = cDV.n
                W("ac", cDV, ev_ss2)
                E("ac", lambda e, tt=tt: e.activation(
                    stats[:, tt, 7:8], stats[:, tt, 5:6], AF.Ln, scale=1.0 / D,
                    bias=epscol[:]))
                E("ac", lambda e, tt=tt: e.activation(
                    stats[:, tt, 7:8], stats[:, tt, 7:8], AF.Exp, scale=-0.5), cAC)
                ev_r2 = cAC.n
                W("dv", cAC, ev_r2)
                E("dv", lambda e, tt=tt, r=r: e.scalar_tensor_tensor(
                    ringx2[:, r, :], wrk[:], stats[:, tt, 7:8], gamma_bc[:],
                    ALU.mult, ALU.mult), cDV)
                ev_tile = cDV.n
                W("sy", cDV, ev_tile)
                E("sy", lambda e, b=b, tt=tt, r=r: e.dma_start(
                    out=xn_s[b, tt * 128:(tt + 1) * 128, :], in_=ringxn[:, r, :]),
                  qSY, 16)
                E("sy", lambda e, tt=tt, r=r: e.dma_start(
                    out=xl_s[tt * 128:(tt + 1) * 128, :], in_=ringxl[:, r, :]),
                  qSY, 16)
                E("sy", lambda e, tt=tt, r=r: e.dma_start(
                    out=x2_s[tt * 128:(tt + 1) * 128, :], in_=ringx2[:, r, :]),
                  qSY, 16)
                st_hist.append(qSY.n)
            ev_stores = qSY.n

            W("sy", cPE, ev_prev_pe)
            W("sy", qSY, ev_stores)
            for dc in range(DCC):
                E("sy", lambda e, b=b, dc=dc: e.dma_start(
                    out=xnT[:, dc, :], in_=xn_s[b][:, dc * 128:(dc + 1) * 128],
                    transpose=True), qSY, 16)
                E("sy", lambda e, dc=dc: e.dma_start(
                    out=xn2T[:, dc, :], in_=x2_s[:, dc * 128:(dc + 1) * 128],
                    transpose=True), qSY, 16)
            ev_T = qSY.n

            if b == 1:
                W("gp", cPE, ev_prev_pe)
                E("gp", lambda e: e.dma_start(
                    out=Wq_b[:], in_=wq_d.rearrange("(c p) e -> p c e", p=128)),
                  qGP, 16)
                ev_wq[1] = qGP.n

            W("pe", cDV, cDV.n)
            W("pe", qSY, ev_T)
            W("pe", qGP, ev_wq[b])
            if b == 1:
                W("pe", cPE, ev_prev_pe)
            ev_qevac_hist = [0, 0]
            ev_qn_last = 0
            for tt in range(TTC):
                r = tt % 2
                if tt >= 2:
                    W("pe", cDV, ev_qevac_hist[0])
                for half in range(2):
                    for dc in range(DCC):
                        E("pe", lambda e, tt=tt, half=half, dc=dc: e.matmul(
                            psA[:, (tt % 2) * 1024 + half * 512:
                                (tt % 2) * 1024 + half * 512 + 512],
                            lhsT=xn2T[:, dc, tt * 128:(tt + 1) * 128],
                            rhs=Wq_b[:, dc, half * 512:(half + 1) * 512],
                            start=(dc == 0), stop=(dc == DCC - 1)),
                          cPE if (dc == DCC - 1 and half == 1) else None)
                ev_qpe = cPE.n
                W("dv", cPE, ev_qpe)
                if tt >= 2:
                    W("dv", qSY, ev_qn_last - 16)
                E("dv", lambda e, tt=tt, r=r: e.tensor_copy(
                    qf_v[:, r, :], psA[:, (tt % 2) * 1024:(tt % 2) * 1024 + 1024]),
                  cDV)
                ev_qevac = cDV.n
                ev_qevac_hist = [ev_qevac_hist[1], ev_qevac]
                E("dv", lambda e, r=r: e.tensor_mul(wrk[:], qf_v[:, r, :],
                                                    qf_v[:, r, :]))
                E("dv", lambda e: e.reduce_sum(
                    gco[:, 0:16], wrk[:].rearrange("p (h d) -> p h d", h=16),
                    AX.X), cDV)
                ev_ssq = cDV.n
                W("ac", cDV, ev_ssq)
                E("ac", lambda e: e.activation(gco[:, 0:16], gco[:, 0:16], AF.Ln))
                E("ac", lambda e: e.activation(gco[:, 0:16], gco[:, 0:16], AF.Exp,
                                               scale=-0.5), cAC)
                ev_rn = cAC.n
                W("dv", cAC, ev_rn)
                E("dv", lambda e: e.tensor_scalar_min(gco[:, 0:16], gco[:, 0:16],
                                                      1e12))
                for h in range(H):
                    E("dv", lambda e, r=r, h=h: e.scalar_tensor_tensor(
                        ringqn[:, r, h * 64:(h + 1) * 64],
                        qf_v[:, r, h * 64:(h + 1) * 64], gco[:, h:h + 1],
                        qs_bc[:], ALU.mult, ALU.mult),
                      cDV if h == H - 1 else None)
                ev_qn = cDV.n
                W("sy", cDV, ev_qn)
                E("sy", lambda e, tt=tt, r=r: e.dma_start(
                    out=qn_s[tt * 128:(tt + 1) * 128, :], in_=ringqn[:, r, :]),
                  qSY, 16)
                ev_qn_last = qSY.n
            ev_qproj_pe = cPE.n

            W("dv", cPE, ev_qproj_pe)
            W("dv", cDV, ev_nkb)
            E("dv", lambda e: e.memset(Vaug[:, :, :], 0.0))
            E("dv", lambda e: e.memset(Vaug[:, :, 64:65], 1.0))
            E("dv", lambda e: e.memset(kT[:, 0:128], 0.0), cDV)
            ev_vprep = cDV.n
            W("pe", cDV, ev_vprep)
            E("pe", lambda e: e.matmul(psB[0:64, 0:2], lhsT=nkb[0:2, 0:64],
                                       rhs=I_bf[0:2, 0:2], start=True, stop=True),
              cPE)
            ev_nkt = cPE.n
            W("dv", cPE, ev_nkt)
            E("dv", lambda e: e.tensor_copy(kT[0:64, 126:128], psB[0:64, 0:2]))
            E("dv", lambda e: e.tensor_copy(kT[64:128, 126:128], psB[0:64, 0:2]),
              cDV)
            ev_nkT = cDV.n
            W("sy", cDV, ev_vprep)
            W("sy", qSY, qSY.n)
            E("sy", lambda e: e.dma_start(out=Vaug[126:128, 0, 0:64],
                                          in_=nkb[0:2, 64:128]), qSY, 16)
            ev_nullv = qSY.n
            W("sy", qSY, qSY.n)

            W("pe", qGP, ev_wkv)
            W("pe", cDV, cDV.n)
            kv_hist = []
            for tt in range(TTC):
                sl = tt % 2
                if len(kv_hist) >= 2:
                    W("pe", cDV, kv_hist[-2])
                for dc in range(DCC):
                    E("pe", lambda e, tt=tt, dc=dc, sl=sl: e.matmul(
                        psA[:, 1024 + sl * 512:1024 + sl * 512 + 128],
                        lhsT=xnT[:, dc, tt * 128:(tt + 1) * 128],
                        rhs=Wkv_b[:, dc, :], start=(dc == 0),
                        stop=(dc == DCC - 1)), cPE if dc == DCC - 1 else None)
                ev_kvpe = cPE.n
                W("dv", cPE, ev_kvpe)
                E("dv", lambda e, tt=tt, sl=sl: e.tensor_copy(
                    Vaug[:, tt + 1, 0:64],
                    psA[:, 1024 + sl * 512 + 64:1024 + sl * 512 + 128]))
                E("dv", lambda e, sl=sl: e.tensor_copy(
                    wrk[:, 0:64], psA[:, 1024 + sl * 512:1024 + sl * 512 + 64]))
                E("dv", lambda e: e.tensor_mul(wrk[:, 64:128], wrk[:, 0:64],
                                               wrk[:, 0:64]))
                E("dv", lambda e: e.reduce_sum(scr1[:], wrk[:, 64:128], AX.X),
                  cDV)
                ev_kss = cDV.n
                W("ac", cDV, ev_kss)
                E("ac", lambda e: e.activation(scr2[:], scr1[:], AF.Ln))
                E("ac", lambda e: e.activation(scr2[:], scr2[:], AF.Exp,
                                               scale=-0.5), cAC)
                ev_krn = cAC.n
                W("dv", cAC, ev_krn)
                E("dv", lambda e: e.tensor_scalar_min(scr2[:], scr2[:], 1e12))
                E("dv", lambda e: e.scalar_tensor_tensor(
                    kb_t[:], wrk[:, 0:64], scr2[:], ks_bc[:], ALU.mult, ALU.mult),
                  cDV)
                ev_kbt = cDV.n
                kv_hist.append(ev_kbt)
                W("pe", cDV, ev_kbt)
                E("pe", lambda e, tt=tt: e.matmul(
                    psB[0:64, (tt % 2) * 512:(tt % 2) * 512 + 128],
                    lhsT=kb_t[:, 0:64], rhs=I_bf[:, 0:128], start=True,
                    stop=True), cPE)
                ev_ktp = cPE.n
                W("dv", cPE, ev_ktp)
                E("dv", lambda e, tt=tt: e.tensor_copy(
                    kT[0:64, 128 + tt * 128:128 + (tt + 1) * 128],
                    psB[0:64, (tt % 2) * 512:(tt % 2) * 512 + 128]))
                E("dv", lambda e, tt=tt: e.tensor_copy(
                    kT[64:128, 128 + tt * 128:128 + (tt + 1) * 128],
                    psB[0:64, (tt % 2) * 512:(tt % 2) * 512 + 128]),
                  cDV)
            ev_kdone_dv = cDV.n
            ev_kv_pe = cPE.n

            W("pe", cDV, ev_gwlo)
            W("pe", cDV, ev_kdone_dv)
            ev_lo_pc = [0] * DCC
            ev_gate_dc = [0] * DCC
            for dc in range(2):
                E("sy", lambda e, dc=dc: e.dma_start(
                    out=xnTlo_w[:, dc % 2, :],
                    in_=xl_s[:, dc * 128:(dc + 1) * 128], transpose=True),
                  qSY, 16)
                ev_lo_pc[dc] = qSY.n
            for dc in range(DCC):
                W("pe", qSY, ev_lo_pc[dc])
                for t4 in range(4):
                    E("pe", lambda e, dc=dc, t4=t4: e.matmul(
                        psB[0:8, t4 * 512:(t4 + 1) * 512], lhsT=gw_hi[:, dc, :],
                        rhs=xnT[:, dc, t4 * 512:(t4 + 1) * 512],
                        start=(dc == 0), stop=False))
                    E("pe", lambda e, dc=dc, t4=t4: e.matmul(
                        psB[0:8, t4 * 512:(t4 + 1) * 512], lhsT=gw_lo[:, dc, :],
                        rhs=xnT[:, dc, t4 * 512:(t4 + 1) * 512],
                        start=False, stop=False))
                    E("pe", lambda e, dc=dc, t4=t4: e.matmul(
                        psB[0:8, t4 * 512:(t4 + 1) * 512], lhsT=gw_hi[:, dc, :],
                        rhs=xnTlo_w[:, dc % 2, t4 * 512:(t4 + 1) * 512],
                        start=False, stop=(dc == DCC - 1)),
                      cPE)
                ev_gate_dc[dc] = cPE.n
                if dc + 2 < DCC:
                    W("sy", cPE, ev_gate_dc[dc])
                    E("sy", lambda e, dc=dc: e.dma_start(
                        out=xnTlo_w[:, dc % 2, :],
                        in_=xl_s[:, (dc + 2) * 128:(dc + 3) * 128],
                        transpose=True), qSY, 16)
                    ev_lo_pc[dc + 2] = qSY.n
            ev_gate_pe = cPE.n
            W("dv", cPE, ev_gate_pe)
            E("dv", lambda e, b=b: e.tensor_copy(pb_f[:, b * NT:(b + 1) * NT],
                                                 psB[0:8, :]), cDV)
            ev_scores = cDV.n

            W("pe", cDV, ev_scores)
            for tt in range(TTC):
                E("pe", lambda e, b=b, tt=tt: e.matmul(
                    psA[:, tt * 8:(tt + 1) * 8],
                    lhsT=pb_f[0:8, b * NT + tt * 128:b * NT + (tt + 1) * 128],
                    rhs=I_f[0:8, 0:8], start=True, stop=True),
                  cPE if tt == TTC - 1 else None)
            ev_sct = cPE.n
            W("dv", cPE, ev_sct)
            E("dv", lambda e: e.tensor_copy(
                probs_tok[:, :, :].rearrange("p c e -> p (c e)"), psA[:, 0:128]))
            E("dv", lambda e: e.tensor_reduce(
                gco[:, 0:16], probs_tok[:, :, :], AX.X, ALU.max, negate=True),
          cDV)
            ev_negmax = cDV.n
            W("ac", cDV, ev_negmax)
            for tt in range(TTC):
                E("ac", lambda e, tt=tt: e.activation(
                    probs_tok[:, tt, :], probs_tok[:, tt, :], AF.Exp,
                    bias=gco[:, tt:tt + 1]), cAC)
                ev_e = cAC.n
                W("dv", cAC, ev_e)
                E("dv", lambda e, tt=tt: e.reduce_sum(
                    stats[:, tt, 8:9], probs_tok[:, tt, :], AX.X))
                E("dv", lambda e, tt=tt: e.reciprocal(scr1[:], stats[:, tt, 8:9]))
                E("dv", lambda e, tt=tt: e.tensor_scalar(
                    probs_tok[:, tt, :], probs_tok[:, tt, :], scr1[:], None,
                    ALU.mult), cDV)
            ev_probs = cDV.n
            E("dv", lambda e, b=b: e.tensor_copy(phi_v[b][:, :, :],
                                                 probs_tok[:, :, :]))
            E("dv", lambda e, b=b: e.tensor_sub(plo_stage[:, :, :],
                                                probs_tok[:, :, :],
                                                phi_v[b][:, :, :]))
            E("dv", lambda e, b=b: e.tensor_copy(plo_v[b][:, :, :],
                                                 plo_stage[:, :, :]), cDV)
            ev_phi[b] = cDV.n
            W("pe", cDV, ev_probs)
            for tt in range(TTC):
                E("pe", lambda e, tt=tt: e.matmul(
                    psB[0:8, tt * 128:(tt + 1) * 128], lhsT=probs_tok[:, tt, :],
                    rhs=I_f[:, 0:128], start=True, stop=True),
                  cPE if tt == TTC - 1 else None)
            ev_pbt = cPE.n
            W("dv", cPE, ev_pbt)
            E("dv", lambda e, b=b: e.tensor_copy(pb_f[:, b * NT:(b + 1) * NT],
                                                 psB[0:8, :]), cDV)
            ev_pb[b] = cDV.n

            W("sy", qSY, ev_qn_last)
            W("sy", cPE, max(ev_kv_pe, ev_gate_pe))
            for dc in range(DCC):
                E("sy", lambda e, dc=dc: e.dma_start(
                    out=QT[:, dc, :], in_=qn_s[:, dc * 128:(dc + 1) * 128],
                    transpose=True), qSY, 16)
            ev_qt = qSY.n
            if ATTN_DBG == 3 and b == 0:
                W("gp", qSY, ev_qt)
                E("gp", lambda e: e.dma_start(
                    out=dbg_d[:, 0:1024], in_=qn_s[0:128, 0:1024]), qGP, 16)
                E("gp", lambda e: e.dma_start(
                    out=dbg_d[:, 1024:2048],
                    in_=xn_s[0][0:128, 0:1024]), qGP, 16)
            return ev_qt, max(ev_kdone_dv, ev_nkT), ev_nullv

        def attn_core(b, ev_qt, ev_kv_dv, ev_nullv):
            W("pe", cDV, cDV.n)
            W("pe", cAC, cAC.n)
            W("pe", qSY, ev_qt)
            W("pe", qSY, ev_nullv)
            W("pe", cDV, ev_kv_dv)
            W("ac", cDV, ev_bcast)
            slot_exp = [0, 0, 0, 0]
            eslot_pe = [0, 0]
            oslot_dv = [0, 0]
            bslot_dv = [0, 0]
            for qg in range(ATTN_QG):
                jts = 4 * qg + 5
                npairs = (jts + 1) // 2
                for h in range(ATTN_H):
                    hq = QT[(h % 2) * 64:(h % 2) * 64 + 64, h // 2,
                            qg * 512:(qg + 1) * 512]
                    so = h % 2
                    sb = 2 + (h % 2)

                    def emit_spair(pr):
                        plan_ops = []
                        for jt in (2 * pr, 2 * pr + 1):
                            if jt >= jts:
                                continue
                            sl = jt % 4
                            kb0 = (h % 2) * 64
                            dt_ = jt - 1 - 4 * qg
                            has_mask = (jt == 0) or (0 <= dt_ <= 3)
                            plan_ops.append(("w", sl))
                            plan_ops.append(lambda e, jt=jt, sl=sl, hq=hq,
                                            kb0=kb0, hm=has_mask: e.matmul(
                                psA[:, sl * 512:(sl + 1) * 512],
                                lhsT=kT[kb0:kb0 + 64,
                                        jt * 128:(jt + 1) * 128], rhs=hq,
                                start=True, stop=not hm))
                            if jt == 0:
                                plan_ops.append(lambda e, sl=sl: e.matmul(
                                    psA[:, sl * 512:(sl + 1) * 512],
                                    lhsT=I_bf[:], rhs=cm_padT[:], start=False,
                                    stop=True))
                            if 0 <= dt_ <= 3:
                                cmt = cm_q0T if (qg == 0 and dt_ == 0) else cm_stdT
                                if dt_ > 0:
                                    plan_ops.append(
                                        lambda e, sl=sl, dt_=dt_: e.matmul(
                                            psA[:, sl * 512:
                                                sl * 512 + dt_ * 128],
                                            lhsT=I_bf[:],
                                            rhs=cm_full[:, 0:dt_ * 128],
                                            start=False, stop=False))
                                plan_ops.append(
                                    lambda e, sl=sl, dt_=dt_, cmt=cmt: e.matmul(
                                        psA[:, sl * 512 + dt_ * 128:
                                            sl * 512 + dt_ * 128 + 128],
                                        lhsT=I_bf[:], rhs=cmt[:], start=False,
                                        stop=True))
                        ret = 0
                        for i, op in enumerate(plan_ops):
                            if isinstance(op, tuple):
                                W("pe", cAC, slot_exp[op[1]])
                                continue
                            last = all(isinstance(o, tuple)
                                       for o in plan_ops[i + 1:])
                            ret = E("pe", op, cPE if last else None)
                        return ret

                    def emit_exp(pr, ev_sp):
                        par = pr % 2
                        jt0 = 2 * pr
                        w = 1024 if jt0 + 1 < jts else 512
                        sl0 = jt0 % 4
                        W("ac", cPE, ev_sp)
                        W("ac", cPE, eslot_pe[par])
                        E("ac", lambda e, sl0=sl0, par=par: e.activation(
                            (EvDbg.ap().rearrange("p (s c) -> p s c", s=2)
                             if USE_EVDBG[0] else Ev)[:, par, 0:512],
                            psA[:, sl0 * 512:sl0 * 512 + 512],
                            EXP_FUNC[0],
                            bias=(zbias.ap()[:, :] if NEGC_ZERO else negC[:]),
                            scale=SCALE),
                          cAC if w == 512 else None)
                        if w == 1024:
                            E("ac", lambda e, sl0=sl0, par=par: e.activation(
                                Ev[:, par, 512:1024],
                                psA[:, sl0 * 512 + 512:sl0 * 512 + 1024],
                                EXP_FUNC[0], bias=negC[:], scale=SCALE), cAC)
                        slot_exp[sl0] = cAC.n
                        if w == 1024:
                            slot_exp[(sl0 + 1) % 4] = cAC.n
                        return cAC.n

                    def emit_pv(pr, ev_ex):
                        par = pr % 2
                        W("pe", cAC, ev_ex)
                        if pr == 0:
                            W("pe", cDV, oslot_dv[so])
                        last_inc = 0
                        for jt in (2 * pr, 2 * pr + 1):
                            if jt >= jts:
                                continue
                            E("pe", lambda e, jt=jt, par=par, so=so, jts=jts:
                              e.matmul(
                                psB[0:65, so * 512:so * 512 + 512],
                                lhsT=Vaug[:, jt, :],
                                rhs=Ev[:, par, (jt % 2) * 512:(jt % 2) * 512 + 512],
                                start=(jt == 0), stop=(jt == jts - 1)),
                              cPE)
                            last_inc = cPE.n
                        eslot_pe[par] = last_inc
                        return last_inc

                    ev_sp = [0] * npairs
                    ev_ex = [0] * npairs
                    if ATTN_SERIAL:
                        for pr in range(npairs):
                            ev_sp[pr] = emit_spair(pr)
                        for pr in range(npairs):
                            if not ATTN_EXP:
                                continue
                            ev_ex[pr] = emit_exp(pr, ev_sp[npairs - 1])
                            if ATTN_PV:
                                ev_pv = emit_pv(pr, ev_ex[pr])
                    else:
                        ev_sp[0] = emit_spair(0)
                        for pr in range(npairs):
                            if pr + 1 < npairs:
                                ev_sp[pr + 1] = emit_spair(pr + 1)
                            if not ATTN_EXP:
                                slot_exp[(2 * pr) % 4] = 0
                                continue
                            if ATTN_EXP == 2:
                                E("ac", lambda e, pr=pr: e.activation(
                                    Ev[:, pr % 2, 0:512],
                                    psA[:, ((2 * pr) % 4) * 512:
                                        ((2 * pr) % 4) * 512 + 512],
                                    AF.Identity), cAC)
                                slot_exp[(2 * pr) % 4] = 0
                                continue
                            ev_ex[pr] = emit_exp(pr, ev_sp[pr])
                            if ATTN_PV:
                                ev_pv = emit_pv(pr, ev_ex[pr])
                    if not (ATTN_NORM and ATTN_EXP and ATTN_PV):
                        continue
                    W("dv", cPE, ev_pv)
                    E("dv", lambda e, so=so: e.reciprocal(
                        linv[0:1, :], psB[64:65, so * 512:so * 512 + 512]))
                    E("dv", lambda e: e.tensor_copy(linv_bf[0:1, :],
                                                    linv[0:1, :]), cDV)
                    ev_linv = cDV.n
                    if ATTN_DBG == 2 and b == 0 and h == 0 and qg == 0:
                        W("gp", cDV, cDV.n)
                        W("gp", qSY, ev_qt)
                        E("gp", lambda e: e.dma_start(
                            out=dbg_d[:, 0:1024], in_=QT[:, 0, 0:1024]), qGP, 16)
                        E("gp", lambda e: e.dma_start(
                            out=dbg_d[:, 1024:2048], in_=kT[:, 0:1024]), qGP, 16)
                    if ATTN_DBG and b == 0 and h == 0:
                        E("dv", lambda e, so=so, qg=qg: e.tensor_copy(
                            AFp[0:1, FXIN + 1024 + qg * 512:
                                FXIN + 1536 + qg * 512],
                            psB[64:65, so * 512:so * 512 + 512]), cDV)
                        evd = cDV.n
                        W("sy", cDV, evd)
                        E("sy", lambda e, qg=qg: e.dma_start(
                            out=dbg_d[0:1, qg * 512:(qg + 1) * 512],
                            in_=AFp[0:1, FXIN + 1024 + qg * 512:
                                    FXIN + 1536 + qg * 512]), qSY, 16)
                    W("pe", cDV, bslot_dv[sb - 2])
                    W("pe", cDV, ev_linv)
                    E("pe", lambda e, sb=sb: e.matmul(
                        psB[0:64, sb * 512:sb * 512 + 512],
                        lhsT=ones_r[0:1, 0:64], rhs=linv_bf[0:1, :],
                        start=True, stop=True), cPE)
                    ev_bc = cPE.n
                    W("dv", cPE, ev_bc)
                    E("dv", lambda e, so=so: e.tensor_copy(
                        osb[0:64, :], psB[0:64, so * 512:so * 512 + 512]), cDV)
                    oslot_dv[so] = cDV.n
                    E("dv", lambda e, h=h, qg=qg, sb=sb: e.tensor_tensor(
                        OT[(h % 2) * 64:(h % 2) * 64 + 64, h // 2,
                           qg * 512:(qg + 1) * 512],
                        osb[0:64, :], psB[0:64, sb * 512:sb * 512 + 512],
                        ALU.mult), cDV)
                    bslot_dv[sb - 2] = cDV.n
            ev_ot_dv = cDV.n
            ev_core_pe = cPE.n

            ev_att_ring = [0, 0]
            wo_slot_ev = [0, 0, 0, 0]
            for dh in range(2 if ATTN_WOUT else 0):
                W("gp", cPE, ev_core_pe)
                E("gp", lambda e, dh=dh: e.dma_start(
                    out=Wo_h[:], in_=wo_d.rearrange("(c p) e -> p c e", p=128)[
                        :, :, dh * 512:(dh + 1) * 512]), qGP, 16)
                ev_wo = qGP.n
                W("pe", qGP, ev_wo)
                W("pe", cDV, ev_ot_dv)
                for tt in range(TTC):
                    sl = tt % 4
                    if wo_slot_ev[sl]:
                        W("pe", cDV, wo_slot_ev[sl])
                    for ec in range(DCC):
                        E("pe", lambda e, tt=tt, ec=ec, sl=sl: e.matmul(
                            psA[:, sl * 512:(sl + 1) * 512],
                            lhsT=OT[:, ec, tt * 128:(tt + 1) * 128],
                            rhs=Wo_h[:, ec, :], start=(ec == 0),
                            stop=(ec == DCC - 1)),
                          cPE if ec == DCC - 1 else None)
                    ev_wpe = cPE.n
                    r = tt % 2
                    W("dv", cPE, ev_wpe)
                    if ev_att_ring[r]:
                        W("dv", qGP, ev_att_ring[r])
                    E("dv", lambda e, dh=dh, sl=sl, r=r: e.tensor_copy(
                        att_v[:, r, dh * 512:(dh + 1) * 512],
                        psA[:, sl * 512:(sl + 1) * 512]), cDV)
                    ev_ev = cDV.n
                    wo_slot_ev[sl] = ev_ev
                    W("gp", cDV, ev_ev)
                    W("gp", qSY, ev_init[b])
                    E("gp", lambda e, b=b, tt=tt, dh=dh, r=r: e.dma_start(
                        out=out_d[b, tt * 128:(tt + 1) * 128,
                                  dh * 512:(dh + 1) * 512],
                        in_=att_v[:, r, dh * 512:(dh + 1) * 512],
                        accum_op=ALU.add), qGP, 16)
                    ev_att_ring[r] = qGP.n
            ev_attn_pe[b] = cPE.n
            ev_wout_add[b] = qGP.n

        def bisect():
            W("dv", cDV, max(ev_pb[0], ev_pb[1]))
            E("dv", lambda e: e.memset(lohi[:, 0:1], 0.0))
            E("dv", lambda e: e.memset(lohi[:, 2:3], 0.0))
            E("dv", lambda e: e.memset(lohi[:, 1:2], 1.0))
            E("dv", lambda e: e.memset(lohi[:, 3:4], 1.0), cDV)
            for it in range(NITER):
                for b in range(NB):
                    lo, hi, mid = 2 * b, 2 * b + 1, 4 + b
                    E("dv", lambda e, lo=lo, hi=hi, mid=mid: e.tensor_add(
                        lohi[:, mid:mid + 1], lohi[:, hi:hi + 1],
                        lohi[:, lo:lo + 1]))
                    E("dv", lambda e, mid=mid: e.tensor_scalar_mul(
                        lohi[:, mid:mid + 1], lohi[:, mid:mid + 1], 0.5))
                    for ch in range(2):
                        E("dv", lambda e, b=b, ch=ch, mid=mid: e.tensor_scalar(
                            bscr.ap()[:, :],
                            pb_f[:, b * NT + ch * 1024:b * NT + (ch + 1) * 1024],
                            lohi[:, mid:mid + 1], None, ALU.is_ge,
                            op1=ALU.add,
                            accum_out=lohi[:, 12 + ch:13 + ch]))
                    E("dv", lambda e: e.tensor_add(lohi[:, 14:15], lohi[:, 12:13],
                                                   lohi[:, 13:14]))
                    E("dv", lambda e: e.tensor_scalar(
                        lohi[:, 15:16], lohi[:, 14:15], float(KS), None,
                        ALU.is_ge))
                    E("dv", lambda e, lo=lo, mid=mid: e.tensor_sub(
                        lohi[:, 10:11], lohi[:, mid:mid + 1], lohi[:, lo:lo + 1]))
                    E("dv", lambda e, lo=lo: e.scalar_tensor_tensor(
                        lohi[:, lo:lo + 1], lohi[:, 10:11], lohi[:, 15:16],
                        lohi[:, lo:lo + 1], ALU.mult, ALU.add))
                    E("dv", lambda e, hi=hi, mid=mid: e.tensor_sub(
                        lohi[:, 10:11], lohi[:, hi:hi + 1], lohi[:, mid:mid + 1]))
                    E("dv", lambda e, hi=hi, mid=mid: e.scalar_tensor_tensor(
                        lohi[:, hi:hi + 1], lohi[:, 10:11], lohi[:, 15:16],
                        lohi[:, mid:mid + 1], ALU.mult, ALU.add), cDV)
            for b in range(NB):
                hi = 2 * b + 1
                for ch in range(2):
                    E("dv", lambda e, b=b, ch=ch, hi=hi: e.tensor_scalar(
                        bscr.ap()[:, :],
                        pb_f[:, b * NT + ch * 1024:b * NT + (ch + 1) * 1024],
                        lohi[:, hi:hi + 1], None, ALU.is_lt))
                    E("dv", lambda e, b=b, ch=ch: e.tensor_mul(
                        bscr.ap()[:, :], bscr.ap()[:, :],
                        pb_f[:, b * NT + ch * 1024:b * NT + (ch + 1) * 1024]))
                    E("dv", lambda e, ch=ch: e.tensor_reduce(
                        theta[:, 4 + ch:5 + ch], bscr.ap()[:, :], AX.X, ALU.max))
                E("dv", lambda e, b=b: e.tensor_tensor(
                    theta[:, b:b + 1], theta[:, 4:5], theta[:, 5:6], ALU.max))
                for ch in range(2):
                    E("dv", lambda e, b=b, ch=ch: e.tensor_scalar(
                        bscr.ap()[:, :],
                        pb_f[:, b * NT + ch * 1024:b * NT + (ch + 1) * 1024],
                        theta[:, b:b + 1], None, ALU.is_gt,
                        op1=ALU.add,
                        accum_out=lohi[:, 12 + ch:13 + ch]))
                E("dv", lambda e: e.tensor_add(lohi[:, 14:15], lohi[:, 12:13],
                                               lohi[:, 13:14]))
                E("dv", lambda e, b=b: e.tensor_scalar(
                    theta[:, 2 + b:3 + b], lohi[:, 14:15], -1.0, float(KS),
                    ALU.mult, ALU.add), cDV)
            return cDV.n

        def build_dst(b, ev_bis):
            W("dv", cDV, ev_bis)
            E("dv", lambda e: e.memset(theta[:, 6:8], 0.0), cDV)
            ev_last = cDV.n
            for tc in range(TTC):
                E("dv", lambda e, b=b, tc=tc: e.tensor_scalar(
                    bch[:, 256:384],
                    pb_f[:, b * NT + tc * 128:b * NT + (tc + 1) * 128],
                    theta[:, b:b + 1], None, ALU.is_gt))
                E("dv", lambda e, b=b, tc=tc: e.tensor_scalar(
                    bch[:, 384:512],
                    pb_f[:, b * NT + tc * 128:b * NT + (tc + 1) * 128],
                    theta[:, b:b + 1], None, ALU.is_equal))
                E("dv", lambda e: e.tensor_tensor_scan(
                    bch[:, 512:640], bch[:, 384:512], bch[:, 0:128],
                    theta[:, 6:7], ALU.add, ALU.add))
                E("dv", lambda e: e.tensor_copy(theta[:, 6:7], bch[:, 639:640]))
                E("dv", lambda e, b=b: e.tensor_scalar(
                    bch[:, 512:640], bch[:, 512:640], theta[:, 2 + b:3 + b],
                    None, ALU.is_le))
                E("dv", lambda e: e.tensor_mul(bch[:, 384:512], bch[:, 384:512],
                                               bch[:, 512:640]))
                E("dv", lambda e: e.tensor_add(bch[:, 256:384], bch[:, 256:384],
                                               bch[:, 384:512]))
                E("dv", lambda e: e.tensor_tensor_scan(
                    bch[:, 512:640], bch[:, 256:384], bch[:, 0:128],
                    theta[:, 7:8], ALU.add, ALU.add))
                E("dv", lambda e: e.tensor_copy(theta[:, 7:8], bch[:, 639:640]))
                E("dv", lambda e: e.tensor_scalar_add(bch[:, 512:640],
                                                      bch[:, 512:640],
                                                      -1.0 - BIGF))
                E("dv", lambda e: e.tensor_mul(bch[:, 512:640], bch[:, 512:640],
                                               bch[:, 256:384]))
                E("dv", lambda e: e.tensor_scalar_add(bch[:, 640:768],
                                                      bch[:, 512:640], BIGF),
                  cDV)
                ev_dst = cDV.n
                W("pe", cDV, ev_dst)
                E("pe", lambda e, tc=tc: e.matmul(
                    psB[0:128, 1024 + (tc % 2) * 512:1024 + (tc % 2) * 512 + 8],
                    lhsT=bch[0:8, 640:768], rhs=I_f[0:8, 0:8], start=True,
                    stop=True), cPE)
                ev_t = cPE.n
                W("dv", cPE, ev_t)
                E("dv", lambda e, b=b, tc=tc: e.tensor_copy(
                    dstT[b][:, tc, :],
                    psB[:, 1024 + (tc % 2) * 512:1024 + (tc % 2) * 512 + 8]),
                  cDV)
                ev_last = cDV.n
            return ev_last

        ev_prep_xgT = {}
        ev_prep_mc = {}
        def prep_a(b, e, par, ev_dst_done):
            W("dv", cDV, ev_dst_done)
            W("dv", cDV, ev_phi[b])
            E("dv", lambda e_, b=b, e=e: e_.tensor_copy(
                pack_v[:, :, 2:3], phi_v[b][:, :, e:e + 1]))
            E("dv", lambda e_, b=b, e=e: e_.tensor_copy(
                pack_v[:, :, 3:4], plo_v[b][:, :, e:e + 1]), cDV)
            ev_pk = cDV.n
            W("pe", cDV, ev_pk)
            ev_round = [0] * 8
            for rnd in range(8):
                pp = rnd % 2
                if rnd >= 2:
                    W("dv", cPE, ev_round[rnd - 2])
                for i in range(2):
                    tc = rnd * 2 + i
                    E("dv", lambda e_, b=b, e=e, tc=tc, pp=pp, i=i:
                      e_.tensor_scalar(
                        Msel_v[:, pp * 2 + i, :], iota_s[:],
                        dstT[b][:, tc, e:e + 1], None, ALU.is_equal),
                      cDV if i == 1 else None)
                ev_ms = cDV.n
                W("pe", cDV, ev_ms)
                for i in range(2):
                    tc = rnd * 2 + i
                    E("pe", lambda e_, tc=tc, pp=pp, i=i: e_.matmul(
                        psB[0:4, 0:512], lhsT=pack_v[:, tc, :],
                        rhs=Msel_v[:, pp * 2 + i, :], start=(tc == 0),
                        stop=(tc == 15)), cPE if i == 1 else None)
                ev_round[rnd] = cPE.n
            ev_ext = cPE.n
            W("dv", cPE, ev_ext)
            E("dv", lambda e_: e_.tensor_copy(ext_sb[0:4, :], psB[0:4, 0:512]),
              cDV)
            ev_exts = cDV.n
            W("pe", cDV, ev_exts)
            for st in range(4):
                E("pe", lambda e_, st=st: e_.matmul(
                    psB[:, 512 + st * 4:512 + st * 4 + 4],
                    lhsT=ext_sb[0:4, st * 128:(st + 1) * 128], rhs=I_f[0:4, 0:4],
                    start=True, stop=True), cPE if st == 3 else None)
            ev_extT = cPE.n
            W("dv", cPE, ev_extT)
            E("dv", lambda e_: e_.tensor_copy(
                mg_v[:, :, :].rearrange("p s k -> p (s k)"), psB[:, 512:528]),
              cDV)
            for st in range(4):
                E("dv", lambda e_, st=st, b=b: e_.tensor_scalar(
                    scr1[:], mg_v[:, st, 0:1], 64.0, float(b * NT), ALU.mult,
                    ALU.add))
                E("dv", lambda e_, st=st: e_.tensor_add(scr1[:], scr1[:],
                                                        mg_v[:, st, 1:2]))
                E("dv", lambda e_, st=st, par=par: e_.tensor_copy(
                    mcol_i[par * 4 + st].ap()[:, :], scr1[:]))
                E("dv", lambda e_, st=st: e_.tensor_add(
                    scr2[:], mg_v[:, st, 2:3], mg_v[:, st, 3:4]))
                E("dv", lambda e_, st=st, par=par: e_.tensor_copy(
                    gco[:, 16 + par * 4 + st:17 + par * 4 + st], scr2[:]), cDV)
            ev_mcol = cDV.n
            ev_prep_mc[(b, e)] = ev_mcol
            W("gp", cDV, ev_mcol)
            gev = []
            for st in (0, 1):
                E("gp", lambda e_, st=st, par=par: e_.indirect_dma_start(
                    out=Xg_v[:, st % 2, :], out_offset=None,
                    in_=xn_s.rearrange("b n d -> (b n) d"),
                    in_offset=IndirectOffsetOnAxis(
                        mcol_i[par * 4 + st].ap()[:, :], 0)),
                  qGP, 16)
                gev.append(qGP.n)
            return gev, par

        def prep_b(b, e, gevpar):
            gev, par = gevpar
            xg_ev = [0, 0]
            for sh in range(2):
                if sh == 1:
                    W("gp", cPE, cPE.n)
                    for st in (2, 3):
                        E("gp", lambda e_, st=st, par=par: e_.indirect_dma_start(
                            out=Xg_v[:, st % 2, :], out_offset=None,
                            in_=xn_s.rearrange("b n d -> (b n) d"),
                            in_offset=IndirectOffsetOnAxis(
                                mcol_i[par * 4 + st].ap()[:, :], 0)),
                          qGP, 16)
                        gev.append(qGP.n)
                W("pe", qGP, gev[2 * sh + 1])
                for st in (2 * sh, 2 * sh + 1):
                    for dg in range(2):
                        par2 = dg % 2
                        if xg_ev[par2]:
                            W("pe", cDV, xg_ev[par2])
                        for dc4 in range(4):
                            dc = dg * 4 + dc4
                            E("pe", lambda e_, st=st, dc=dc, dc4=dc4, par2=par2:
                              e_.matmul(
                                psB[:, 512 + par2 * 512 + dc4 * 128:
                                    512 + par2 * 512 + dc4 * 128 + 128],
                                lhsT=Xg_v[:, st % 2, dc * 128:(dc + 1) * 128],
                                rhs=I_bf[:], start=True, stop=True),
                              cPE if dc4 == 3 else None)
                        evt = cPE.n
                        W("dv", cPE, evt)
                        E("dv", lambda e_, st=st, dg=dg, par2=par2:
                          e_.tensor_copy(
                            XgT_v[:, dg * 4:(dg + 1) * 4,
                                  st * 128:(st + 1) * 128],
                            psB[:, 512 + par2 * 512:1024 + par2 * 512]
                            .rearrange("p (c t) -> p c t", c=4)), cDV)
                        xg_ev[par2] = cDV.n
            ev_prep_xgT[(b, e)] = cDV.n

        def ffn(b):
            W("pe", cDV, cDV.n)
            W("pe", cAC, cAC.n)
            ev_w1 = {}
            W("gp", cPE, ev_attn_pe[1])
            for hh in range(2):
                E("gp", lambda e, hh=hh: e.dma_start(
                    out=w1f[:, hh * 4:(hh + 1) * 4, :],
                    in_=w1_d[0].rearrange("(c p) f -> p c f", p=128)[
                        :, hh * 4:(hh + 1) * 4, 0:2048]), qGP, 16)
            ev_w1[(0, 0)] = qGP.n
            ev_scat = ev_wout_add[b]
            ev_gelu = {}
            ev_psA_free = [0, 0, 0, 0]
            ev_psB_free = [0, 0, 0, 0]
            ev_w2_rd = [0, 0]
            for e in range(NE):
                par = e % 2
                for fh in range(2):
                    W("pe", qGP, ev_w1[(e, fh)])
                    W("pe", cDV, ev_prep_xgT[(b, e)])
                    for ft in range(16):
                        sl = ft % 2
                        if (e, fh, ft - 2) in ev_gelu:
                            W("pe", cAC, ev_gelu[(e, fh, ft - 2)])
                        if ft < 2:
                            W("pe", cDV, ev_psA_free[sl])
                        for dc in range(DCC):
                            E("pe", lambda e_, ft=ft, dc=dc, sl=sl:
                              e_.matmul(
                                psA[:, sl * 512:(sl + 1) * 512],
                                lhsT=w1f[:, dc, ft * 128:(ft + 1) * 128],
                                rhs=XgT_v[:, dc, :], start=(dc == 0),
                                stop=(dc == DCC - 1)),
                              cPE if dc == DCC - 1 else None)
                        ev_h = cPE.n
                        W("ac", cPE, ev_h)
                        E("ac", lambda e_, ft=ft, sl=sl: e_.activation(
                            hT_v[:, ft, :], psA[:, sl * 512:(sl + 1) * 512],
                            AF.Gelu_apprx_tanh), cAC)
                        ev_gelu[(e, fh, ft)] = cAC.n
                    ev_w1_done = cPE.n
                    W("gp", cPE, ev_w1_done)
                    if fh == 0:
                        for hh in range(2):
                            E("gp", lambda e_, e=e, hh=hh: e_.dma_start(
                                out=w1f[:, hh * 4:(hh + 1) * 4, :],
                                in_=w1_d[e].rearrange(
                                    "(c p) f -> p c f", p=128)[
                                    :, hh * 4:(hh + 1) * 4, 2048:4096]),
                              qGP, 16)
                        ev_w1[(e, 1)] = qGP.n
                    elif e < NE - 1:
                        for hh in range(2):
                            E("gp", lambda e_, e=e, hh=hh: e_.dma_start(
                                out=w1f[:, hh * 4:(hh + 1) * 4, :],
                                in_=w1_d[e + 1].rearrange(
                                    "(c p) f -> p c f", p=128)[
                                    :, hh * 4:(hh + 1) * 4, 0:2048]), qGP, 16)
                        ev_w1[(e + 1, 0)] = qGP.n
                    if fh == 0 and e < NE - 1:
                        gev = prep_a(b, e + 1, (e + 1) % 2, 0)
                    if fh == 1 and e < NE - 1:
                        prep_b(b, e + 1, gev)
                        W("pe", cDV, ev_prep_xgT[(b, e + 1)])
                    W("pe", cDV, max(ev_psB_free))
                    for g in range(4):
                        pr = g % 2
                        W("gp", cPE, ev_w2_rd[pr])
                        E("gp", lambda e_, e=e, fh=fh, g=g, pr=pr: e_.dma_start(
                            out=w2g_v[:, pr, :, :],
                            in_=w2_d[e].rearrange("(c p) d -> p c d", p=128)[
                                :, fh * 16 + g * 4:fh * 16 + (g + 1) * 4, :]),
                          qGP, 16)
                        ev_w2 = qGP.n
                        W("pe", qGP, ev_w2)
                        for fc in range(4):
                            fcg = g * 4 + fc
                            W("pe", cAC, ev_gelu[(e, fh, fcg)])
                            for st in range(4):
                                for dh in range(2):
                                    ps = psA if dh == 0 else psB
                                    first = (g == 0 and fc == 0)
                                    last = (g == 3 and fc == 3)
                                    E("pe", lambda e_, fcg=fcg, st=st, dh=dh,
                                      pr=pr, fc=fc, first=first, last=last,
                                      ps=ps: e_.matmul(
                                        ps[:, st * 512:(st + 1) * 512],
                                        lhsT=hT_v[:, fcg,
                                                  st * 128:(st + 1) * 128],
                                        rhs=w2g_v[:, pr, fc,
                                                  dh * 512:(dh + 1) * 512],
                                        start=first, stop=last),
                                      cPE if (st == 3 and dh == 1) else None)
                        ev_w2_rd[pr] = cPE.n
                    ev_w2pe = cPE.n
                    W("dv", cPE, ev_w2pe)
                    for st in range(4):
                        for dh in range(2):
                            ps = psA if dh == 0 else psB
                            if fh == 0:
                                E("dv", lambda e_, st=st, dh=dh, ps=ps, par=par:
                                  e_.tensor_scalar(
                                    gOut[:, st, dh * 512:(dh + 1) * 512],
                                    ps[:, st * 512:(st + 1) * 512],
                                    gco[:, 16 + par * 4 + st:
                                        17 + par * 4 + st], None, ALU.mult),
                                  cDV)
                            else:
                                E("dv", lambda e_, st=st, dh=dh, ps=ps, par=par:
                                  e_.scalar_tensor_tensor(
                                    gOut[:, st, dh * 512:(dh + 1) * 512],
                                    ps[:, st * 512:(st + 1) * 512],
                                    gco[:, 16 + par * 4 + st:
                                        17 + par * 4 + st],
                                    gOut[:, st, dh * 512:(dh + 1) * 512],
                                    ALU.mult, ALU.add), cDV)
                            if dh == 0:
                                ev_psA_free[st] = cDV.n
                            else:
                                ev_psB_free[st] = cDV.n
                    ev_gout = cDV.n
                W("gp", cDV, ev_gout)
                W("gp", qGP, ev_scat)
                for st in range(4):
                    E("gp", lambda e_, st=st, par=par: e_.indirect_dma_start(
                        out=out_d.rearrange("b n d -> (b n) d"),
                        out_offset=IndirectOffsetOnAxis(
                            mcol_i[par * 4 + st].ap()[:, :], 0),
                        in_=gOut[:, st, :], in_offset=None, compute_op=ALU.add),
                      qGP, 16)
                ev_scat = qGP.n
            return ev_scat

        if stage >= 1:
            r0 = p1(0, 0)
        if stage >= 2:
            attn_core(0, *r0)
        if stage >= 3:
            r1 = p1(1, ev_attn_pe[0])
        if stage >= 4:
            ev_bis = bisect()
            ev_dst0 = build_dst(0, ev_bis)
            gev0 = prep_a(0, 0, 0, ev_dst0)
            prep_b(0, 0, gev0)
        if stage >= 5:
            attn_core(1, *r1)
        if stage >= 6:
            ffn(0)
        if stage >= 7:
            ev_dst1 = build_dst(1, ev_bis)
            gev1 = prep_a(1, 0, 0, ev_dst1)
            prep_b(1, 0, gev1)
            ffn(1)

        W("sy", qGP, qGP.n)
        W("sy", qSY, qSY.n)

        blk = ctx.enter_context(nc.Block())
        blk.tensor(lambda t: [f(t) for f in P["pe"]])
        blk.vector(lambda v: [f(v) for f in P["dv"]])
        blk.scalar(lambda s: [f(s) for f in P["ac"]])
        blk.gpsimd(lambda g: [f(g) for f in P["gp"]])
        blk.sync(lambda s: [f(s) for f in P["sy"]])
    return nc


_NC_CACHE = [None]


def kernel_stage(stage, ins):
    nc = build_program(stage)
    return _run(nc, ins)


def _run(nc, kw):
    x = np.ascontiguousarray(np.asarray(kw["L0_attn_x"], dtype=np.float32))
    shared = {
        "gamma": np.ascontiguousarray(
            np.asarray(kw["L0_attn_gamma"], np.float32).reshape(1, D)),
        "Wq": np.ascontiguousarray(np.asarray(kw["L0_attn_Wq"], np.float32)),
        "Wkv": np.ascontiguousarray(np.asarray(kw["L0_attn_Wkv"], np.float32)),
        "qsc": np.ascontiguousarray(
            np.asarray(kw["L0_attn_q_scale"], np.float32).reshape(1, DH)),
        "ksc": np.ascontiguousarray(
            np.asarray(kw["L0_attn_k_scale"], np.float32).reshape(1, DH)),
        "nullkv": np.ascontiguousarray(
            np.asarray(kw["L0_attn_null_kv"], np.float32)),
        "Wout": np.ascontiguousarray(np.asarray(kw["L0_attn_Wout"], np.float32)),
        "gate_w": np.ascontiguousarray(
            np.asarray(kw["L0_ffn_gate_w"], np.float32)),
        "w1": np.ascontiguousarray(np.asarray(kw["L0_ffn_w1"], np.float32)),
        "w2": np.ascontiguousarray(np.asarray(kw["L0_ffn_w2"], np.float32)),
    }
    in_maps = []
    for c in range(8):
        m = dict(shared)
        m["x"] = x[c * NB:(c + 1) * NB]
        in_maps.append(m)
    res = run_bass_kernel_spmd(nc, in_maps, core_ids=list(range(8)))
    return np.concatenate([r["y"] for r in res.results], axis=0)


def kernel(L0_attn_x, L0_attn_gamma, L0_attn_Wq, L0_attn_Wkv, L0_attn_q_scale,
           L0_attn_k_scale, L0_attn_null_kv, L0_attn_Wout, L0_ffn_gate_w,
           L0_ffn_w1, L0_ffn_w2):
    if _NC_CACHE[0] is None:
        _NC_CACHE[0] = build_program()
    return _run(_NC_CACHE[0], dict(
        L0_attn_x=L0_attn_x, L0_attn_gamma=L0_attn_gamma, L0_attn_Wq=L0_attn_Wq,
        L0_attn_Wkv=L0_attn_Wkv, L0_attn_q_scale=L0_attn_q_scale,
        L0_attn_k_scale=L0_attn_k_scale, L0_attn_null_kv=L0_attn_null_kv,
        L0_attn_Wout=L0_attn_Wout, L0_ffn_gate_w=L0_ffn_gate_w,
        L0_ffn_w1=L0_ffn_w1, L0_ffn_w2=L0_ffn_w2))


# version 02
# speedup vs baseline: 1.0222x; 1.0222x over previous
import numpy as np
import ml_dtypes
import concourse.bass as bass
import concourse.mybir as mybir
from concourse.bass_utils import run_bass_kernel_spmd
from concourse.bass_core_rust import IndirectOffsetOnAxis
from contextlib import ExitStack

F32 = mybir.dt.float32
BF16 = mybir.dt.bfloat16
I32 = mybir.dt.int32
AF = mybir.ActivationFunctionType
ALU = mybir.AluOpType
AX = mybir.AxisListType

NB, NT, D, H, DH, NE, KS, HID = 2, 2048, 1024, 16, 64, 8, 512, 4096
TTC, DCC = NT // 128, D // 128
NEG = -1000.0
BIGF = 65536.0
LN_EPS = 1e-5
SCALE = 8.0
HEADROOM = 60.0
NITER = 48
EVDBG = False
ATTN_DBG = False
ATTN_SERIAL = False
NEGC_ZERO = False
ATTN_QG = 4
ATTN_H = 16
ATTN_WOUT = True
ATTN_NORM = True
ATTN_EXP = True
EXP_FUNC = [AF.Exp]
ATTN_PV = True


class Sem:
    __slots__ = ("h", "n")
    def __init__(self, h):
        self.h = h
        self.n = 0


def build_program(stage=9):
    nc = bass.Bass("TRN2", target_bir_lowering=False, detect_race_conditions=False)

    x_d = nc.dram_tensor("x", [NB, NT, D], F32, kind="ExternalInput").ap()
    gamma_d = nc.dram_tensor("gamma", [1, D], F32, kind="ExternalInput").ap()
    wq_d = nc.dram_tensor("Wq", [D, D], F32, kind="ExternalInput").ap()
    wkv_d = nc.dram_tensor("Wkv", [D, 2 * DH], F32, kind="ExternalInput").ap()
    qs_d = nc.dram_tensor("qsc", [1, DH], F32, kind="ExternalInput").ap()
    ks_d = nc.dram_tensor("ksc", [1, DH], F32, kind="ExternalInput").ap()
    nkv_d = nc.dram_tensor("nullkv", [2, 2, DH], F32, kind="ExternalInput").ap()
    wo_d = nc.dram_tensor("Wout", [D, D], F32, kind="ExternalInput").ap()
    gw_d = nc.dram_tensor("gate_w", [D, NE], F32, kind="ExternalInput").ap()
    w1_d = nc.dram_tensor("w1", [NE, D, HID], F32, kind="ExternalInput").ap()
    w2_d = nc.dram_tensor("w2", [NE, HID, D], F32, kind="ExternalInput").ap()
    out_d = nc.dram_tensor("y", [NB, NT, D], F32, kind="ExternalOutput").ap()
    dbg_d = nc.dram_tensor("dbg", [128, 2048], F32, kind="ExternalOutput").ap()
    xn_s = nc.dram_tensor("xn_scr", [NB, NT, D], BF16, kind="Internal").ap()
    xl_s = nc.dram_tensor("xl_scr", [NT, D], BF16, kind="Internal").ap()
    x2_s = nc.dram_tensor("x2_scr", [NT, D], BF16, kind="Internal").ap()
    qn_s = nc.dram_tensor("qn_scr", [NT, D], BF16, kind="Internal").ap()
    w2b_s = nc.dram_tensor("w2b_scr", [NE, HID, D], BF16, kind="Internal").ap()

    ctx = ExitStack()
    with ctx:
        BFC = 58700
        AB = ctx.enter_context(nc.sbuf_tensor("AB", [128, BFC], BF16))
        bo = [0]
        def bfa(n):
            c = bo[0]; bo[0] += n
            assert bo[0] <= BFC
            return c
        R1 = bfa(16384); R2 = bfa(16384); R3 = bfa(4096); R4 = bfa(8192)
        W8 = bfa(8192); RM = bfa(1280); RON = bfa(128); RQK = bfa(128)
        RPK = bfa(64); RPH = bfa(512); RKV = bfa(1024); RGW = bfa(128)
        RLB = bfa(512); RSQ = bfa(512); RGM = bfa(1024); RKB = bfa(64)

        bvv = lambda c, n: AB[:, c:c + n]
        xnT = bvv(R1, 16384).rearrange("p (c t) -> p c t", c=DCC)
        QT = xnT
        xn2T = bvv(R2, 16384).rearrange("p (c t) -> p c t", c=DCC)
        OT = xn2T
        xnTlo_w = bvv(R3, 4096).rearrange("p (r t) -> p r t", r=2)
        w1f = bvv(R1, 16384).rearrange("p (c f) -> p c f", c=DCC)
        w2g_v = bvv(R2, 8192).rearrange("p (s c d) -> p s c d", s=2, c=4)
        hT_v = bvv(R2 + 8192, 8192).rearrange("p (c s) -> p c s", c=16)
        ringxn = bvv(R4, 2048).rearrange("p (r c) -> p r c", r=2)
        ringqn = ringxn
        ringx2 = bvv(R4 + 2048, 2048).rearrange("p (r c) -> p r c", r=2)
        ringxl = bvv(R4 + 4096, 2048).rearrange("p (r c) -> p r c", r=2)
        XgT_v = bvv(R4, 4096).rearrange("p (c s) -> p c s", c=DCC)
        Msel_v = bvv(R4 + 4096, 2048).rearrange("p (r s) -> p r s", r=4)
        Xg_v = bvv(R4 + 6144, 2048).rearrange("p (s d) -> p s d", s=2)
        Wq_b = bvv(W8, 8192).rearrange("p (c e) -> p c e", c=DCC)
        kT = bvv(W8, 2176)
        Vaug = bvv(W8 + 2176, 17 * 65).rearrange("p (j v) -> p j v", j=17)
        Ev = bvv(W8 + 3296, 2048).rearrange("p (s c) -> p s c", s=2)
        Wo_h = bvv(W8 + 4096, 4096).rearrange("p (c e) -> p c e", c=DCC)
        cm_stdT = bvv(RM, 128)
        cm_q0T = bvv(RM + 128, 128)
        I_bf = bvv(RM + 256, 128)
        cm_padT = bvv(RM + 384, 512)
        cm_full = bvv(RM + 896, 384)
        ones_r = bvv(RON, 128)
        qs_bf = bvv(RQK, 64)
        ks_bf = bvv(RQK + 64, 64)
        pack_v = bvv(RPK, 64).rearrange("p (c k) -> p c k", c=16)
        phi_v = [bvv(RPH, 128).rearrange("p (c e) -> p c e", c=16),
                 bvv(RPH + 256, 128).rearrange("p (c e) -> p c e", c=16)]
        plo_v = [bvv(RPH + 128, 128).rearrange("p (c e) -> p c e", c=16),
                 bvv(RPH + 384, 128).rearrange("p (c e) -> p c e", c=16)]
        Wkv_b = bvv(RKV, 1024).rearrange("p (c e) -> p c e", c=DCC)
        gw_hi = bvv(RGW, 64).rearrange("p (c e) -> p c e", c=DCC)
        gw_lo = bvv(RGW + 64, 64).rearrange("p (c e) -> p c e", c=DCC)
        linv_bf = bvv(RLB, 512)
        sq_bf = bvv(RSQ, 512)
        gamma_bc = bvv(RGM, 1024)
        kb_t = bvv(RKB, 64)

        FC = 10700
        AFp = ctx.enter_context(nc.sbuf_tensor("AFp", [128, FC], F32))
        fo = [0]
        def fa(n):
            c = fo[0]; fo[0] += n
            assert fo[0] <= FC
            return c
        fvv = lambda c, n: AFp[:, c:c + n]
        FXIN = fa(2048); FQ = fa(2048); FWRK = fa(1024); FXNF = fa(1024)
        FST = fa(256); FQS = fa(128); FIOT = fa(512); FCOL = fa(16)
        FDT = fa(256); FMG = fa(64); FNC = fa(16); FIF = fa(128)
        FATT = fa(2048); FOSB = fa(512); FLV = fa(512); FGW = fa(64)
        xin_v = fvv(FXIN, 2048).rearrange("p (r c) -> p r c", r=2)
        qf_v = fvv(FQ, 2048).rearrange("p (r c) -> p r c", r=2)
        gOut = fvv(FXIN, 4096).rearrange("p (s d) -> p s d", s=4)
        wrk = fvv(FWRK, 1024)
        probs_tok = fvv(FWRK, 128).rearrange("p (c e) -> p c e", c=16)
        ext_sb = fvv(FWRK + 128, 512)
        plo_stage = fvv(FWRK + 640, 128).rearrange("p (c e) -> p c e", c=16)
        xnf = fvv(FXNF, 1024)
        stats = fvv(FST, 256).rearrange("p (c k) -> p c k", c=16)
        qs_bc = fvv(FQS, 64)
        ks_bc = fvv(FQS + 64, 64)
        iota_s = fvv(FIOT, 512)
        p_col = fvv(FCOL, 1)
        ge_col = fvv(FCOL + 1, 1)
        tmod_col = fvv(FCOL + 2, 1)
        scr1 = fvv(FCOL + 3, 1)
        scr2 = fvv(FCOL + 4, 1)
        epscol = fvv(FCOL + 5, 1)
        dstT = [fvv(FDT, 128).rearrange("p (c e) -> p c e", c=16),
                fvv(FDT + 128, 128).rearrange("p (c e) -> p c e", c=16)]
        mg_v = fvv(FMG, 16).rearrange("p (s k) -> p s k", s=4)
        negC = fvv(FNC, 1)
        I_f = fvv(FIF, 128)
        att_v = fvv(FATT, 2048).rearrange("p (r c) -> p r c", r=2)
        osb = fvv(FOSB, 512)
        linv = fvv(FLV, 512)
        gwf_st = fvv(FGW, 64).rearrange("p (c e) -> p c e", c=DCC)

        pb = ctx.enter_context(nc.sbuf_tensor("pb", [8, 2 * NT], F32))
        pb_i = pb.ap().bitcast(I32)
        pb_f = pb.ap()
        bscr = ctx.enter_context(nc.sbuf_tensor("bscr", [8, 1024], F32))
        bch = ctx.enter_context(nc.sbuf_tensor("bch", [8, 768], F32))
        lohi = ctx.enter_context(nc.sbuf_tensor("lohi", [8, 16], F32))
        lohi_i = lohi.ap().bitcast(I32)
        theta = ctx.enter_context(nc.sbuf_tensor("theta", [8, 8], F32))
        mcol_i = [ctx.enter_context(nc.sbuf_tensor(f"mcol{k}", [128, 1], I32))
                  for k in range(64)]
        gco = ctx.enter_context(nc.sbuf_tensor("gco", [128, 24], F32))
        nkv_sb = ctx.enter_context(nc.sbuf_tensor("nkv_sb", [2, 4 * DH], F32))
        nkv2 = ctx.enter_context(nc.sbuf_tensor("nkv2", [2, 4 * DH], F32))
        nkb = ctx.enter_context(nc.sbuf_tensor("nkb", [2, 2 * DH], BF16))
        EvDbg = ctx.enter_context(nc.sbuf_tensor("EvDbg", [128, 2048], BF16))
        zbias = ctx.enter_context(nc.sbuf_tensor("zbias", [128, 1], F32))
        USE_EVDBG = [EVDBG]

        psA = ctx.enter_context(nc.psum_tensor("psA", [128, 2048], F32))
        psB = ctx.enter_context(nc.psum_tensor("psB", [128, 2048], F32))

        qSY = Sem(ctx.enter_context(nc.semaphore("qSY")))
        qGP = Sem(ctx.enter_context(nc.semaphore("qGP")))
        cPE = Sem(ctx.enter_context(nc.semaphore("cPE")))
        cDV = Sem(ctx.enter_context(nc.semaphore("cDV")))
        cAC = Sem(ctx.enter_context(nc.semaphore("cAC")))
        cGP = Sem(ctx.enter_context(nc.semaphore("cGP")))

        P = {k: [] for k in ("pe", "dv", "ac", "gp", "sy")}
        max_waited = {}
        last_qwait = {}
        def W(eng, sem, val):
            if val > 0:
                if id(sem) in max_waited:
                    max_waited[id(sem)] = max(max_waited[id(sem)], val)
                P[eng].append(lambda e, s=sem.h, v=val: e.wait_ge(s, v))
        auto_sem = {"dv": cDV, "ac": cAC, "gp": cGP}
        def E(eng, fn, sem=None, k=1):
            if sem is not None and k == 16:
                if id(sem) not in max_waited:
                    max_waited[id(sem)] = 0
                    last_qwait[id(sem)] = 0
                need = max_waited[id(sem)]
                if last_qwait[id(sem)] < need:
                    last_qwait[id(sem)] = need
                    P[eng].append(lambda e, s=sem.h, v=need: e.wait_ge(s, v))
            if sem is None and eng in auto_sem:
                sem = auto_sem[eng]
            if sem is None:
                P[eng].append(fn)
                return 0
            sem.n += k
            P[eng].append(lambda e, f=fn, s=sem.h, kk=k: f(e).then_inc(s, kk))
            if k == 1 and eng in auto_sem and sem is auto_sem[eng]:
                P[eng].append(lambda e, s=sem.h, v=sem.n: e.wait_ge(s, v))
            return sem.n

        ev_init = []
        for b in range(NB):
            E("sy", lambda e, b=b: e.dma_start(out=out_d[b], in_=x_d[b]), qSY, 16)
            ev_init.append(qSY.n)

        E("gp", lambda e: e.iota(iota_s[:], pattern=[[1, 512]], base=0,
                                 channel_multiplier=0,
                                 allow_small_or_imprecise_dtypes=True))
        E("gp", lambda e: e.iota(p_col[:], pattern=[[1, 1]], base=0,
                                 channel_multiplier=1,
                                 allow_small_or_imprecise_dtypes=True))
        E("gp", lambda e: e.memset(cm_full[:], NEG))
        E("gp", lambda e: e.memset(wrk[:, 0:512], 0.0), cGP)
        ev_z = cGP.n
        W("gp", cGP, ev_z)
        E("gp", lambda e: e.affine_select(cm_stdT[:], wrk[:, 0:128],
                                          pattern=[[1, 128]], compare_op=ALU.is_ge,
                                          fill=NEG, base=0, channel_multiplier=-1))
        E("gp", lambda e: e.affine_select(cm_padT[:], wrk[:, 0:512],
                                          pattern=[[0, 512]], compare_op=ALU.is_ge,
                                          fill=NEG, base=-126, channel_multiplier=1),
          cGP)
        ev_gpc = cGP.n

        E("gp", lambda e: e.dma_start(
            out=Wq_b[:], in_=wq_d.rearrange("(c p) e -> p c e", p=128)), qGP, 16)
        ev_wq = [qGP.n, 0]
        E("gp", lambda e: e.dma_start(
            out=Wkv_b[:], in_=wkv_d.rearrange("(c p) e -> p c e", p=128)), qGP, 16)
        ev_wkv = qGP.n
        E("gp", lambda e: e.dma_start(
            out=gw_hi[:], in_=gw_d.rearrange("(c p) e -> p c e", p=128)), qGP, 16)
        E("gp", lambda e: e.dma_start(
            out=gwf_st[:, :, :],
            in_=gw_d.rearrange("(c p) e -> p c e", p=128)), qGP, 16)
        ev_gwf = qGP.n

        E("sy", lambda e: e.dma_start(out=qs_bc[0:1, :], in_=qs_d), qSY, 16)
        E("sy", lambda e: e.dma_start(out=ks_bc[0:1, :], in_=ks_d), qSY, 16)
        E("sy", lambda e: e.dma_start(out=xnf[0:1, :], in_=gamma_d), qSY, 16)
        E("sy", lambda e: e.dma_start(
            out=nkv_sb[:, 0:128].rearrange("b (a d) -> b a d", a=2),
            in_=nkv_d.rearrange("a b d -> b a d")), qSY, 16)
        ev_smallw = qSY.n

        W("dv", qSY, ev_smallw)
        W("dv", cGP, ev_gpc)
        E("dv", lambda e: e.memset(epscol[:], LN_EPS))
        E("dv", lambda e: e.memset(zbias.ap()[:, :], 0.0))
        E("dv", lambda e: e.memset(ones_r[:], 1.0))
        E("dv", lambda e: e.tensor_scalar(I_f[:], iota_s[:, 0:128], p_col[:], None,
                                          ALU.is_equal))
        E("dv", lambda e: e.tensor_copy(I_bf[:], I_f[:]))
        E("dv", lambda e: e.tensor_copy(cm_q0T[:], cm_stdT[:]))
        E("dv", lambda e: e.memset(cm_q0T[0:64, 0:64], 0.0))
        E("dv", lambda e: e.tensor_scalar(ge_col[:], p_col[:], 64.0, None,
                                          ALU.is_ge))
        E("dv", lambda e: e.scalar_tensor_tensor(tmod_col[:], ge_col[:], -64.0,
                                                 p_col[:], ALU.mult, ALU.add))
        for tc in range(16):
            E("dv", lambda e, tc=tc: e.tensor_scalar(
                pack_v[:, tc, 0:1], ge_col[:], 1.0, 2.0 * tc, ALU.mult, ALU.add))
            E("dv", lambda e, tc=tc: e.tensor_copy(pack_v[:, tc, 1:2], tmod_col[:]))
        E("dv", lambda e: e.tensor_copy(qs_bf[0:1, :], qs_bc[0:1, :]))
        E("dv", lambda e: e.tensor_copy(ks_bf[0:1, :], ks_bc[0:1, :]))
        E("dv", lambda e: e.tensor_copy(linv_bf[0:1, :], xnf[0:1, 0:512]))
        E("dv", lambda e: e.tensor_copy(sq_bf[0:1, :], xnf[0:1, 512:1024]))
        E("dv", lambda e: e.memset(bch[:, 0:128], 0.0))
        E("dv", lambda e: e.memset(bch[:, 128:256], BIGF), cDV)
        ev_bcrows = cDV.n
        W("dv", qGP, ev_gwf)
        E("dv", lambda e: e.tensor_sub(
            wrk[:, 0:64].rearrange("p (c e) -> p c e", c=DCC), gwf_st, gw_hi[:]))
        E("dv", lambda e: e.tensor_copy(
            gw_lo[:], wrk[:, 0:64].rearrange("p (c e) -> p c e", c=DCC)), cDV)
        ev_gwlo = cDV.n

        W("pe", cDV, ev_bcrows)
        E("pe", lambda e: e.matmul(psA[:, 0:64], lhsT=ones_r[0:1, :],
                                   rhs=qs_bf[0:1, :], start=True, stop=True))
        E("pe", lambda e: e.matmul(psA[:, 64:128], lhsT=ones_r[0:1, :],
                                   rhs=ks_bf[0:1, :], start=True, stop=True))
        E("pe", lambda e: e.matmul(psA[:, 512:1024], lhsT=ones_r[0:1, :],
                                   rhs=linv_bf[0:1, :], start=True, stop=True))
        E("pe", lambda e: e.matmul(psA[:, 1024:1536], lhsT=ones_r[0:1, :],
                                   rhs=sq_bf[0:1, :], start=True, stop=True), cPE)
        ev_bcmm = cPE.n
        W("dv", cPE, ev_bcmm)
        E("dv", lambda e: e.tensor_copy(qs_bc[:], psA[:, 0:64]))
        E("dv", lambda e: e.tensor_copy(ks_bc[:], psA[:, 64:128]))
        E("dv", lambda e: e.tensor_copy(gamma_bc[:, 0:512], psA[:, 512:1024]))
        E("dv", lambda e: e.tensor_copy(gamma_bc[:, 512:1024], psA[:, 1024:1536]))
        E("dv", lambda e: e.tensor_reduce(scr1[:], qs_bc[:], AX.X, ALU.max,
                                          apply_absolute_value=True))
        E("dv", lambda e: e.tensor_reduce(scr2[:], ks_bc[:], AX.X, ALU.max,
                                          apply_absolute_value=True))
        E("dv", lambda e: e.tensor_mul(negC[:], scr1[:], scr2[:]))
        E("dv", lambda e: e.tensor_scalar(negC[:], negC[:], -SCALE, HEADROOM,
                                          ALU.mult, ALU.add))
        E("dv", lambda e: e.tensor_scalar_min(negC[:], negC[:], 0.0), cDV)
        ev_bcast = cDV.n

        E("dv", lambda e: e.tensor_mul(nkv2[0:2, 0:64], nkv_sb[0:2, 0:64],
                                       nkv_sb[0:2, 0:64]))
        E("dv", lambda e: e.reduce_sum(nkv2[0:2, 64:65], nkv2[0:2, 0:64], AX.X),
          cDV)
        ev_nk1 = cDV.n
        W("ac", cDV, ev_nk1)
        E("ac", lambda e: e.activation(nkv2[0:2, 65:66], nkv2[0:2, 64:65], AF.Ln))
        E("ac", lambda e: e.activation(nkv2[0:2, 66:67], nkv2[0:2, 65:66], AF.Exp,
                                       scale=-0.5), cAC)
        ev_nk2 = cAC.n
        W("dv", cAC, ev_nk2)
        E("dv", lambda e: e.tensor_scalar_min(nkv2[0:2, 66:67], nkv2[0:2, 66:67],
                                              1e12))
        E("dv", lambda e: e.tensor_scalar(nkv2[0:2, 0:64], nkv_sb[0:2, 0:64],
                                          nkv2[0:2, 66:67], None, ALU.mult))
        E("dv", lambda e: e.tensor_mul(nkb[0:2, 0:64], nkv2[0:2, 0:64],
                                       ks_bc[0:2, 0:64]))
        E("dv", lambda e: e.tensor_copy(nkb[0:2, 64:128], nkv_sb[0:2, 64:128]),
          cDV)
        ev_nkb = cDV.n

        ev_pb = [0, 0]
        ev_phi = [0, 0]
        ev_attn_pe = [0, 0]
        ev_wout_add = [0, 0]

        def p1(b, ev_prev_pe):
            st_hist = []
            for tt in range(TTC):
                r = tt % 2
                if len(st_hist) >= 2:
                    W("sy", qSY, st_hist[-2])
                    W("dv", qSY, st_hist[-2])
                E("sy", lambda e, b=b, tt=tt, r=r: e.dma_start(
                    out=xin_v[:, r, :], in_=x_d[b, tt * 128:(tt + 1) * 128, :]),
                  qSY, 16)
                evx = qSY.n
                W("dv", qSY, evx)
                E("dv", lambda e, tt=tt, r=r: e.reduce_sum(
                    stats[:, tt, 0:1], xin_v[:, r, :], AX.X, negate=True))
                E("dv", lambda e, tt=tt: e.tensor_scalar_mul(
                    stats[:, tt, 0:1], stats[:, tt, 0:1], 1.0 / D))
                E("dv", lambda e, tt=tt, r=r: e.tensor_scalar_add(
                    wrk[:], xin_v[:, r, :], stats[:, tt, 0:1]), cDV)
                ev_cen = cDV.n
                E("dv", lambda e, r=r: e.tensor_mul(xin_v[:, r, :], wrk[:],
                                                    wrk[:]))
                E("dv", lambda e, tt=tt, r=r: e.reduce_sum(
                    stats[:, tt, 1:2], xin_v[:, r, :], AX.X), cDV)
                ev_ss = cDV.n
                W("ac", cDV, ev_ss)
                E("ac", lambda e, tt=tt: e.activation(
                    stats[:, tt, 3:4], stats[:, tt, 1:2], AF.Ln, scale=1.0 / D,
                    bias=epscol[:]))
                E("ac", lambda e, tt=tt: e.activation(
                    stats[:, tt, 3:4], stats[:, tt, 3:4], AF.Exp, scale=-0.5), cAC)
                ev_r1 = cAC.n
                W("dv", cAC, ev_r1)
                E("dv", lambda e, tt=tt: e.tensor_scalar(
                    xnf[:], wrk[:], stats[:, tt, 3:4], None, ALU.mult))
                E("dv", lambda e, r=r: e.tensor_copy(ringxn[:, r, :], xnf[:]))
                E("dv", lambda e, r=r: e.tensor_sub(
                    ringxl[:, r, :], xnf[:], ringxn[:, r, :]))
                E("dv", lambda e, tt=tt: e.reduce_sum(
                    stats[:, tt, 4:5], xnf[:], AX.X, negate=True))
                E("dv", lambda e, tt=tt: e.tensor_scalar_mul(
                    stats[:, tt, 4:5], stats[:, tt, 4:5], 1.0 / D))
                E("dv", lambda e, tt=tt: e.tensor_scalar_add(
                    wrk[:], xnf[:], stats[:, tt, 4:5]), cDV)
                ev_cen2 = cDV.n
                E("dv", lambda e, r=r: e.tensor_mul(xin_v[:, r, :], wrk[:],
                                                    wrk[:]))
                E("dv", lambda e, tt=tt, r=r: e.reduce_sum(
                    stats[:, tt, 5:6], xin_v[:, r, :], AX.X), cDV)
                ev_ss2 = cDV.n
                W("ac", cDV, ev_ss2)
                E("ac", lambda e, tt=tt: e.activation(
                    stats[:, tt, 7:8], stats[:, tt, 5:6], AF.Ln, scale=1.0 / D,
                    bias=epscol[:]))
                E("ac", lambda e, tt=tt: e.activation(
                    stats[:, tt, 7:8], stats[:, tt, 7:8], AF.Exp, scale=-0.5), cAC)
                ev_r2 = cAC.n
                W("dv", cAC, ev_r2)
                E("dv", lambda e, tt=tt, r=r: e.scalar_tensor_tensor(
                    ringx2[:, r, :], wrk[:], stats[:, tt, 7:8], gamma_bc[:],
                    ALU.mult, ALU.mult), cDV)
                ev_tile = cDV.n
                W("sy", cDV, ev_tile)
                E("sy", lambda e, b=b, tt=tt, r=r: e.dma_start(
                    out=xn_s[b, tt * 128:(tt + 1) * 128, :], in_=ringxn[:, r, :]),
                  qSY, 16)
                E("sy", lambda e, tt=tt, r=r: e.dma_start(
                    out=xl_s[tt * 128:(tt + 1) * 128, :], in_=ringxl[:, r, :]),
                  qSY, 16)
                E("sy", lambda e, tt=tt, r=r: e.dma_start(
                    out=x2_s[tt * 128:(tt + 1) * 128, :], in_=ringx2[:, r, :]),
                  qSY, 16)
                st_hist.append(qSY.n)
            ev_stores = qSY.n

            W("sy", cPE, ev_prev_pe)
            W("sy", qSY, ev_stores)
            for dc in range(DCC):
                E("sy", lambda e, b=b, dc=dc: e.dma_start(
                    out=xnT[:, dc, :], in_=xn_s[b][:, dc * 128:(dc + 1) * 128],
                    transpose=True), qSY, 16)
                E("sy", lambda e, dc=dc: e.dma_start(
                    out=xn2T[:, dc, :], in_=x2_s[:, dc * 128:(dc + 1) * 128],
                    transpose=True), qSY, 16)
            ev_T = qSY.n

            if b == 1:
                W("gp", cPE, ev_prev_pe)
                E("gp", lambda e: e.dma_start(
                    out=Wq_b[:], in_=wq_d.rearrange("(c p) e -> p c e", p=128)),
                  qGP, 16)
                ev_wq[1] = qGP.n

            W("pe", cDV, cDV.n)
            W("pe", qSY, ev_T)
            W("pe", qGP, ev_wq[b])
            if b == 1:
                W("pe", cPE, ev_prev_pe)
            ev_qevac_hist = [0, 0]
            ev_qn_last = 0
            for tt in range(TTC):
                r = tt % 2
                if tt >= 2:
                    W("pe", cDV, ev_qevac_hist[0])
                for half in range(2):
                    for dc in range(DCC):
                        E("pe", lambda e, tt=tt, half=half, dc=dc: e.matmul(
                            psA[:, (tt % 2) * 1024 + half * 512:
                                (tt % 2) * 1024 + half * 512 + 512],
                            lhsT=xn2T[:, dc, tt * 128:(tt + 1) * 128],
                            rhs=Wq_b[:, dc, half * 512:(half + 1) * 512],
                            start=(dc == 0), stop=(dc == DCC - 1)),
                          cPE if (dc == DCC - 1 and half == 1) else None)
                ev_qpe = cPE.n
                W("dv", cPE, ev_qpe)
                if tt >= 2:
                    W("dv", qSY, ev_qn_last - 16)
                E("dv", lambda e, tt=tt, r=r: e.tensor_copy(
                    qf_v[:, r, :], psA[:, (tt % 2) * 1024:(tt % 2) * 1024 + 1024]),
                  cDV)
                ev_qevac = cDV.n
                ev_qevac_hist = [ev_qevac_hist[1], ev_qevac]
                E("dv", lambda e, r=r: e.tensor_mul(wrk[:], qf_v[:, r, :],
                                                    qf_v[:, r, :]))
                E("dv", lambda e: e.reduce_sum(
                    gco[:, 0:16], wrk[:].rearrange("p (h d) -> p h d", h=16),
                    AX.X), cDV)
                ev_ssq = cDV.n
                W("ac", cDV, ev_ssq)
                E("ac", lambda e: e.activation(gco[:, 0:16], gco[:, 0:16], AF.Ln))
                E("ac", lambda e: e.activation(gco[:, 0:16], gco[:, 0:16], AF.Exp,
                                               scale=-0.5), cAC)
                ev_rn = cAC.n
                W("dv", cAC, ev_rn)
                E("dv", lambda e: e.tensor_scalar_min(gco[:, 0:16], gco[:, 0:16],
                                                      1e12))
                for h in range(H):
                    E("dv", lambda e, r=r, h=h: e.scalar_tensor_tensor(
                        ringqn[:, r, h * 64:(h + 1) * 64],
                        qf_v[:, r, h * 64:(h + 1) * 64], gco[:, h:h + 1],
                        qs_bc[:], ALU.mult, ALU.mult),
                      cDV if h == H - 1 else None)
                ev_qn = cDV.n
                W("sy", cDV, ev_qn)
                E("sy", lambda e, tt=tt, r=r: e.dma_start(
                    out=qn_s[tt * 128:(tt + 1) * 128, :], in_=ringqn[:, r, :]),
                  qSY, 16)
                ev_qn_last = qSY.n
            ev_qproj_pe = cPE.n

            W("dv", cPE, ev_qproj_pe)
            W("dv", cDV, ev_nkb)
            E("dv", lambda e: e.memset(Vaug[:, :, :], 0.0))
            E("dv", lambda e: e.memset(Vaug[:, :, 64:65], 1.0))
            E("dv", lambda e: e.memset(kT[:, 0:128], 0.0), cDV)
            ev_vprep = cDV.n
            W("pe", cDV, ev_vprep)
            E("pe", lambda e: e.matmul(psB[0:64, 0:2], lhsT=nkb[0:2, 0:64],
                                       rhs=I_bf[0:2, 0:2], start=True, stop=True),
              cPE)
            ev_nkt = cPE.n
            W("dv", cPE, ev_nkt)
            E("dv", lambda e: e.tensor_copy(kT[0:64, 126:128], psB[0:64, 0:2]))
            E("dv", lambda e: e.tensor_copy(kT[64:128, 126:128], psB[0:64, 0:2]),
              cDV)
            ev_nkT = cDV.n
            W("sy", cDV, ev_vprep)
            W("sy", qSY, qSY.n)
            E("sy", lambda e: e.dma_start(out=Vaug[126:128, 0, 0:64],
                                          in_=nkb[0:2, 64:128]), qSY, 16)
            ev_nullv = qSY.n
            W("sy", qSY, qSY.n)

            W("pe", qGP, ev_wkv)
            W("pe", cDV, cDV.n)
            kv_hist = []
            for tt in range(TTC):
                sl = tt % 2
                if len(kv_hist) >= 2:
                    W("pe", cDV, kv_hist[-2])
                for dc in range(DCC):
                    E("pe", lambda e, tt=tt, dc=dc, sl=sl: e.matmul(
                        psA[:, 1024 + sl * 512:1024 + sl * 512 + 128],
                        lhsT=xnT[:, dc, tt * 128:(tt + 1) * 128],
                        rhs=Wkv_b[:, dc, :], start=(dc == 0),
                        stop=(dc == DCC - 1)), cPE if dc == DCC - 1 else None)
                ev_kvpe = cPE.n
                W("dv", cPE, ev_kvpe)
                E("dv", lambda e, tt=tt, sl=sl: e.tensor_copy(
                    Vaug[:, tt + 1, 0:64],
                    psA[:, 1024 + sl * 512 + 64:1024 + sl * 512 + 128]))
                E("dv", lambda e, sl=sl: e.tensor_copy(
                    wrk[:, 0:64], psA[:, 1024 + sl * 512:1024 + sl * 512 + 64]))
                E("dv", lambda e: e.tensor_mul(wrk[:, 64:128], wrk[:, 0:64],
                                               wrk[:, 0:64]))
                E("dv", lambda e: e.reduce_sum(scr1[:], wrk[:, 64:128], AX.X),
                  cDV)
                ev_kss = cDV.n
                W("ac", cDV, ev_kss)
                E("ac", lambda e: e.activation(scr2[:], scr1[:], AF.Ln))
                E("ac", lambda e: e.activation(scr2[:], scr2[:], AF.Exp,
                                               scale=-0.5), cAC)
                ev_krn = cAC.n
                W("dv", cAC, ev_krn)
                E("dv", lambda e: e.tensor_scalar_min(scr2[:], scr2[:], 1e12))
                E("dv", lambda e: e.scalar_tensor_tensor(
                    kb_t[:], wrk[:, 0:64], scr2[:], ks_bc[:], ALU.mult, ALU.mult),
                  cDV)
                ev_kbt = cDV.n
                kv_hist.append(ev_kbt)
                W("pe", cDV, ev_kbt)
                E("pe", lambda e, tt=tt: e.matmul(
                    psB[0:64, (tt % 2) * 512:(tt % 2) * 512 + 128],
                    lhsT=kb_t[:, 0:64], rhs=I_bf[:, 0:128], start=True,
                    stop=True), cPE)
                ev_ktp = cPE.n
                W("dv", cPE, ev_ktp)
                E("dv", lambda e, tt=tt: e.tensor_copy(
                    kT[0:64, 128 + tt * 128:128 + (tt + 1) * 128],
                    psB[0:64, (tt % 2) * 512:(tt % 2) * 512 + 128]))
                E("dv", lambda e, tt=tt: e.tensor_copy(
                    kT[64:128, 128 + tt * 128:128 + (tt + 1) * 128],
                    psB[0:64, (tt % 2) * 512:(tt % 2) * 512 + 128]),
                  cDV)
            ev_kdone_dv = cDV.n
            ev_kv_pe = cPE.n

            W("pe", cDV, ev_gwlo)
            W("pe", cDV, ev_kdone_dv)
            ev_lo_pc = [0] * DCC
            ev_gate_dc = [0] * DCC
            for dc in range(2):
                E("sy", lambda e, dc=dc: e.dma_start(
                    out=xnTlo_w[:, dc % 2, :],
                    in_=xl_s[:, dc * 128:(dc + 1) * 128], transpose=True),
                  qSY, 16)
                ev_lo_pc[dc] = qSY.n
            for dc in range(DCC):
                W("pe", qSY, ev_lo_pc[dc])
                for t4 in range(4):
                    E("pe", lambda e, dc=dc, t4=t4: e.matmul(
                        psB[0:8, t4 * 512:(t4 + 1) * 512], lhsT=gw_hi[:, dc, :],
                        rhs=xnT[:, dc, t4 * 512:(t4 + 1) * 512],
                        start=(dc == 0), stop=False))
                    E("pe", lambda e, dc=dc, t4=t4: e.matmul(
                        psB[0:8, t4 * 512:(t4 + 1) * 512], lhsT=gw_lo[:, dc, :],
                        rhs=xnT[:, dc, t4 * 512:(t4 + 1) * 512],
                        start=False, stop=False))
                    E("pe", lambda e, dc=dc, t4=t4: e.matmul(
                        psB[0:8, t4 * 512:(t4 + 1) * 512], lhsT=gw_hi[:, dc, :],
                        rhs=xnTlo_w[:, dc % 2, t4 * 512:(t4 + 1) * 512],
                        start=False, stop=(dc == DCC - 1)),
                      cPE)
                ev_gate_dc[dc] = cPE.n
                if dc + 2 < DCC:
                    W("sy", cPE, ev_gate_dc[dc])
                    E("sy", lambda e, dc=dc: e.dma_start(
                        out=xnTlo_w[:, dc % 2, :],
                        in_=xl_s[:, (dc + 2) * 128:(dc + 3) * 128],
                        transpose=True), qSY, 16)
                    ev_lo_pc[dc + 2] = qSY.n
            ev_gate_pe = cPE.n
            W("dv", cPE, ev_gate_pe)
            E("dv", lambda e, b=b: e.tensor_copy(pb_f[:, b * NT:(b + 1) * NT],
                                                 psB[0:8, :]), cDV)
            ev_scores = cDV.n

            W("pe", cDV, ev_scores)
            for tt in range(TTC):
                E("pe", lambda e, b=b, tt=tt: e.matmul(
                    psA[:, tt * 8:(tt + 1) * 8],
                    lhsT=pb_f[0:8, b * NT + tt * 128:b * NT + (tt + 1) * 128],
                    rhs=I_f[0:8, 0:8], start=True, stop=True),
                  cPE if tt == TTC - 1 else None)
            ev_sct = cPE.n
            W("dv", cPE, ev_sct)
            E("dv", lambda e: e.tensor_copy(
                probs_tok[:, :, :].rearrange("p c e -> p (c e)"), psA[:, 0:128]))
            E("dv", lambda e: e.tensor_reduce(
                gco[:, 0:16], probs_tok[:, :, :], AX.X, ALU.max, negate=True),
          cDV)
            ev_negmax = cDV.n
            W("ac", cDV, ev_negmax)
            for tt in range(TTC):
                E("ac", lambda e, tt=tt: e.activation(
                    probs_tok[:, tt, :], probs_tok[:, tt, :], AF.Exp,
                    bias=gco[:, tt:tt + 1]), cAC)
                ev_e = cAC.n
                W("dv", cAC, ev_e)
                E("dv", lambda e, tt=tt: e.reduce_sum(
                    stats[:, tt, 8:9], probs_tok[:, tt, :], AX.X))
                E("dv", lambda e, tt=tt: e.reciprocal(scr1[:], stats[:, tt, 8:9]))
                E("dv", lambda e, tt=tt: e.tensor_scalar(
                    probs_tok[:, tt, :], probs_tok[:, tt, :], scr1[:], None,
                    ALU.mult), cDV)
            ev_probs = cDV.n
            E("dv", lambda e, b=b: e.tensor_copy(phi_v[b][:, :, :],
                                                 probs_tok[:, :, :]))
            E("dv", lambda e, b=b: e.tensor_sub(plo_stage[:, :, :],
                                                probs_tok[:, :, :],
                                                phi_v[b][:, :, :]))
            E("dv", lambda e, b=b: e.tensor_copy(plo_v[b][:, :, :],
                                                 plo_stage[:, :, :]), cDV)
            ev_phi[b] = cDV.n
            W("pe", cDV, ev_probs)
            for tt in range(TTC):
                E("pe", lambda e, tt=tt: e.matmul(
                    psB[0:8, tt * 128:(tt + 1) * 128], lhsT=probs_tok[:, tt, :],
                    rhs=I_f[:, 0:128], start=True, stop=True),
                  cPE if tt == TTC - 1 else None)
            ev_pbt = cPE.n
            W("dv", cPE, ev_pbt)
            E("dv", lambda e, b=b: e.tensor_copy(pb_f[:, b * NT:(b + 1) * NT],
                                                 psB[0:8, :]), cDV)
            ev_pb[b] = cDV.n

            W("sy", qSY, ev_qn_last)
            W("sy", cPE, max(ev_kv_pe, ev_gate_pe))
            for dc in range(DCC):
                E("sy", lambda e, dc=dc: e.dma_start(
                    out=QT[:, dc, :], in_=qn_s[:, dc * 128:(dc + 1) * 128],
                    transpose=True), qSY, 16)
            ev_qt = qSY.n
            if ATTN_DBG == 3 and b == 0:
                W("gp", qSY, ev_qt)
                E("gp", lambda e: e.dma_start(
                    out=dbg_d[:, 0:1024], in_=qn_s[0:128, 0:1024]), qGP, 16)
                E("gp", lambda e: e.dma_start(
                    out=dbg_d[:, 1024:2048],
                    in_=xn_s[0][0:128, 0:1024]), qGP, 16)
            return ev_qt, max(ev_kdone_dv, ev_nkT), ev_nullv

        def attn_core(b, ev_qt, ev_kv_dv, ev_nullv):
            W("pe", cDV, cDV.n)
            W("pe", cAC, cAC.n)
            W("pe", qSY, ev_qt)
            W("pe", qSY, ev_nullv)
            W("pe", cDV, ev_kv_dv)
            W("ac", cDV, ev_bcast)
            slot_exp = [0, 0, 0, 0]
            eslot_pe = [0, 0]
            oslot_dv = [0, 0]
            bslot_dv = [0, 0]
            for qg in range(ATTN_QG):
                jts = 4 * qg + 5
                npairs = (jts + 1) // 2
                for h in range(ATTN_H):
                    hq = QT[(h % 2) * 64:(h % 2) * 64 + 64, h // 2,
                            qg * 512:(qg + 1) * 512]
                    so = h % 2
                    sb = 2 + (h % 2)

                    def emit_spair(pr):
                        plan_ops = []
                        for jt in (2 * pr, 2 * pr + 1):
                            if jt >= jts:
                                continue
                            sl = jt % 4
                            kb0 = (h % 2) * 64
                            dt_ = jt - 1 - 4 * qg
                            has_mask = (jt == 0) or (0 <= dt_ <= 3)
                            plan_ops.append(("w", sl))
                            plan_ops.append(lambda e, jt=jt, sl=sl, hq=hq,
                                            kb0=kb0, hm=has_mask: e.matmul(
                                psA[:, sl * 512:(sl + 1) * 512],
                                lhsT=kT[kb0:kb0 + 64,
                                        jt * 128:(jt + 1) * 128], rhs=hq,
                                start=True, stop=not hm))
                            if jt == 0:
                                plan_ops.append(lambda e, sl=sl: e.matmul(
                                    psA[:, sl * 512:(sl + 1) * 512],
                                    lhsT=I_bf[:], rhs=cm_padT[:], start=False,
                                    stop=True))
                            if 0 <= dt_ <= 3:
                                cmt = cm_q0T if (qg == 0 and dt_ == 0) else cm_stdT
                                if dt_ > 0:
                                    plan_ops.append(
                                        lambda e, sl=sl, dt_=dt_: e.matmul(
                                            psA[:, sl * 512:
                                                sl * 512 + dt_ * 128],
                                            lhsT=I_bf[:],
                                            rhs=cm_full[:, 0:dt_ * 128],
                                            start=False, stop=False))
                                plan_ops.append(
                                    lambda e, sl=sl, dt_=dt_, cmt=cmt: e.matmul(
                                        psA[:, sl * 512 + dt_ * 128:
                                            sl * 512 + dt_ * 128 + 128],
                                        lhsT=I_bf[:], rhs=cmt[:], start=False,
                                        stop=True))
                        ret = 0
                        for i, op in enumerate(plan_ops):
                            if isinstance(op, tuple):
                                W("pe", cAC, slot_exp[op[1]])
                                continue
                            last = all(isinstance(o, tuple)
                                       for o in plan_ops[i + 1:])
                            ret = E("pe", op, cPE if last else None)
                        return ret

                    def emit_exp(pr, ev_sp):
                        par = pr % 2
                        jt0 = 2 * pr
                        w = 1024 if jt0 + 1 < jts else 512
                        sl0 = jt0 % 4
                        W("ac", cPE, ev_sp)
                        W("ac", cPE, eslot_pe[par])
                        E("ac", lambda e, sl0=sl0, par=par: e.activation(
                            (EvDbg.ap().rearrange("p (s c) -> p s c", s=2)
                             if USE_EVDBG[0] else Ev)[:, par, 0:512],
                            psA[:, sl0 * 512:sl0 * 512 + 512],
                            EXP_FUNC[0],
                            bias=(zbias.ap()[:, :] if NEGC_ZERO else negC[:]),
                            scale=SCALE),
                          cAC if w == 512 else None)
                        if w == 1024:
                            E("ac", lambda e, sl0=sl0, par=par: e.activation(
                                Ev[:, par, 512:1024],
                                psA[:, sl0 * 512 + 512:sl0 * 512 + 1024],
                                EXP_FUNC[0], bias=negC[:], scale=SCALE), cAC)
                        slot_exp[sl0] = cAC.n
                        if w == 1024:
                            slot_exp[(sl0 + 1) % 4] = cAC.n
                        return cAC.n

                    def emit_pv(pr, ev_ex):
                        par = pr % 2
                        W("pe", cAC, ev_ex)
                        if pr == 0:
                            W("pe", cDV, oslot_dv[so])
                        last_inc = 0
                        for jt in (2 * pr, 2 * pr + 1):
                            if jt >= jts:
                                continue
                            E("pe", lambda e, jt=jt, par=par, so=so, jts=jts:
                              e.matmul(
                                psB[0:65, so * 512:so * 512 + 512],
                                lhsT=Vaug[:, jt, :],
                                rhs=Ev[:, par, (jt % 2) * 512:(jt % 2) * 512 + 512],
                                start=(jt == 0), stop=(jt == jts - 1)),
                              cPE)
                            last_inc = cPE.n
                        eslot_pe[par] = last_inc
                        return last_inc

                    ev_sp = [0] * npairs
                    ev_ex = [0] * npairs
                    if ATTN_SERIAL:
                        for pr in range(npairs):
                            ev_sp[pr] = emit_spair(pr)
                        for pr in range(npairs):
                            if not ATTN_EXP:
                                continue
                            ev_ex[pr] = emit_exp(pr, ev_sp[npairs - 1])
                            if ATTN_PV:
                                ev_pv = emit_pv(pr, ev_ex[pr])
                    else:
                        ev_sp[0] = emit_spair(0)
                        for pr in range(npairs):
                            if pr + 1 < npairs:
                                ev_sp[pr + 1] = emit_spair(pr + 1)
                            if not ATTN_EXP:
                                slot_exp[(2 * pr) % 4] = 0
                                continue
                            if ATTN_EXP == 2:
                                E("ac", lambda e, pr=pr: e.activation(
                                    Ev[:, pr % 2, 0:512],
                                    psA[:, ((2 * pr) % 4) * 512:
                                        ((2 * pr) % 4) * 512 + 512],
                                    AF.Identity), cAC)
                                slot_exp[(2 * pr) % 4] = 0
                                continue
                            ev_ex[pr] = emit_exp(pr, ev_sp[pr])
                            if ATTN_PV:
                                ev_pv = emit_pv(pr, ev_ex[pr])
                    if not (ATTN_NORM and ATTN_EXP and ATTN_PV):
                        continue
                    W("dv", cPE, ev_pv)
                    E("dv", lambda e, so=so: e.reciprocal(
                        linv[0:1, :], psB[64:65, so * 512:so * 512 + 512]))
                    E("dv", lambda e: e.tensor_copy(linv_bf[0:1, :],
                                                    linv[0:1, :]), cDV)
                    ev_linv = cDV.n
                    if ATTN_DBG == 2 and b == 0 and h == 0 and qg == 0:
                        W("gp", cDV, cDV.n)
                        W("gp", qSY, ev_qt)
                        E("gp", lambda e: e.dma_start(
                            out=dbg_d[:, 0:1024], in_=QT[:, 0, 0:1024]), qGP, 16)
                        E("gp", lambda e: e.dma_start(
                            out=dbg_d[:, 1024:2048], in_=kT[:, 0:1024]), qGP, 16)
                    if ATTN_DBG and b == 0 and h == 0:
                        E("dv", lambda e, so=so, qg=qg: e.tensor_copy(
                            AFp[0:1, FXIN + 1024 + qg * 512:
                                FXIN + 1536 + qg * 512],
                            psB[64:65, so * 512:so * 512 + 512]), cDV)
                        evd = cDV.n
                        W("sy", cDV, evd)
                        E("sy", lambda e, qg=qg: e.dma_start(
                            out=dbg_d[0:1, qg * 512:(qg + 1) * 512],
                            in_=AFp[0:1, FXIN + 1024 + qg * 512:
                                    FXIN + 1536 + qg * 512]), qSY, 16)
                    W("pe", cDV, bslot_dv[sb - 2])
                    W("pe", cDV, ev_linv)
                    E("pe", lambda e, sb=sb: e.matmul(
                        psB[0:64, sb * 512:sb * 512 + 512],
                        lhsT=ones_r[0:1, 0:64], rhs=linv_bf[0:1, :],
                        start=True, stop=True), cPE)
                    ev_bc = cPE.n
                    W("dv", cPE, ev_bc)
                    E("dv", lambda e, so=so: e.tensor_copy(
                        osb[0:64, :], psB[0:64, so * 512:so * 512 + 512]), cDV)
                    oslot_dv[so] = cDV.n
                    E("dv", lambda e, h=h, qg=qg, sb=sb: e.tensor_tensor(
                        OT[(h % 2) * 64:(h % 2) * 64 + 64, h // 2,
                           qg * 512:(qg + 1) * 512],
                        osb[0:64, :], psB[0:64, sb * 512:sb * 512 + 512],
                        ALU.mult), cDV)
                    bslot_dv[sb - 2] = cDV.n
            ev_ot_dv = cDV.n
            ev_core_pe = cPE.n

            ev_att_ring = [0, 0]
            wo_slot_ev = [0, 0, 0, 0]
            for dh in range(2 if ATTN_WOUT else 0):
                W("gp", cPE, ev_core_pe)
                E("gp", lambda e, dh=dh: e.dma_start(
                    out=Wo_h[:], in_=wo_d.rearrange("(c p) e -> p c e", p=128)[
                        :, :, dh * 512:(dh + 1) * 512]), qGP, 16)
                ev_wo = qGP.n
                W("pe", qGP, ev_wo)
                W("pe", cDV, ev_ot_dv)
                for tt in range(TTC):
                    sl = tt % 4
                    if wo_slot_ev[sl]:
                        W("pe", cDV, wo_slot_ev[sl])
                    for ec in range(DCC):
                        E("pe", lambda e, tt=tt, ec=ec, sl=sl: e.matmul(
                            psA[:, sl * 512:(sl + 1) * 512],
                            lhsT=OT[:, ec, tt * 128:(tt + 1) * 128],
                            rhs=Wo_h[:, ec, :], start=(ec == 0),
                            stop=(ec == DCC - 1)),
                          cPE if ec == DCC - 1 else None)
                    ev_wpe = cPE.n
                    r = tt % 2
                    W("dv", cPE, ev_wpe)
                    if ev_att_ring[r]:
                        W("dv", qGP, ev_att_ring[r])
                    E("dv", lambda e, dh=dh, sl=sl, r=r: e.tensor_copy(
                        att_v[:, r, dh * 512:(dh + 1) * 512],
                        psA[:, sl * 512:(sl + 1) * 512]), cDV)
                    ev_ev = cDV.n
                    wo_slot_ev[sl] = ev_ev
                    W("gp", cDV, ev_ev)
                    W("gp", qSY, ev_init[b])
                    E("gp", lambda e, b=b, tt=tt, dh=dh, r=r: e.dma_start(
                        out=out_d[b, tt * 128:(tt + 1) * 128,
                                  dh * 512:(dh + 1) * 512],
                        in_=att_v[:, r, dh * 512:(dh + 1) * 512],
                        accum_op=ALU.add), qGP, 16)
                    ev_att_ring[r] = qGP.n
            ev_attn_pe[b] = cPE.n
            ev_wout_add[b] = qGP.n

        def bisect():
            W("dv", cDV, max(ev_pb[0], ev_pb[1]))
            E("dv", lambda e: e.memset(lohi[:, 0:1], 0.0))
            E("dv", lambda e: e.memset(lohi[:, 2:3], 0.0))
            E("dv", lambda e: e.memset(lohi[:, 1:2], 1.0))
            E("dv", lambda e: e.memset(lohi[:, 3:4], 1.0), cDV)
            for it in range(NITER):
                for b in range(NB):
                    lo, hi, mid = 2 * b, 2 * b + 1, 4 + b
                    E("dv", lambda e, lo=lo, hi=hi, mid=mid: e.tensor_add(
                        lohi[:, mid:mid + 1], lohi[:, hi:hi + 1],
                        lohi[:, lo:lo + 1]))
                    E("dv", lambda e, mid=mid: e.tensor_scalar_mul(
                        lohi[:, mid:mid + 1], lohi[:, mid:mid + 1], 0.5))
                    for ch in range(2):
                        E("dv", lambda e, b=b, ch=ch, mid=mid: e.tensor_scalar(
                            bscr.ap()[:, :],
                            pb_f[:, b * NT + ch * 1024:b * NT + (ch + 1) * 1024],
                            lohi[:, mid:mid + 1], None, ALU.is_ge,
                            op1=ALU.add,
                            accum_out=lohi[:, 12 + ch:13 + ch]))
                    E("dv", lambda e: e.tensor_add(lohi[:, 14:15], lohi[:, 12:13],
                                                   lohi[:, 13:14]))
                    E("dv", lambda e: e.tensor_scalar(
                        lohi[:, 15:16], lohi[:, 14:15], float(KS), None,
                        ALU.is_ge))
                    E("dv", lambda e, lo=lo, mid=mid: e.tensor_sub(
                        lohi[:, 10:11], lohi[:, mid:mid + 1], lohi[:, lo:lo + 1]))
                    E("dv", lambda e, lo=lo: e.scalar_tensor_tensor(
                        lohi[:, lo:lo + 1], lohi[:, 10:11], lohi[:, 15:16],
                        lohi[:, lo:lo + 1], ALU.mult, ALU.add))
                    E("dv", lambda e, hi=hi, mid=mid: e.tensor_sub(
                        lohi[:, 10:11], lohi[:, hi:hi + 1], lohi[:, mid:mid + 1]))
                    E("dv", lambda e, hi=hi, mid=mid: e.scalar_tensor_tensor(
                        lohi[:, hi:hi + 1], lohi[:, 10:11], lohi[:, 15:16],
                        lohi[:, mid:mid + 1], ALU.mult, ALU.add), cDV)
            for b in range(NB):
                hi = 2 * b + 1
                for ch in range(2):
                    E("dv", lambda e, b=b, ch=ch, hi=hi: e.tensor_scalar(
                        bscr.ap()[:, :],
                        pb_f[:, b * NT + ch * 1024:b * NT + (ch + 1) * 1024],
                        lohi[:, hi:hi + 1], None, ALU.is_lt))
                    E("dv", lambda e, b=b, ch=ch: e.tensor_mul(
                        bscr.ap()[:, :], bscr.ap()[:, :],
                        pb_f[:, b * NT + ch * 1024:b * NT + (ch + 1) * 1024]))
                    E("dv", lambda e, ch=ch: e.tensor_reduce(
                        theta[:, 4 + ch:5 + ch], bscr.ap()[:, :], AX.X, ALU.max))
                E("dv", lambda e, b=b: e.tensor_tensor(
                    theta[:, b:b + 1], theta[:, 4:5], theta[:, 5:6], ALU.max))
                for ch in range(2):
                    E("dv", lambda e, b=b, ch=ch: e.tensor_scalar(
                        bscr.ap()[:, :],
                        pb_f[:, b * NT + ch * 1024:b * NT + (ch + 1) * 1024],
                        theta[:, b:b + 1], None, ALU.is_gt,
                        op1=ALU.add,
                        accum_out=lohi[:, 12 + ch:13 + ch]))
                E("dv", lambda e: e.tensor_add(lohi[:, 14:15], lohi[:, 12:13],
                                               lohi[:, 13:14]))
                E("dv", lambda e, b=b: e.tensor_scalar(
                    theta[:, 2 + b:3 + b], lohi[:, 14:15], -1.0, float(KS),
                    ALU.mult, ALU.add), cDV)
            return cDV.n

        def build_dst(b, ev_bis):
            W("dv", cDV, ev_bis)
            E("dv", lambda e: e.memset(theta[:, 6:8], 0.0), cDV)
            ev_last = cDV.n
            for tc in range(TTC):
                E("dv", lambda e, b=b, tc=tc: e.tensor_scalar(
                    bch[:, 256:384],
                    pb_f[:, b * NT + tc * 128:b * NT + (tc + 1) * 128],
                    theta[:, b:b + 1], None, ALU.is_gt))
                E("dv", lambda e, b=b, tc=tc: e.tensor_scalar(
                    bch[:, 384:512],
                    pb_f[:, b * NT + tc * 128:b * NT + (tc + 1) * 128],
                    theta[:, b:b + 1], None, ALU.is_equal))
                E("dv", lambda e: e.tensor_tensor_scan(
                    bch[:, 512:640], bch[:, 384:512], bch[:, 0:128],
                    theta[:, 6:7], ALU.add, ALU.add))
                E("dv", lambda e: e.tensor_copy(theta[:, 6:7], bch[:, 639:640]))
                E("dv", lambda e, b=b: e.tensor_scalar(
                    bch[:, 512:640], bch[:, 512:640], theta[:, 2 + b:3 + b],
                    None, ALU.is_le))
                E("dv", lambda e: e.tensor_mul(bch[:, 384:512], bch[:, 384:512],
                                               bch[:, 512:640]))
                E("dv", lambda e: e.tensor_add(bch[:, 256:384], bch[:, 256:384],
                                               bch[:, 384:512]))
                E("dv", lambda e: e.tensor_tensor_scan(
                    bch[:, 512:640], bch[:, 256:384], bch[:, 0:128],
                    theta[:, 7:8], ALU.add, ALU.add))
                E("dv", lambda e: e.tensor_copy(theta[:, 7:8], bch[:, 639:640]))
                E("dv", lambda e: e.tensor_scalar_add(bch[:, 512:640],
                                                      bch[:, 512:640],
                                                      -1.0 - BIGF))
                E("dv", lambda e: e.tensor_mul(bch[:, 512:640], bch[:, 512:640],
                                               bch[:, 256:384]))
                E("dv", lambda e: e.tensor_scalar_add(bch[:, 640:768],
                                                      bch[:, 512:640], BIGF),
                  cDV)
                ev_dst = cDV.n
                W("pe", cDV, ev_dst)
                E("pe", lambda e, tc=tc: e.matmul(
                    psB[0:128, 1024 + (tc % 2) * 512:1024 + (tc % 2) * 512 + 8],
                    lhsT=bch[0:8, 640:768], rhs=I_f[0:8, 0:8], start=True,
                    stop=True), cPE)
                ev_t = cPE.n
                W("dv", cPE, ev_t)
                E("dv", lambda e, b=b, tc=tc: e.tensor_copy(
                    dstT[b][:, tc, :],
                    psB[:, 1024 + (tc % 2) * 512:1024 + (tc % 2) * 512 + 8]),
                  cDV)
                ev_last = cDV.n
            return ev_last

        ev_prep_xgT = {}
        ev_prep_mc = {}
        def prep_a(b, e, par, ev_dst_done):
            W("dv", cDV, ev_dst_done)
            W("dv", cDV, ev_phi[b])
            E("dv", lambda e_, b=b, e=e: e_.tensor_copy(
                pack_v[:, :, 2:3], phi_v[b][:, :, e:e + 1]))
            E("dv", lambda e_, b=b, e=e: e_.tensor_copy(
                pack_v[:, :, 3:4], plo_v[b][:, :, e:e + 1]), cDV)
            ev_pk = cDV.n
            W("pe", cDV, ev_pk)
            ev_round = [0] * 8
            for rnd in range(8):
                pp = rnd % 2
                if rnd >= 2:
                    W("dv", cPE, ev_round[rnd - 2])
                for i in range(2):
                    tc = rnd * 2 + i
                    E("dv", lambda e_, b=b, e=e, tc=tc, pp=pp, i=i:
                      e_.tensor_scalar(
                        Msel_v[:, pp * 2 + i, :], iota_s[:],
                        dstT[b][:, tc, e:e + 1], None, ALU.is_equal),
                      cDV if i == 1 else None)
                ev_ms = cDV.n
                W("pe", cDV, ev_ms)
                for i in range(2):
                    tc = rnd * 2 + i
                    E("pe", lambda e_, tc=tc, pp=pp, i=i: e_.matmul(
                        psB[0:4, 0:512], lhsT=pack_v[:, tc, :],
                        rhs=Msel_v[:, pp * 2 + i, :], start=(tc == 0),
                        stop=(tc == 15)), cPE if i == 1 else None)
                ev_round[rnd] = cPE.n
            ev_ext = cPE.n
            W("dv", cPE, ev_ext)
            E("dv", lambda e_: e_.tensor_copy(ext_sb[0:4, :], psB[0:4, 0:512]),
              cDV)
            ev_exts = cDV.n
            W("pe", cDV, ev_exts)
            for st in range(4):
                E("pe", lambda e_, st=st: e_.matmul(
                    psB[:, 512 + st * 4:512 + st * 4 + 4],
                    lhsT=ext_sb[0:4, st * 128:(st + 1) * 128], rhs=I_f[0:4, 0:4],
                    start=True, stop=True), cPE if st == 3 else None)
            ev_extT = cPE.n
            W("dv", cPE, ev_extT)
            E("dv", lambda e_: e_.tensor_copy(
                mg_v[:, :, :].rearrange("p s k -> p (s k)"), psB[:, 512:528]),
              cDV)
            for st in range(4):
                E("dv", lambda e_, st=st, b=b: e_.tensor_scalar(
                    scr1[:], mg_v[:, st, 0:1], 64.0, float(b * NT), ALU.mult,
                    ALU.add))
                E("dv", lambda e_, st=st: e_.tensor_add(scr1[:], scr1[:],
                                                        mg_v[:, st, 1:2]))
                E("dv", lambda e_, st=st, b=b, e=e: e_.tensor_copy(
                    mcol_i[(b * 8 + e) * 4 + st].ap()[:, :], scr1[:]))
                E("dv", lambda e_, st=st: e_.tensor_add(
                    scr2[:], mg_v[:, st, 2:3], mg_v[:, st, 3:4]))
                E("dv", lambda e_, st=st, par=par: e_.tensor_copy(
                    gco[:, 16 + par * 4 + st:17 + par * 4 + st], scr2[:]), cDV)
            ev_mcol = cDV.n
            ev_prep_mc[(b, e)] = ev_mcol
            W("gp", cDV, ev_mcol)
            gev = []
            for st in (0, 1):
                E("gp", lambda e_, st=st, b=b, e=e: e_.indirect_dma_start(
                    out=Xg_v[:, st % 2, :], out_offset=None,
                    in_=xn_s.rearrange("b n d -> (b n) d"),
                    in_offset=IndirectOffsetOnAxis(
                        mcol_i[(b * 8 + e) * 4 + st].ap()[:, :], 0)),
                  qGP, 16)
                gev.append(qGP.n)
            return gev, par

        def prep_b(b, e, gevpar):
            gev, par = gevpar
            xg_ev = [0, 0]
            for sh in range(2):
                if sh == 1:
                    W("gp", cPE, cPE.n)
                    for st in (2, 3):
                        E("gp", lambda e_, st=st, b=b, e=e: e_.indirect_dma_start(
                            out=Xg_v[:, st % 2, :], out_offset=None,
                            in_=xn_s.rearrange("b n d -> (b n) d"),
                            in_offset=IndirectOffsetOnAxis(
                                mcol_i[(b * 8 + e) * 4 + st].ap()[:, :], 0)),
                          qGP, 16)
                        gev.append(qGP.n)
                W("pe", qGP, gev[2 * sh + 1])
                for st in (2 * sh, 2 * sh + 1):
                    for dg in range(2):
                        par2 = dg % 2
                        if xg_ev[par2]:
                            W("pe", cDV, xg_ev[par2])
                        for dc4 in range(4):
                            dc = dg * 4 + dc4
                            E("pe", lambda e_, st=st, dc=dc, dc4=dc4, par2=par2:
                              e_.matmul(
                                psB[:, 512 + par2 * 512 + dc4 * 128:
                                    512 + par2 * 512 + dc4 * 128 + 128],
                                lhsT=Xg_v[:, st % 2, dc * 128:(dc + 1) * 128],
                                rhs=I_bf[:], start=True, stop=True),
                              cPE if dc4 == 3 else None)
                        evt = cPE.n
                        W("dv", cPE, evt)
                        E("dv", lambda e_, st=st, dg=dg, par2=par2:
                          e_.tensor_copy(
                            XgT_v[:, dg * 4:(dg + 1) * 4,
                                  st * 128:(st + 1) * 128],
                            psB[:, 512 + par2 * 512:1024 + par2 * 512]
                            .rearrange("p (c t) -> p c t", c=4)), cDV)
                        xg_ev[par2] = cDV.n
            ev_prep_xgT[(b, e)] = cDV.n

        ev_pc_box = [0]
        def ffn(b):
            W("pe", cDV, cDV.n)
            W("pe", cAC, cAC.n)
            ev_w1 = {}
            W("sy", cGP, ev_pc_box[0])
            W("sy", cPE, ev_attn_pe[1])
            W("sy", cDV, cDV.n)
            W("gp", cPE, ev_attn_pe[1])
            for hh in range(2):
                E("gp", lambda e, hh=hh: e.dma_start(
                    out=w1f[:, hh * 4:(hh + 1) * 4, :],
                    in_=w1_d[0].rearrange("(c p) f -> p c f", p=128)[
                        :, hh * 4:(hh + 1) * 4, 0:2048]), qGP, 16)
            ev_w1[(0, 0)] = qGP.n
            ev_scat = ev_wout_add[b]
            ev_gelu = {}
            ev_psA_free = [0, 0, 0, 0]
            ev_psB_free = [0, 0, 0, 0]
            ev_w2_rd = [0, 0]
            for e in range(NE):
                par = e % 2
                for fh in range(2):
                    W("pe", qGP, ev_w1[(e, fh)])
                    W("pe", cDV, ev_prep_xgT[(b, e)])
                    for ft in range(16):
                        sl = ft % 2
                        if (e, fh, ft - 2) in ev_gelu:
                            W("pe", cAC, ev_gelu[(e, fh, ft - 2)])
                        if ft < 2:
                            W("pe", cDV, ev_psA_free[sl])
                        for dc in range(DCC):
                            E("pe", lambda e_, ft=ft, dc=dc, sl=sl:
                              e_.matmul(
                                psA[:, sl * 512:(sl + 1) * 512],
                                lhsT=w1f[:, dc, ft * 128:(ft + 1) * 128],
                                rhs=XgT_v[:, dc, :], start=(dc == 0),
                                stop=(dc == DCC - 1)),
                              cPE if dc == DCC - 1 else None)
                        ev_h = cPE.n
                        W("ac", cPE, ev_h)
                        E("ac", lambda e_, ft=ft, sl=sl: e_.activation(
                            hT_v[:, ft, :], psA[:, sl * 512:(sl + 1) * 512],
                            AF.Gelu_apprx_tanh), cAC)
                        ev_gelu[(e, fh, ft)] = cAC.n
                    ev_w1_done = cPE.n
                    W("gp", cPE, ev_w1_done)
                    if fh == 0:
                        for hh in range(2):
                            E("gp", lambda e_, e=e, hh=hh: e_.dma_start(
                                out=w1f[:, hh * 4:(hh + 1) * 4, :],
                                in_=w1_d[e].rearrange(
                                    "(c p) f -> p c f", p=128)[
                                    :, hh * 4:(hh + 1) * 4, 2048:4096]),
                              qGP, 16)
                        ev_w1[(e, 1)] = qGP.n
                    elif e < NE - 1:
                        for hh in range(2):
                            E("gp", lambda e_, e=e, hh=hh: e_.dma_start(
                                out=w1f[:, hh * 4:(hh + 1) * 4, :],
                                in_=w1_d[e + 1].rearrange(
                                    "(c p) f -> p c f", p=128)[
                                    :, hh * 4:(hh + 1) * 4, 0:2048]), qGP, 16)
                        ev_w1[(e + 1, 0)] = qGP.n
                    if fh == 0 and e < NE - 1:
                        gev = prep_a(b, e + 1, (e + 1) % 2, 0)
                    if fh == 1 and e < NE - 1:
                        prep_b(b, e + 1, gev)
                        W("pe", cDV, ev_prep_xgT[(b, e + 1)])
                    W("pe", cDV, max(ev_psB_free))
                    W("pe", cAC, ev_gelu[(e, fh, 15)])
                    for g in range(4):
                        pr = g % 2
                        W("sy", cPE, ev_w2_rd[pr])
                        E("sy", lambda e_, e=e, fh=fh, g=g, pr=pr: e_.dma_start(
                            out=w2g_v[:, pr, :, :],
                            in_=w2b_s[e].rearrange("(c p) d -> p c d", p=128)[
                                :, fh * 16 + g * 4:fh * 16 + (g + 1) * 4, :]),
                          qSY, 16)
                        ev_w2 = qSY.n
                        W("pe", qSY, ev_w2)
                        for fc in range(4):
                            fcg = g * 4 + fc
                            W("pe", cAC, ev_gelu[(e, fh, fcg)])
                            for st in range(4):
                                for dh in range(2):
                                    ps = psA if dh == 0 else psB
                                    first = (g == 0 and fc == 0)
                                    last = (g == 3 and fc == 3)
                                    E("pe", lambda e_, fcg=fcg, st=st, dh=dh,
                                      pr=pr, fc=fc, first=first, last=last,
                                      ps=ps: e_.matmul(
                                        ps[:, st * 512:(st + 1) * 512],
                                        lhsT=hT_v[:, fcg,
                                                  st * 128:(st + 1) * 128],
                                        rhs=w2g_v[:, pr, fc,
                                                  dh * 512:(dh + 1) * 512],
                                        start=first, stop=last),
                                      cPE if (st == 3 and dh == 1) else None)
                        ev_w2_rd[pr] = cPE.n
                    ev_w2pe = cPE.n
                    W("dv", cPE, ev_w2pe)
                    for st in range(4):
                        for dh in range(2):
                            ps = psA if dh == 0 else psB
                            if fh == 0:
                                E("dv", lambda e_, st=st, dh=dh, ps=ps, par=par:
                                  e_.tensor_scalar(
                                    gOut[:, st, dh * 512:(dh + 1) * 512],
                                    ps[:, st * 512:(st + 1) * 512],
                                    gco[:, 16 + par * 4 + st:
                                        17 + par * 4 + st], None, ALU.mult),
                                  cDV)
                            else:
                                E("dv", lambda e_, st=st, dh=dh, ps=ps, par=par:
                                  e_.scalar_tensor_tensor(
                                    gOut[:, st, dh * 512:(dh + 1) * 512],
                                    ps[:, st * 512:(st + 1) * 512],
                                    gco[:, 16 + par * 4 + st:
                                        17 + par * 4 + st],
                                    gOut[:, st, dh * 512:(dh + 1) * 512],
                                    ALU.mult, ALU.add), cDV)
                            if dh == 0:
                                ev_psA_free[st] = cDV.n
                            else:
                                ev_psB_free[st] = cDV.n
                    ev_gout = cDV.n
                W("gp", cDV, ev_gout)
                W("gp", qGP, ev_scat)
                for st in range(4):
                    E("gp", lambda e_, st=st, b=b, e=e: e_.indirect_dma_start(
                        out=out_d.rearrange("b n d -> (b n) d"),
                        out_offset=IndirectOffsetOnAxis(
                            mcol_i[(b * 8 + e) * 4 + st].ap()[:, :], 0),
                        in_=gOut[:, st, :], in_offset=None, compute_op=ALU.add),
                      qGP, 16)
                ev_scat = qGP.n
            return ev_scat

        if stage >= 1:
            r0 = p1(0, 0)
        if stage >= 6:
            for e in range(NE):
                E("gp", lambda e_, e=e: e_.dma_start(
                    out=w2b_s[e], in_=w2_d[e]), qGP, 16)
            W("gp", qGP, qGP.n)
            E("gp", lambda e_: e_.memset(zbias.ap()[:, :], 0.0), cGP)
            ev_pc_box[0] = cGP.n
        if stage >= 2:
            attn_core(0, *r0)
        if stage >= 3:
            r1 = p1(1, ev_attn_pe[0])
        if stage >= 5:
            attn_core(1, *r1)
        if stage >= 4:
            ev_bis = bisect()
            ev_dst0 = build_dst(0, ev_bis)
            gev0 = prep_a(0, 0, 0, ev_dst0)
            prep_b(0, 0, gev0)
        if stage >= 6:
            ffn(0)
        if stage >= 7:
            ev_dst1 = build_dst(1, ev_bis)
            gev1 = prep_a(1, 0, 0, ev_dst1)
            prep_b(1, 0, gev1)
            ffn(1)

        W("sy", qGP, qGP.n)
        W("sy", qSY, qSY.n)

        blk = ctx.enter_context(nc.Block())
        blk.tensor(lambda t: [f(t) for f in P["pe"]])
        blk.vector(lambda v: [f(v) for f in P["dv"]])
        blk.scalar(lambda s: [f(s) for f in P["ac"]])
        blk.gpsimd(lambda g: [f(g) for f in P["gp"]])
        blk.sync(lambda s: [f(s) for f in P["sy"]])
    return nc


_NC_CACHE = [None]


def kernel_stage(stage, ins):
    nc = build_program(stage)
    return _run(nc, ins)


def _run(nc, kw):
    x = np.ascontiguousarray(np.asarray(kw["L0_attn_x"], dtype=np.float32))
    shared = {
        "gamma": np.ascontiguousarray(
            np.asarray(kw["L0_attn_gamma"], np.float32).reshape(1, D)),
        "Wq": np.ascontiguousarray(np.asarray(kw["L0_attn_Wq"], np.float32)),
        "Wkv": np.ascontiguousarray(np.asarray(kw["L0_attn_Wkv"], np.float32)),
        "qsc": np.ascontiguousarray(
            np.asarray(kw["L0_attn_q_scale"], np.float32).reshape(1, DH)),
        "ksc": np.ascontiguousarray(
            np.asarray(kw["L0_attn_k_scale"], np.float32).reshape(1, DH)),
        "nullkv": np.ascontiguousarray(
            np.asarray(kw["L0_attn_null_kv"], np.float32)),
        "Wout": np.ascontiguousarray(np.asarray(kw["L0_attn_Wout"], np.float32)),
        "gate_w": np.ascontiguousarray(
            np.asarray(kw["L0_ffn_gate_w"], np.float32)),
        "w1": np.ascontiguousarray(np.asarray(kw["L0_ffn_w1"], np.float32)),
        "w2": np.ascontiguousarray(np.asarray(kw["L0_ffn_w2"], np.float32)),
    }
    in_maps = []
    for c in range(8):
        m = dict(shared)
        m["x"] = x[c * NB:(c + 1) * NB]
        in_maps.append(m)
    res = run_bass_kernel_spmd(nc, in_maps, core_ids=list(range(8)))
    return np.concatenate([r["y"] for r in res.results], axis=0)


def kernel(L0_attn_x, L0_attn_gamma, L0_attn_Wq, L0_attn_Wkv, L0_attn_q_scale,
           L0_attn_k_scale, L0_attn_null_kv, L0_attn_Wout, L0_ffn_gate_w,
           L0_ffn_w1, L0_ffn_w2):
    if _NC_CACHE[0] is None:
        _NC_CACHE[0] = build_program()
    return _run(_NC_CACHE[0], dict(
        L0_attn_x=L0_attn_x, L0_attn_gamma=L0_attn_gamma, L0_attn_Wq=L0_attn_Wq,
        L0_attn_Wkv=L0_attn_Wkv, L0_attn_q_scale=L0_attn_q_scale,
        L0_attn_k_scale=L0_attn_k_scale, L0_attn_null_kv=L0_attn_null_kv,
        L0_attn_Wout=L0_attn_Wout, L0_ffn_gate_w=L0_ffn_gate_w,
        L0_ffn_w1=L0_ffn_w1, L0_ffn_w2=L0_ffn_w2))


# version 03 (final)
# speedup vs baseline: 1.1622x; 1.1369x over previous
import numpy as np
import ml_dtypes
import concourse.bass as bass
import concourse.mybir as mybir
from concourse.bass_utils import run_bass_kernel_spmd
from concourse.bass_core_rust import IndirectOffsetOnAxis
from contextlib import ExitStack

F32 = mybir.dt.float32
BF16 = mybir.dt.bfloat16
I32 = mybir.dt.int32
AF = mybir.ActivationFunctionType
ALU = mybir.AluOpType
AX = mybir.AxisListType

NB, NT, D, H, DH, NE, KS, HID = 2, 2048, 1024, 16, 64, 8, 512, 4096
TTC, DCC = NT // 128, D // 128
NEG = -1000.0
BIGF = 65536.0
LN_EPS = 1e-5
SCALE = 8.0
HEADROOM = 60.0
NITER = 48
EVDBG = False
ATTN_DBG = False
ATTN_SERIAL = False
NEGC_ZERO = False
ATTN_QG = 4
ATTN_H = 16
ATTN_WOUT = True
ATTN_NORM = True
ATTN_EXP = True
EXP_FUNC = [AF.Exp]
ATTN_PV = True


class Sem:
    __slots__ = ("h", "n")
    def __init__(self, h):
        self.h = h
        self.n = 0


def build_program(stage=9):
    nc = bass.Bass("TRN2", target_bir_lowering=False, detect_race_conditions=False)

    x_d = nc.dram_tensor("x", [NB, NT, D], F32, kind="ExternalInput").ap()
    gamma_d = nc.dram_tensor("gamma", [1, D], F32, kind="ExternalInput").ap()
    wq_d = nc.dram_tensor("Wq", [D, D], F32, kind="ExternalInput").ap()
    wkv_d = nc.dram_tensor("Wkv", [D, 2 * DH], F32, kind="ExternalInput").ap()
    qs_d = nc.dram_tensor("qsc", [1, DH], F32, kind="ExternalInput").ap()
    ks_d = nc.dram_tensor("ksc", [1, DH], F32, kind="ExternalInput").ap()
    nkv_d = nc.dram_tensor("nullkv", [2, 2, DH], F32, kind="ExternalInput").ap()
    wo_d = nc.dram_tensor("Wout", [D, D], F32, kind="ExternalInput").ap()
    gw_d = nc.dram_tensor("gate_w", [D, NE], F32, kind="ExternalInput").ap()
    w1_d = nc.dram_tensor("w1", [NE, D, HID], F32, kind="ExternalInput").ap()
    w2_d = nc.dram_tensor("w2", [NE, HID, D], F32, kind="ExternalInput").ap()
    out_d = nc.dram_tensor("y", [NB, NT, D], F32, kind="ExternalOutput").ap()
    dbg_d = nc.dram_tensor("dbg", [128, 2048], F32, kind="ExternalOutput").ap()
    xn_s = nc.dram_tensor("xn_scr", [NB, NT, D], BF16, kind="Internal").ap()
    xl_s = nc.dram_tensor("xl_scr", [NT, D], BF16, kind="Internal").ap()
    x2_s = nc.dram_tensor("x2_scr", [NT, D], BF16, kind="Internal").ap()
    qn_s = nc.dram_tensor("qn_scr", [NT, D], BF16, kind="Internal").ap()
    w2b_s = nc.dram_tensor("w2b_scr", [NE, HID, D], BF16, kind="Internal").ap()

    ctx = ExitStack()
    with ctx:
        BFC = 58700
        AB = ctx.enter_context(nc.sbuf_tensor("AB", [128, BFC], BF16))
        bo = [0]
        def bfa(n):
            c = bo[0]; bo[0] += n
            assert bo[0] <= BFC
            return c
        R1 = bfa(16384); R2 = bfa(16384); R3 = bfa(4096); R4 = bfa(8192)
        W8 = bfa(8192); RM = bfa(1280); RON = bfa(128); RQK = bfa(128)
        RPK = bfa(64); RPH = bfa(512); RKV = bfa(1024); RGW = bfa(128)
        RLB = bfa(512); RSQ = bfa(512); RGM = bfa(1024); RKB = bfa(64)

        bvv = lambda c, n: AB[:, c:c + n]
        xnT = bvv(R1, 16384).rearrange("p (c t) -> p c t", c=DCC)
        QT = xnT
        xn2T = bvv(R2, 16384).rearrange("p (c t) -> p c t", c=DCC)
        OT = xn2T
        xnTlo_w = bvv(R3, 4096).rearrange("p (r t) -> p r t", r=2)
        w1f = bvv(R1, 16384).rearrange("p (c f) -> p c f", c=DCC)
        w2g_v = bvv(R2, 8192).rearrange("p (s c d) -> p s c d", s=2, c=4)
        hT_v = bvv(R2 + 8192, 8192).rearrange("p (c s) -> p c s", c=16)
        ringxn = bvv(R4, 2048).rearrange("p (r c) -> p r c", r=2)
        ringqn = ringxn
        ringx2 = bvv(R4 + 2048, 2048).rearrange("p (r c) -> p r c", r=2)
        ringxl = bvv(R4 + 4096, 2048).rearrange("p (r c) -> p r c", r=2)
        XgT_v = bvv(R4, 4096).rearrange("p (c s) -> p c s", c=DCC)
        Msel_v = bvv(R4 + 4096, 2048).rearrange("p (r s) -> p r s", r=4)
        Xg_v = bvv(R4 + 6144, 2048).rearrange("p (s d) -> p s d", s=2)
        Wq_b = bvv(W8, 8192).rearrange("p (c e) -> p c e", c=DCC)
        kT = bvv(W8, 2176)
        Vaug = bvv(W8 + 2176, 17 * 65).rearrange("p (j v) -> p j v", j=17)
        Ev = bvv(W8 + 3296, 2048).rearrange("p (s c) -> p s c", s=2)
        Wo_h = bvv(W8 + 4096, 4096).rearrange("p (c e) -> p c e", c=DCC)
        cm_stdT = bvv(RM, 128)
        cm_q0T = bvv(RM + 128, 128)
        I_bf = bvv(RM + 256, 128)
        cm_padT = bvv(RM + 384, 512)
        cm_full = bvv(RM + 896, 384)
        ones_r = bvv(RON, 128)
        qs_bf = bvv(RQK, 64)
        ks_bf = bvv(RQK + 64, 64)
        pack_v = bvv(RPK, 64).rearrange("p (c k) -> p c k", c=16)
        phi_v = [bvv(RPH, 128).rearrange("p (c e) -> p c e", c=16),
                 bvv(RPH + 256, 128).rearrange("p (c e) -> p c e", c=16)]
        plo_v = [bvv(RPH + 128, 128).rearrange("p (c e) -> p c e", c=16),
                 bvv(RPH + 384, 128).rearrange("p (c e) -> p c e", c=16)]
        Wkv_b = bvv(RKV, 1024).rearrange("p (c e) -> p c e", c=DCC)
        gw_hi = bvv(RGW, 64).rearrange("p (c e) -> p c e", c=DCC)
        gw_lo = bvv(RGW + 64, 64).rearrange("p (c e) -> p c e", c=DCC)
        linv_bf = bvv(RLB, 512)
        sq_bf = bvv(RSQ, 512)
        gamma_bc = bvv(RGM, 1024)
        kb_t = bvv(RKB, 64)

        FC = 10700
        AFp = ctx.enter_context(nc.sbuf_tensor("AFp", [128, FC], F32))
        fo = [0]
        def fa(n):
            c = fo[0]; fo[0] += n
            assert fo[0] <= FC
            return c
        fvv = lambda c, n: AFp[:, c:c + n]
        FXIN = fa(2048); FQ = fa(2048); FWRK = fa(1024); FXNF = fa(1024)
        FST = fa(256); FQS = fa(128); FIOT = fa(512); FCOL = fa(16)
        FDT = fa(256); FMG = fa(64); FNC = fa(16); FIF = fa(128)
        FATT = fa(2048); FOSB = fa(512); FLV = fa(512); FGW = fa(64)
        xin_v = fvv(FXIN, 2048).rearrange("p (r c) -> p r c", r=2)
        qf_v = fvv(FQ, 2048).rearrange("p (r c) -> p r c", r=2)
        gOut = fvv(FXIN, 4096).rearrange("p (s d) -> p s d", s=4)
        wrk = fvv(FWRK, 1024)
        probs_tok = fvv(FWRK, 128).rearrange("p (c e) -> p c e", c=16)
        ext_sb = fvv(FWRK + 128, 512)
        plo_stage = fvv(FWRK + 640, 128).rearrange("p (c e) -> p c e", c=16)
        xnf = fvv(FXNF, 1024)
        stats = fvv(FST, 256).rearrange("p (c k) -> p c k", c=16)
        qs_bc = fvv(FQS, 64)
        ks_bc = fvv(FQS + 64, 64)
        iota_s = fvv(FIOT, 512)
        p_col = fvv(FCOL, 1)
        ge_col = fvv(FCOL + 1, 1)
        tmod_col = fvv(FCOL + 2, 1)
        scr1 = fvv(FCOL + 3, 1)
        scr2 = fvv(FCOL + 4, 1)
        epscol = fvv(FCOL + 5, 1)
        dstT = [fvv(FDT, 128).rearrange("p (c e) -> p c e", c=16),
                fvv(FDT + 128, 128).rearrange("p (c e) -> p c e", c=16)]
        mg_v = fvv(FMG, 16).rearrange("p (s k) -> p s k", s=4)
        negC = fvv(FNC, 1)
        I_f = fvv(FIF, 128)
        att_v = fvv(FATT, 2048).rearrange("p (r c) -> p r c", r=2)
        osb = fvv(FOSB, 512)
        linv = fvv(FLV, 512)
        gwf_st = fvv(FGW, 64).rearrange("p (c e) -> p c e", c=DCC)

        pb = ctx.enter_context(nc.sbuf_tensor("pb", [8, 2 * NT], F32))
        pb_i = pb.ap().bitcast(I32)
        pb_f = pb.ap()
        bscr = ctx.enter_context(nc.sbuf_tensor("bscr", [8, 1024], F32))
        bch = ctx.enter_context(nc.sbuf_tensor("bch", [8, 768], F32))
        lohi = ctx.enter_context(nc.sbuf_tensor("lohi", [8, 16], F32))
        lohi_i = lohi.ap().bitcast(I32)
        theta = ctx.enter_context(nc.sbuf_tensor("theta", [8, 8], F32))
        mcol_i = [ctx.enter_context(nc.sbuf_tensor(f"mcol{k}", [128, 1], I32))
                  for k in range(64)]
        gco = ctx.enter_context(nc.sbuf_tensor("gco", [128, 24], F32))
        nkv_sb = ctx.enter_context(nc.sbuf_tensor("nkv_sb", [2, 4 * DH], F32))
        nkv2 = ctx.enter_context(nc.sbuf_tensor("nkv2", [2, 4 * DH], F32))
        nkb = ctx.enter_context(nc.sbuf_tensor("nkb", [2, 2 * DH], BF16))
        EvDbg = ctx.enter_context(nc.sbuf_tensor("EvDbg", [128, 2048], BF16))
        zbias = ctx.enter_context(nc.sbuf_tensor("zbias", [128, 1], F32))
        USE_EVDBG = [EVDBG]

        psA = ctx.enter_context(nc.psum_tensor("psA", [128, 2048], F32))
        psB = ctx.enter_context(nc.psum_tensor("psB", [128, 2048], F32))

        qSY = Sem(ctx.enter_context(nc.semaphore("qSY")))
        qGP = Sem(ctx.enter_context(nc.semaphore("qGP")))
        cPE = Sem(ctx.enter_context(nc.semaphore("cPE")))
        cDV = Sem(ctx.enter_context(nc.semaphore("cDV")))
        cAC = Sem(ctx.enter_context(nc.semaphore("cAC")))
        cGP = Sem(ctx.enter_context(nc.semaphore("cGP")))

        P = {k: [] for k in ("pe", "dv", "ac", "gp", "sy")}
        max_waited = {}
        last_qwait = {}
        def W(eng, sem, val):
            if val > 0:
                if id(sem) in max_waited:
                    max_waited[id(sem)] = max(max_waited[id(sem)], val)
                P[eng].append(lambda e, s=sem.h, v=val: e.wait_ge(s, v))
        auto_sem = {"dv": cDV, "ac": cAC, "gp": cGP}
        def E(eng, fn, sem=None, k=1):
            if sem is not None and k == 16:
                if id(sem) not in max_waited:
                    max_waited[id(sem)] = 0
                    last_qwait[id(sem)] = 0
                need = max_waited[id(sem)]
                if last_qwait[id(sem)] < need:
                    last_qwait[id(sem)] = need
                    P[eng].append(lambda e, s=sem.h, v=need: e.wait_ge(s, v))
            if sem is None and eng in auto_sem:
                sem = auto_sem[eng]
            if sem is None:
                P[eng].append(fn)
                return 0
            sem.n += k
            P[eng].append(lambda e, f=fn, s=sem.h, kk=k: f(e).then_inc(s, kk))
            if k == 1 and eng in auto_sem and sem is auto_sem[eng]:
                P[eng].append(lambda e, s=sem.h, v=sem.n: e.wait_ge(s, v))
            return sem.n
        EMIT = E
        WAIT = W

        ev_init = []
        for b in range(NB):
            E("sy", lambda e, b=b: e.dma_start(out=out_d[b], in_=x_d[b]), qSY, 16)
            ev_init.append(qSY.n)

        E("gp", lambda e: e.iota(iota_s[:], pattern=[[1, 512]], base=0,
                                 channel_multiplier=0,
                                 allow_small_or_imprecise_dtypes=True))
        E("gp", lambda e: e.iota(p_col[:], pattern=[[1, 1]], base=0,
                                 channel_multiplier=1,
                                 allow_small_or_imprecise_dtypes=True))
        E("gp", lambda e: e.memset(cm_full[:], NEG))
        E("gp", lambda e: e.memset(wrk[:, 0:512], 0.0), cGP)
        ev_z = cGP.n
        W("gp", cGP, ev_z)
        E("gp", lambda e: e.affine_select(cm_stdT[:], wrk[:, 0:128],
                                          pattern=[[1, 128]], compare_op=ALU.is_ge,
                                          fill=NEG, base=0, channel_multiplier=-1))
        E("gp", lambda e: e.affine_select(cm_padT[:], wrk[:, 0:512],
                                          pattern=[[0, 512]], compare_op=ALU.is_ge,
                                          fill=NEG, base=-126, channel_multiplier=1),
          cGP)
        ev_gpc = cGP.n

        E("gp", lambda e: e.dma_start(
            out=Wq_b[:], in_=wq_d.rearrange("(c p) e -> p c e", p=128)), qGP, 16)
        ev_wq = [qGP.n, 0]
        E("gp", lambda e: e.dma_start(
            out=Wkv_b[:], in_=wkv_d.rearrange("(c p) e -> p c e", p=128)), qGP, 16)
        ev_wkv = qGP.n
        E("gp", lambda e: e.dma_start(
            out=gw_hi[:], in_=gw_d.rearrange("(c p) e -> p c e", p=128)), qGP, 16)
        E("gp", lambda e: e.dma_start(
            out=gwf_st[:, :, :],
            in_=gw_d.rearrange("(c p) e -> p c e", p=128)), qGP, 16)
        ev_gwf = qGP.n

        E("sy", lambda e: e.dma_start(out=qs_bc[0:1, :], in_=qs_d), qSY, 16)
        E("sy", lambda e: e.dma_start(out=ks_bc[0:1, :], in_=ks_d), qSY, 16)
        E("sy", lambda e: e.dma_start(out=xnf[0:1, :], in_=gamma_d), qSY, 16)
        E("sy", lambda e: e.dma_start(
            out=nkv_sb[:, 0:128].rearrange("b (a d) -> b a d", a=2),
            in_=nkv_d.rearrange("a b d -> b a d")), qSY, 16)
        ev_smallw = qSY.n

        W("dv", qSY, ev_smallw)
        W("dv", cGP, ev_gpc)
        E("dv", lambda e: e.memset(epscol[:], LN_EPS))
        E("dv", lambda e: e.memset(zbias.ap()[:, :], 0.0))
        E("dv", lambda e: e.memset(ones_r[:], 1.0))
        E("dv", lambda e: e.tensor_scalar(I_f[:], iota_s[:, 0:128], p_col[:], None,
                                          ALU.is_equal))
        E("dv", lambda e: e.tensor_copy(I_bf[:], I_f[:]))
        E("dv", lambda e: e.tensor_copy(cm_q0T[:], cm_stdT[:]))
        E("dv", lambda e: e.memset(cm_q0T[0:64, 0:64], 0.0))
        E("dv", lambda e: e.tensor_scalar(ge_col[:], p_col[:], 64.0, None,
                                          ALU.is_ge))
        E("dv", lambda e: e.scalar_tensor_tensor(tmod_col[:], ge_col[:], -64.0,
                                                 p_col[:], ALU.mult, ALU.add))
        for tc in range(16):
            E("dv", lambda e, tc=tc: e.tensor_scalar(
                pack_v[:, tc, 0:1], ge_col[:], 1.0, 2.0 * tc, ALU.mult, ALU.add))
            E("dv", lambda e, tc=tc: e.tensor_copy(pack_v[:, tc, 1:2], tmod_col[:]))
        E("dv", lambda e: e.tensor_copy(qs_bf[0:1, :], qs_bc[0:1, :]))
        E("dv", lambda e: e.tensor_copy(ks_bf[0:1, :], ks_bc[0:1, :]))
        E("dv", lambda e: e.tensor_copy(linv_bf[0:1, :], xnf[0:1, 0:512]))
        E("dv", lambda e: e.tensor_copy(sq_bf[0:1, :], xnf[0:1, 512:1024]))
        E("dv", lambda e: e.memset(bch[:, 0:128], 0.0))
        E("dv", lambda e: e.memset(bch[:, 128:256], BIGF), cDV)
        ev_bcrows = cDV.n
        W("dv", qGP, ev_gwf)
        E("dv", lambda e: e.tensor_sub(
            wrk[:, 0:64].rearrange("p (c e) -> p c e", c=DCC), gwf_st, gw_hi[:]))
        E("dv", lambda e: e.tensor_copy(
            gw_lo[:], wrk[:, 0:64].rearrange("p (c e) -> p c e", c=DCC)), cDV)
        ev_gwlo = cDV.n

        W("pe", cDV, ev_bcrows)
        E("pe", lambda e: e.matmul(psA[:, 0:64], lhsT=ones_r[0:1, :],
                                   rhs=qs_bf[0:1, :], start=True, stop=True))
        E("pe", lambda e: e.matmul(psA[:, 64:128], lhsT=ones_r[0:1, :],
                                   rhs=ks_bf[0:1, :], start=True, stop=True))
        E("pe", lambda e: e.matmul(psA[:, 512:1024], lhsT=ones_r[0:1, :],
                                   rhs=linv_bf[0:1, :], start=True, stop=True))
        E("pe", lambda e: e.matmul(psA[:, 1024:1536], lhsT=ones_r[0:1, :],
                                   rhs=sq_bf[0:1, :], start=True, stop=True), cPE)
        ev_bcmm = cPE.n
        W("dv", cPE, ev_bcmm)
        E("dv", lambda e: e.tensor_copy(qs_bc[:], psA[:, 0:64]))
        E("dv", lambda e: e.tensor_copy(ks_bc[:], psA[:, 64:128]))
        E("dv", lambda e: e.tensor_copy(gamma_bc[:, 0:512], psA[:, 512:1024]))
        E("dv", lambda e: e.tensor_copy(gamma_bc[:, 512:1024], psA[:, 1024:1536]))
        E("dv", lambda e: e.tensor_reduce(scr1[:], qs_bc[:], AX.X, ALU.max,
                                          apply_absolute_value=True))
        E("dv", lambda e: e.tensor_reduce(scr2[:], ks_bc[:], AX.X, ALU.max,
                                          apply_absolute_value=True))
        E("dv", lambda e: e.tensor_mul(negC[:], scr1[:], scr2[:]))
        E("dv", lambda e: e.tensor_scalar(negC[:], negC[:], -SCALE, HEADROOM,
                                          ALU.mult, ALU.add))
        E("dv", lambda e: e.tensor_scalar_min(negC[:], negC[:], 0.0), cDV)
        ev_bcast = cDV.n

        E("dv", lambda e: e.tensor_mul(nkv2[0:2, 0:64], nkv_sb[0:2, 0:64],
                                       nkv_sb[0:2, 0:64]))
        E("dv", lambda e: e.reduce_sum(nkv2[0:2, 64:65], nkv2[0:2, 0:64], AX.X),
          cDV)
        ev_nk1 = cDV.n
        W("ac", cDV, ev_nk1)
        E("ac", lambda e: e.activation(nkv2[0:2, 65:66], nkv2[0:2, 64:65], AF.Ln))
        E("ac", lambda e: e.activation(nkv2[0:2, 66:67], nkv2[0:2, 65:66], AF.Exp,
                                       scale=-0.5), cAC)
        ev_nk2 = cAC.n
        W("dv", cAC, ev_nk2)
        E("dv", lambda e: e.tensor_scalar_min(nkv2[0:2, 66:67], nkv2[0:2, 66:67],
                                              1e12))
        E("dv", lambda e: e.tensor_scalar(nkv2[0:2, 0:64], nkv_sb[0:2, 0:64],
                                          nkv2[0:2, 66:67], None, ALU.mult))
        E("dv", lambda e: e.tensor_mul(nkb[0:2, 0:64], nkv2[0:2, 0:64],
                                       ks_bc[0:2, 0:64]))
        E("dv", lambda e: e.tensor_copy(nkb[0:2, 64:128], nkv_sb[0:2, 64:128]),
          cDV)
        ev_nkb = cDV.n

        ev_pb = [0, 0]
        ev_phi = [0, 0]
        ev_attn_pe = [0, 0]
        ev_wout_add = [0, 0]

        def p1(b, ev_prev_pe):
            st_hist = []
            for tt in range(TTC):
                r = tt % 2
                if len(st_hist) >= 2:
                    W("sy", qSY, st_hist[-2])
                    W("dv", qSY, st_hist[-2])
                E("sy", lambda e, b=b, tt=tt, r=r: e.dma_start(
                    out=xin_v[:, r, :], in_=x_d[b, tt * 128:(tt + 1) * 128, :]),
                  qSY, 16)
                evx = qSY.n
                W("dv", qSY, evx)
                E("dv", lambda e, tt=tt, r=r: e.reduce_sum(
                    stats[:, tt, 0:1], xin_v[:, r, :], AX.X, negate=True))
                E("dv", lambda e, tt=tt: e.tensor_scalar_mul(
                    stats[:, tt, 0:1], stats[:, tt, 0:1], 1.0 / D))
                E("dv", lambda e, tt=tt, r=r: e.tensor_scalar_add(
                    wrk[:], xin_v[:, r, :], stats[:, tt, 0:1]), cDV)
                ev_cen = cDV.n
                E("dv", lambda e, r=r: e.tensor_mul(xin_v[:, r, :], wrk[:],
                                                    wrk[:]))
                E("dv", lambda e, tt=tt, r=r: e.reduce_sum(
                    stats[:, tt, 1:2], xin_v[:, r, :], AX.X), cDV)
                ev_ss = cDV.n
                W("ac", cDV, ev_ss)
                E("ac", lambda e, tt=tt: e.activation(
                    stats[:, tt, 3:4], stats[:, tt, 1:2], AF.Ln, scale=1.0 / D,
                    bias=epscol[:]))
                E("ac", lambda e, tt=tt: e.activation(
                    stats[:, tt, 3:4], stats[:, tt, 3:4], AF.Exp, scale=-0.5), cAC)
                ev_r1 = cAC.n
                W("dv", cAC, ev_r1)
                E("dv", lambda e, tt=tt: e.tensor_scalar(
                    xnf[:], wrk[:], stats[:, tt, 3:4], None, ALU.mult))
                E("dv", lambda e, r=r: e.tensor_copy(ringxn[:, r, :], xnf[:]))
                E("dv", lambda e, r=r: e.tensor_sub(
                    ringxl[:, r, :], xnf[:], ringxn[:, r, :]))
                E("dv", lambda e, tt=tt: e.reduce_sum(
                    stats[:, tt, 4:5], xnf[:], AX.X, negate=True))
                E("dv", lambda e, tt=tt: e.tensor_scalar_mul(
                    stats[:, tt, 4:5], stats[:, tt, 4:5], 1.0 / D))
                E("dv", lambda e, tt=tt: e.tensor_scalar_add(
                    wrk[:], xnf[:], stats[:, tt, 4:5]), cDV)
                ev_cen2 = cDV.n
                E("dv", lambda e, r=r: e.tensor_mul(xin_v[:, r, :], wrk[:],
                                                    wrk[:]))
                E("dv", lambda e, tt=tt, r=r: e.reduce_sum(
                    stats[:, tt, 5:6], xin_v[:, r, :], AX.X), cDV)
                ev_ss2 = cDV.n
                W("ac", cDV, ev_ss2)
                E("ac", lambda e, tt=tt: e.activation(
                    stats[:, tt, 7:8], stats[:, tt, 5:6], AF.Ln, scale=1.0 / D,
                    bias=epscol[:]))
                E("ac", lambda e, tt=tt: e.activation(
                    stats[:, tt, 7:8], stats[:, tt, 7:8], AF.Exp, scale=-0.5), cAC)
                ev_r2 = cAC.n
                W("dv", cAC, ev_r2)
                E("dv", lambda e, tt=tt, r=r: e.scalar_tensor_tensor(
                    ringx2[:, r, :], wrk[:], stats[:, tt, 7:8], gamma_bc[:],
                    ALU.mult, ALU.mult), cDV)
                ev_tile = cDV.n
                W("sy", cDV, ev_tile)
                E("sy", lambda e, b=b, tt=tt, r=r: e.dma_start(
                    out=xn_s[b, tt * 128:(tt + 1) * 128, :], in_=ringxn[:, r, :]),
                  qSY, 16)
                E("sy", lambda e, tt=tt, r=r: e.dma_start(
                    out=xl_s[tt * 128:(tt + 1) * 128, :], in_=ringxl[:, r, :]),
                  qSY, 16)
                E("sy", lambda e, tt=tt, r=r: e.dma_start(
                    out=x2_s[tt * 128:(tt + 1) * 128, :], in_=ringx2[:, r, :]),
                  qSY, 16)
                st_hist.append(qSY.n)
            ev_stores = qSY.n

            W("sy", cPE, ev_prev_pe)
            W("sy", qSY, ev_stores)
            for dc in range(DCC):
                E("sy", lambda e, b=b, dc=dc: e.dma_start(
                    out=xnT[:, dc, :], in_=xn_s[b][:, dc * 128:(dc + 1) * 128],
                    transpose=True), qSY, 16)
                E("sy", lambda e, dc=dc: e.dma_start(
                    out=xn2T[:, dc, :], in_=x2_s[:, dc * 128:(dc + 1) * 128],
                    transpose=True), qSY, 16)
            ev_T = qSY.n

            if b == 1:
                W("gp", cPE, ev_prev_pe)
                E("gp", lambda e: e.dma_start(
                    out=Wq_b[:], in_=wq_d.rearrange("(c p) e -> p c e", p=128)),
                  qGP, 16)
                ev_wq[1] = qGP.n

            W("pe", cDV, cDV.n)
            W("pe", qSY, ev_T)
            W("pe", qGP, ev_wq[b])
            if b == 1:
                W("pe", cPE, ev_prev_pe)
            ev_qevac_hist = [0, 0]
            ev_qn_last = 0
            for tt in range(TTC):
                r = tt % 2
                if tt >= 2:
                    W("pe", cDV, ev_qevac_hist[0])
                for half in range(2):
                    for dc in range(DCC):
                        E("pe", lambda e, tt=tt, half=half, dc=dc: e.matmul(
                            psA[:, (tt % 2) * 1024 + half * 512:
                                (tt % 2) * 1024 + half * 512 + 512],
                            lhsT=xn2T[:, dc, tt * 128:(tt + 1) * 128],
                            rhs=Wq_b[:, dc, half * 512:(half + 1) * 512],
                            start=(dc == 0), stop=(dc == DCC - 1)),
                          cPE if (dc == DCC - 1 and half == 1) else None)
                ev_qpe = cPE.n
                W("dv", cPE, ev_qpe)
                if tt >= 2:
                    W("dv", qSY, ev_qn_last - 16)
                E("dv", lambda e, tt=tt, r=r: e.tensor_copy(
                    qf_v[:, r, :], psA[:, (tt % 2) * 1024:(tt % 2) * 1024 + 1024]),
                  cDV)
                ev_qevac = cDV.n
                ev_qevac_hist = [ev_qevac_hist[1], ev_qevac]
                E("dv", lambda e, r=r: e.tensor_mul(wrk[:], qf_v[:, r, :],
                                                    qf_v[:, r, :]))
                E("dv", lambda e: e.reduce_sum(
                    gco[:, 0:16], wrk[:].rearrange("p (h d) -> p h d", h=16),
                    AX.X), cDV)
                ev_ssq = cDV.n
                W("ac", cDV, ev_ssq)
                E("ac", lambda e: e.activation(gco[:, 0:16], gco[:, 0:16], AF.Ln))
                E("ac", lambda e: e.activation(gco[:, 0:16], gco[:, 0:16], AF.Exp,
                                               scale=-0.5), cAC)
                ev_rn = cAC.n
                W("dv", cAC, ev_rn)
                E("dv", lambda e: e.tensor_scalar_min(gco[:, 0:16], gco[:, 0:16],
                                                      1e12))
                for h in range(H):
                    E("dv", lambda e, r=r, h=h: e.scalar_tensor_tensor(
                        ringqn[:, r, h * 64:(h + 1) * 64],
                        qf_v[:, r, h * 64:(h + 1) * 64], gco[:, h:h + 1],
                        qs_bc[:], ALU.mult, ALU.mult),
                      cDV if h == H - 1 else None)
                ev_qn = cDV.n
                W("sy", cDV, ev_qn)
                E("sy", lambda e, tt=tt, r=r: e.dma_start(
                    out=qn_s[tt * 128:(tt + 1) * 128, :], in_=ringqn[:, r, :]),
                  qSY, 16)
                ev_qn_last = qSY.n
            ev_qproj_pe = cPE.n

            W("dv", cPE, ev_qproj_pe)
            W("dv", cDV, ev_nkb)
            E("dv", lambda e: e.memset(Vaug[:, :, :], 0.0))
            E("dv", lambda e: e.memset(Vaug[:, :, 64:65], 1.0))
            E("dv", lambda e: e.memset(kT[:, 0:128], 0.0), cDV)
            ev_vprep = cDV.n
            W("pe", cDV, ev_vprep)
            E("pe", lambda e: e.matmul(psB[0:64, 0:2], lhsT=nkb[0:2, 0:64],
                                       rhs=I_bf[0:2, 0:2], start=True, stop=True),
              cPE)
            ev_nkt = cPE.n
            W("dv", cPE, ev_nkt)
            E("dv", lambda e: e.tensor_copy(kT[0:64, 126:128], psB[0:64, 0:2]))
            E("dv", lambda e: e.tensor_copy(kT[64:128, 126:128], psB[0:64, 0:2]),
              cDV)
            ev_nkT = cDV.n
            W("sy", cDV, ev_vprep)
            W("sy", qSY, qSY.n)
            E("sy", lambda e: e.dma_start(out=Vaug[126:128, 0, 0:64],
                                          in_=nkb[0:2, 64:128]), qSY, 16)
            ev_nullv = qSY.n
            W("sy", qSY, qSY.n)

            W("pe", qGP, ev_wkv)
            W("pe", cDV, cDV.n)
            kv_hist = []
            for tt in range(TTC):
                sl = tt % 2
                if len(kv_hist) >= 2:
                    W("pe", cDV, kv_hist[-2])
                for dc in range(DCC):
                    E("pe", lambda e, tt=tt, dc=dc, sl=sl: e.matmul(
                        psA[:, 1024 + sl * 512:1024 + sl * 512 + 128],
                        lhsT=xnT[:, dc, tt * 128:(tt + 1) * 128],
                        rhs=Wkv_b[:, dc, :], start=(dc == 0),
                        stop=(dc == DCC - 1)), cPE if dc == DCC - 1 else None)
                ev_kvpe = cPE.n
                W("dv", cPE, ev_kvpe)
                E("dv", lambda e, tt=tt, sl=sl: e.tensor_copy(
                    Vaug[:, tt + 1, 0:64],
                    psA[:, 1024 + sl * 512 + 64:1024 + sl * 512 + 128]))
                E("dv", lambda e, sl=sl: e.tensor_copy(
                    wrk[:, 0:64], psA[:, 1024 + sl * 512:1024 + sl * 512 + 64]))
                E("dv", lambda e: e.tensor_mul(wrk[:, 64:128], wrk[:, 0:64],
                                               wrk[:, 0:64]))
                E("dv", lambda e: e.reduce_sum(scr1[:], wrk[:, 64:128], AX.X),
                  cDV)
                ev_kss = cDV.n
                W("ac", cDV, ev_kss)
                E("ac", lambda e: e.activation(scr2[:], scr1[:], AF.Ln))
                E("ac", lambda e: e.activation(scr2[:], scr2[:], AF.Exp,
                                               scale=-0.5), cAC)
                ev_krn = cAC.n
                W("dv", cAC, ev_krn)
                E("dv", lambda e: e.tensor_scalar_min(scr2[:], scr2[:], 1e12))
                E("dv", lambda e: e.scalar_tensor_tensor(
                    kb_t[:], wrk[:, 0:64], scr2[:], ks_bc[:], ALU.mult, ALU.mult),
                  cDV)
                ev_kbt = cDV.n
                kv_hist.append(ev_kbt)
                W("pe", cDV, ev_kbt)
                E("pe", lambda e, tt=tt: e.matmul(
                    psB[0:64, (tt % 2) * 512:(tt % 2) * 512 + 128],
                    lhsT=kb_t[:, 0:64], rhs=I_bf[:, 0:128], start=True,
                    stop=True), cPE)
                ev_ktp = cPE.n
                W("dv", cPE, ev_ktp)
                E("dv", lambda e, tt=tt: e.tensor_copy(
                    kT[0:64, 128 + tt * 128:128 + (tt + 1) * 128],
                    psB[0:64, (tt % 2) * 512:(tt % 2) * 512 + 128]))
                E("dv", lambda e, tt=tt: e.tensor_copy(
                    kT[64:128, 128 + tt * 128:128 + (tt + 1) * 128],
                    psB[0:64, (tt % 2) * 512:(tt % 2) * 512 + 128]),
                  cDV)
            ev_kdone_dv = cDV.n
            ev_kv_pe = cPE.n

            W("pe", cDV, ev_gwlo)
            W("pe", cDV, ev_kdone_dv)
            ev_lo_pc = [0] * DCC
            ev_gate_dc = [0] * DCC
            for dc in range(2):
                E("sy", lambda e, dc=dc: e.dma_start(
                    out=xnTlo_w[:, dc % 2, :],
                    in_=xl_s[:, dc * 128:(dc + 1) * 128], transpose=True),
                  qSY, 16)
                ev_lo_pc[dc] = qSY.n
            for dc in range(DCC):
                W("pe", qSY, ev_lo_pc[dc])
                for t4 in range(4):
                    E("pe", lambda e, dc=dc, t4=t4: e.matmul(
                        psB[0:8, t4 * 512:(t4 + 1) * 512], lhsT=gw_hi[:, dc, :],
                        rhs=xnT[:, dc, t4 * 512:(t4 + 1) * 512],
                        start=(dc == 0), stop=False))
                    E("pe", lambda e, dc=dc, t4=t4: e.matmul(
                        psB[0:8, t4 * 512:(t4 + 1) * 512], lhsT=gw_lo[:, dc, :],
                        rhs=xnT[:, dc, t4 * 512:(t4 + 1) * 512],
                        start=False, stop=False))
                    E("pe", lambda e, dc=dc, t4=t4: e.matmul(
                        psB[0:8, t4 * 512:(t4 + 1) * 512], lhsT=gw_hi[:, dc, :],
                        rhs=xnTlo_w[:, dc % 2, t4 * 512:(t4 + 1) * 512],
                        start=False, stop=(dc == DCC - 1)),
                      cPE)
                ev_gate_dc[dc] = cPE.n
                if dc + 2 < DCC:
                    W("sy", cPE, ev_gate_dc[dc])
                    E("sy", lambda e, dc=dc: e.dma_start(
                        out=xnTlo_w[:, dc % 2, :],
                        in_=xl_s[:, (dc + 2) * 128:(dc + 3) * 128],
                        transpose=True), qSY, 16)
                    ev_lo_pc[dc + 2] = qSY.n
            ev_gate_pe = cPE.n
            W("dv", cPE, ev_gate_pe)
            E("dv", lambda e, b=b: e.tensor_copy(pb_f[:, b * NT:(b + 1) * NT],
                                                 psB[0:8, :]), cDV)
            ev_scores = cDV.n

            W("pe", cDV, ev_scores)
            for tt in range(TTC):
                E("pe", lambda e, b=b, tt=tt: e.matmul(
                    psA[:, tt * 8:(tt + 1) * 8],
                    lhsT=pb_f[0:8, b * NT + tt * 128:b * NT + (tt + 1) * 128],
                    rhs=I_f[0:8, 0:8], start=True, stop=True),
                  cPE if tt == TTC - 1 else None)
            ev_sct = cPE.n
            W("dv", cPE, ev_sct)
            E("dv", lambda e: e.tensor_copy(
                probs_tok[:, :, :].rearrange("p c e -> p (c e)"), psA[:, 0:128]))
            E("dv", lambda e: e.tensor_reduce(
                gco[:, 0:16], probs_tok[:, :, :], AX.X, ALU.max, negate=True),
          cDV)
            ev_negmax = cDV.n
            W("ac", cDV, ev_negmax)
            for tt in range(TTC):
                E("ac", lambda e, tt=tt: e.activation(
                    probs_tok[:, tt, :], probs_tok[:, tt, :], AF.Exp,
                    bias=gco[:, tt:tt + 1]), cAC)
                ev_e = cAC.n
                W("dv", cAC, ev_e)
                E("dv", lambda e, tt=tt: e.reduce_sum(
                    stats[:, tt, 8:9], probs_tok[:, tt, :], AX.X))
                E("dv", lambda e, tt=tt: e.reciprocal(scr1[:], stats[:, tt, 8:9]))
                E("dv", lambda e, tt=tt: e.tensor_scalar(
                    probs_tok[:, tt, :], probs_tok[:, tt, :], scr1[:], None,
                    ALU.mult), cDV)
            ev_probs = cDV.n
            E("dv", lambda e, b=b: e.tensor_copy(phi_v[b][:, :, :],
                                                 probs_tok[:, :, :]))
            E("dv", lambda e, b=b: e.tensor_sub(plo_stage[:, :, :],
                                                probs_tok[:, :, :],
                                                phi_v[b][:, :, :]))
            E("dv", lambda e, b=b: e.tensor_copy(plo_v[b][:, :, :],
                                                 plo_stage[:, :, :]), cDV)
            ev_phi[b] = cDV.n
            W("pe", cDV, ev_probs)
            for tt in range(TTC):
                E("pe", lambda e, tt=tt: e.matmul(
                    psB[0:8, tt * 128:(tt + 1) * 128], lhsT=probs_tok[:, tt, :],
                    rhs=I_f[:, 0:128], start=True, stop=True),
                  cPE if tt == TTC - 1 else None)
            ev_pbt = cPE.n
            W("dv", cPE, ev_pbt)
            E("dv", lambda e, b=b: e.tensor_copy(pb_f[:, b * NT:(b + 1) * NT],
                                                 psB[0:8, :]), cDV)
            ev_pb[b] = cDV.n

            W("sy", qSY, ev_qn_last)
            W("sy", cPE, max(ev_kv_pe, ev_gate_pe))
            for dc in range(DCC):
                E("sy", lambda e, dc=dc: e.dma_start(
                    out=QT[:, dc, :], in_=qn_s[:, dc * 128:(dc + 1) * 128],
                    transpose=True), qSY, 16)
            ev_qt = qSY.n
            if ATTN_DBG == 3 and b == 0:
                W("gp", qSY, ev_qt)
                E("gp", lambda e: e.dma_start(
                    out=dbg_d[:, 0:1024], in_=qn_s[0:128, 0:1024]), qGP, 16)
                E("gp", lambda e: e.dma_start(
                    out=dbg_d[:, 1024:2048],
                    in_=xn_s[0][0:128, 0:1024]), qGP, 16)
            return ev_qt, max(ev_kdone_dv, ev_nkT), ev_nullv

        def attn_core(b, ev_qt, ev_kv_dv, ev_nullv):
            W("pe", cDV, cDV.n)
            W("pe", cAC, cAC.n)
            W("pe", qSY, ev_qt)
            W("pe", qSY, ev_nullv)
            W("pe", cDV, ev_kv_dv)
            W("ac", cDV, ev_bcast)
            slot_exp = [0, 0, 0, 0]
            eslot_pe = [0, 0]
            oslot_dv = [0, 0]
            bslot_dv = [0, 0]
            for qg in range(ATTN_QG):
                jts = 4 * qg + 5
                npairs = (jts + 1) // 2
                for h in range(ATTN_H):
                    hq = QT[(h % 2) * 64:(h % 2) * 64 + 64, h // 2,
                            qg * 512:(qg + 1) * 512]
                    so = h % 2
                    sb = 2 + (h % 2)

                    def emit_spair(pr):
                        plan_ops = []
                        for jt in (2 * pr, 2 * pr + 1):
                            if jt >= jts:
                                continue
                            sl = jt % 4
                            kb0 = (h % 2) * 64
                            dt_ = jt - 1 - 4 * qg
                            has_mask = (jt == 0) or (0 <= dt_ <= 3)
                            plan_ops.append(("w", sl))
                            plan_ops.append(lambda e, jt=jt, sl=sl, hq=hq,
                                            kb0=kb0, hm=has_mask: e.matmul(
                                psA[:, sl * 512:(sl + 1) * 512],
                                lhsT=kT[kb0:kb0 + 64,
                                        jt * 128:(jt + 1) * 128], rhs=hq,
                                start=True, stop=not hm))
                            if jt == 0:
                                plan_ops.append(lambda e, sl=sl: e.matmul(
                                    psA[:, sl * 512:(sl + 1) * 512],
                                    lhsT=I_bf[:], rhs=cm_padT[:], start=False,
                                    stop=True))
                            if 0 <= dt_ <= 3:
                                cmt = cm_q0T if (qg == 0 and dt_ == 0) else cm_stdT
                                if dt_ > 0:
                                    plan_ops.append(
                                        lambda e, sl=sl, dt_=dt_: e.matmul(
                                            psA[:, sl * 512:
                                                sl * 512 + dt_ * 128],
                                            lhsT=I_bf[:],
                                            rhs=cm_full[:, 0:dt_ * 128],
                                            start=False, stop=False))
                                plan_ops.append(
                                    lambda e, sl=sl, dt_=dt_, cmt=cmt: e.matmul(
                                        psA[:, sl * 512 + dt_ * 128:
                                            sl * 512 + dt_ * 128 + 128],
                                        lhsT=I_bf[:], rhs=cmt[:], start=False,
                                        stop=True))
                        ret = 0
                        for i, op in enumerate(plan_ops):
                            if isinstance(op, tuple):
                                W("pe", cAC, slot_exp[op[1]])
                                continue
                            last = all(isinstance(o, tuple)
                                       for o in plan_ops[i + 1:])
                            ret = E("pe", op, cPE if last else None)
                        return ret

                    def emit_exp(pr, ev_sp):
                        par = pr % 2
                        jt0 = 2 * pr
                        w = 1024 if jt0 + 1 < jts else 512
                        sl0 = jt0 % 4
                        W("ac", cPE, ev_sp)
                        W("ac", cPE, eslot_pe[par])
                        E("ac", lambda e, sl0=sl0, par=par: e.activation(
                            (EvDbg.ap().rearrange("p (s c) -> p s c", s=2)
                             if USE_EVDBG[0] else Ev)[:, par, 0:512],
                            psA[:, sl0 * 512:sl0 * 512 + 512],
                            EXP_FUNC[0],
                            bias=(zbias.ap()[:, :] if NEGC_ZERO else negC[:]),
                            scale=SCALE),
                          cAC if w == 512 else None)
                        if w == 1024:
                            E("ac", lambda e, sl0=sl0, par=par: e.activation(
                                Ev[:, par, 512:1024],
                                psA[:, sl0 * 512 + 512:sl0 * 512 + 1024],
                                EXP_FUNC[0], bias=negC[:], scale=SCALE), cAC)
                        slot_exp[sl0] = cAC.n
                        if w == 1024:
                            slot_exp[(sl0 + 1) % 4] = cAC.n
                        return cAC.n

                    def emit_pv(pr, ev_ex):
                        par = pr % 2
                        W("pe", cAC, ev_ex)
                        if pr == 0:
                            W("pe", cDV, oslot_dv[so])
                        last_inc = 0
                        for jt in (2 * pr, 2 * pr + 1):
                            if jt >= jts:
                                continue
                            E("pe", lambda e, jt=jt, par=par, so=so, jts=jts:
                              e.matmul(
                                psB[0:65, so * 512:so * 512 + 512],
                                lhsT=Vaug[:, jt, :],
                                rhs=Ev[:, par, (jt % 2) * 512:(jt % 2) * 512 + 512],
                                start=(jt == 0), stop=(jt == jts - 1)),
                              cPE)
                            last_inc = cPE.n
                        eslot_pe[par] = last_inc
                        return last_inc

                    ev_sp = [0] * npairs
                    ev_ex = [0] * npairs
                    if ATTN_SERIAL:
                        for pr in range(npairs):
                            ev_sp[pr] = emit_spair(pr)
                        for pr in range(npairs):
                            if not ATTN_EXP:
                                continue
                            ev_ex[pr] = emit_exp(pr, ev_sp[npairs - 1])
                            if ATTN_PV:
                                ev_pv = emit_pv(pr, ev_ex[pr])
                    else:
                        ev_sp[0] = emit_spair(0)
                        for pr in range(npairs):
                            if pr + 1 < npairs:
                                ev_sp[pr + 1] = emit_spair(pr + 1)
                            if not ATTN_EXP:
                                slot_exp[(2 * pr) % 4] = 0
                                continue
                            if ATTN_EXP == 2:
                                E("ac", lambda e, pr=pr: e.activation(
                                    Ev[:, pr % 2, 0:512],
                                    psA[:, ((2 * pr) % 4) * 512:
                                        ((2 * pr) % 4) * 512 + 512],
                                    AF.Identity), cAC)
                                slot_exp[(2 * pr) % 4] = 0
                                continue
                            ev_ex[pr] = emit_exp(pr, ev_sp[pr])
                            if ATTN_PV:
                                ev_pv = emit_pv(pr, ev_ex[pr])
                    if not (ATTN_NORM and ATTN_EXP and ATTN_PV):
                        continue
                    W("dv", cPE, ev_pv)
                    E("dv", lambda e, so=so: e.reciprocal(
                        linv[0:1, :], psB[64:65, so * 512:so * 512 + 512]))
                    E("dv", lambda e: e.tensor_copy(linv_bf[0:1, :],
                                                    linv[0:1, :]), cDV)
                    ev_linv = cDV.n
                    if ATTN_DBG == 2 and b == 0 and h == 0 and qg == 0:
                        W("gp", cDV, cDV.n)
                        W("gp", qSY, ev_qt)
                        E("gp", lambda e: e.dma_start(
                            out=dbg_d[:, 0:1024], in_=QT[:, 0, 0:1024]), qGP, 16)
                        E("gp", lambda e: e.dma_start(
                            out=dbg_d[:, 1024:2048], in_=kT[:, 0:1024]), qGP, 16)
                    if ATTN_DBG and b == 0 and h == 0:
                        E("dv", lambda e, so=so, qg=qg: e.tensor_copy(
                            AFp[0:1, FXIN + 1024 + qg * 512:
                                FXIN + 1536 + qg * 512],
                            psB[64:65, so * 512:so * 512 + 512]), cDV)
                        evd = cDV.n
                        W("sy", cDV, evd)
                        E("sy", lambda e, qg=qg: e.dma_start(
                            out=dbg_d[0:1, qg * 512:(qg + 1) * 512],
                            in_=AFp[0:1, FXIN + 1024 + qg * 512:
                                    FXIN + 1536 + qg * 512]), qSY, 16)
                    W("pe", cDV, bslot_dv[sb - 2])
                    W("pe", cDV, ev_linv)
                    E("pe", lambda e, sb=sb: e.matmul(
                        psB[0:64, sb * 512:sb * 512 + 512],
                        lhsT=ones_r[0:1, 0:64], rhs=linv_bf[0:1, :],
                        start=True, stop=True), cPE)
                    ev_bc = cPE.n
                    W("dv", cPE, ev_bc)
                    E("dv", lambda e, so=so: e.tensor_copy(
                        osb[0:64, :], psB[0:64, so * 512:so * 512 + 512]), cDV)
                    oslot_dv[so] = cDV.n
                    E("dv", lambda e, h=h, qg=qg, sb=sb: e.tensor_tensor(
                        OT[(h % 2) * 64:(h % 2) * 64 + 64, h // 2,
                           qg * 512:(qg + 1) * 512],
                        osb[0:64, :], psB[0:64, sb * 512:sb * 512 + 512],
                        ALU.mult), cDV)
                    bslot_dv[sb - 2] = cDV.n
                    for _ in range(16):
                        if FEED:
                            FEED.pop(0)()
            ev_ot_dv = cDV.n
            ev_core_pe = cPE.n

            ev_att_ring = [0, 0]
            wo_slot_ev = [0, 0, 0, 0]
            for dh in range(2 if ATTN_WOUT else 0):
                W("gp", cPE, ev_core_pe)
                E("gp", lambda e, dh=dh: e.dma_start(
                    out=Wo_h[:], in_=wo_d.rearrange("(c p) e -> p c e", p=128)[
                        :, :, dh * 512:(dh + 1) * 512]), qGP, 16)
                ev_wo = qGP.n
                W("pe", qGP, ev_wo)
                W("pe", cDV, ev_ot_dv)
                for tt in range(TTC):
                    sl = tt % 4
                    if wo_slot_ev[sl]:
                        W("pe", cDV, wo_slot_ev[sl])
                    for ec in range(DCC):
                        E("pe", lambda e, tt=tt, ec=ec, sl=sl: e.matmul(
                            psA[:, sl * 512:(sl + 1) * 512],
                            lhsT=OT[:, ec, tt * 128:(tt + 1) * 128],
                            rhs=Wo_h[:, ec, :], start=(ec == 0),
                            stop=(ec == DCC - 1)),
                          cPE if ec == DCC - 1 else None)
                    ev_wpe = cPE.n
                    r = tt % 2
                    W("dv", cPE, ev_wpe)
                    if ev_att_ring[r]:
                        W("dv", qGP, ev_att_ring[r])
                    E("dv", lambda e, dh=dh, sl=sl, r=r: e.tensor_copy(
                        att_v[:, r, dh * 512:(dh + 1) * 512],
                        psA[:, sl * 512:(sl + 1) * 512]), cDV)
                    ev_ev = cDV.n
                    wo_slot_ev[sl] = ev_ev
                    W("gp", cDV, ev_ev)
                    W("gp", qSY, ev_init[b])
                    E("gp", lambda e, b=b, tt=tt, dh=dh, r=r: e.dma_start(
                        out=out_d[b, tt * 128:(tt + 1) * 128,
                                  dh * 512:(dh + 1) * 512],
                        in_=att_v[:, r, dh * 512:(dh + 1) * 512],
                        accum_op=ALU.add), qGP, 16)
                    ev_att_ring[r] = qGP.n
            ev_attn_pe[b] = cPE.n
            ev_wout_add[b] = qGP.n

        FEED = []
        def bisect():
            th = []
            def E(eng, fn, sem=None, k=1):
                th.append(lambda: EMIT(eng, fn, sem, k))
                return None
            def W(eng, sem, val):
                th.append(lambda: WAIT(eng, sem, val))
            W("dv", cDV, max(ev_pb[0], ev_pb[1]))
            E("dv", lambda e: e.memset(lohi[:, 0:1], 0.0))
            E("dv", lambda e: e.memset(lohi[:, 2:3], 0.0))
            E("dv", lambda e: e.memset(lohi[:, 1:2], 1.0))
            E("dv", lambda e: e.memset(lohi[:, 3:4], 1.0), cDV)
            for it in range(NITER):
                for b in range(NB):
                    lo, hi, mid = 2 * b, 2 * b + 1, 4 + b
                    E("dv", lambda e, lo=lo, hi=hi, mid=mid: e.tensor_add(
                        lohi[:, mid:mid + 1], lohi[:, hi:hi + 1],
                        lohi[:, lo:lo + 1]))
                    E("dv", lambda e, mid=mid: e.tensor_scalar_mul(
                        lohi[:, mid:mid + 1], lohi[:, mid:mid + 1], 0.5))
                    for ch in range(2):
                        E("dv", lambda e, b=b, ch=ch, mid=mid: e.tensor_scalar(
                            bscr.ap()[:, :],
                            pb_f[:, b * NT + ch * 1024:b * NT + (ch + 1) * 1024],
                            lohi[:, mid:mid + 1], None, ALU.is_ge,
                            op1=ALU.add,
                            accum_out=lohi[:, 12 + ch:13 + ch]))
                    E("dv", lambda e: e.tensor_add(lohi[:, 14:15], lohi[:, 12:13],
                                                   lohi[:, 13:14]))
                    E("dv", lambda e: e.tensor_scalar(
                        lohi[:, 15:16], lohi[:, 14:15], float(KS), None,
                        ALU.is_ge))
                    E("dv", lambda e, lo=lo, mid=mid: e.tensor_sub(
                        lohi[:, 10:11], lohi[:, mid:mid + 1], lohi[:, lo:lo + 1]))
                    E("dv", lambda e, lo=lo: e.scalar_tensor_tensor(
                        lohi[:, lo:lo + 1], lohi[:, 10:11], lohi[:, 15:16],
                        lohi[:, lo:lo + 1], ALU.mult, ALU.add))
                    E("dv", lambda e, hi=hi, mid=mid: e.tensor_sub(
                        lohi[:, 10:11], lohi[:, hi:hi + 1], lohi[:, mid:mid + 1]))
                    E("dv", lambda e, hi=hi, mid=mid: e.scalar_tensor_tensor(
                        lohi[:, hi:hi + 1], lohi[:, 10:11], lohi[:, 15:16],
                        lohi[:, mid:mid + 1], ALU.mult, ALU.add), cDV)
            for b in range(NB):
                hi = 2 * b + 1
                for ch in range(2):
                    E("dv", lambda e, b=b, ch=ch, hi=hi: e.tensor_scalar(
                        bscr.ap()[:, :],
                        pb_f[:, b * NT + ch * 1024:b * NT + (ch + 1) * 1024],
                        lohi[:, hi:hi + 1], None, ALU.is_lt))
                    E("dv", lambda e, b=b, ch=ch: e.tensor_mul(
                        bscr.ap()[:, :], bscr.ap()[:, :],
                        pb_f[:, b * NT + ch * 1024:b * NT + (ch + 1) * 1024]))
                    E("dv", lambda e, ch=ch: e.tensor_reduce(
                        theta[:, 4 + ch:5 + ch], bscr.ap()[:, :], AX.X, ALU.max))
                E("dv", lambda e, b=b: e.tensor_tensor(
                    theta[:, b:b + 1], theta[:, 4:5], theta[:, 5:6], ALU.max))
                for ch in range(2):
                    E("dv", lambda e, b=b, ch=ch: e.tensor_scalar(
                        bscr.ap()[:, :],
                        pb_f[:, b * NT + ch * 1024:b * NT + (ch + 1) * 1024],
                        theta[:, b:b + 1], None, ALU.is_gt,
                        op1=ALU.add,
                        accum_out=lohi[:, 12 + ch:13 + ch]))
                E("dv", lambda e: e.tensor_add(lohi[:, 14:15], lohi[:, 12:13],
                                               lohi[:, 13:14]))
                E("dv", lambda e, b=b: e.tensor_scalar(
                    theta[:, 2 + b:3 + b], lohi[:, 14:15], -1.0, float(KS),
                    ALU.mult, ALU.add), cDV)
            return th

        def build_dst(b, ev_bis):
            W("dv", cDV, ev_bis)
            E("dv", lambda e: e.memset(theta[:, 6:8], 0.0), cDV)
            ev_last = cDV.n
            for tc in range(TTC):
                E("dv", lambda e, b=b, tc=tc: e.tensor_scalar(
                    bch[:, 256:384],
                    pb_f[:, b * NT + tc * 128:b * NT + (tc + 1) * 128],
                    theta[:, b:b + 1], None, ALU.is_gt))
                E("dv", lambda e, b=b, tc=tc: e.tensor_scalar(
                    bch[:, 384:512],
                    pb_f[:, b * NT + tc * 128:b * NT + (tc + 1) * 128],
                    theta[:, b:b + 1], None, ALU.is_equal))
                E("dv", lambda e: e.tensor_tensor_scan(
                    bch[:, 512:640], bch[:, 384:512], bch[:, 0:128],
                    theta[:, 6:7], ALU.add, ALU.add))
                E("dv", lambda e: e.tensor_copy(theta[:, 6:7], bch[:, 639:640]))
                E("dv", lambda e, b=b: e.tensor_scalar(
                    bch[:, 512:640], bch[:, 512:640], theta[:, 2 + b:3 + b],
                    None, ALU.is_le))
                E("dv", lambda e: e.tensor_mul(bch[:, 384:512], bch[:, 384:512],
                                               bch[:, 512:640]))
                E("dv", lambda e: e.tensor_add(bch[:, 256:384], bch[:, 256:384],
                                               bch[:, 384:512]))
                E("dv", lambda e: e.tensor_tensor_scan(
                    bch[:, 512:640], bch[:, 256:384], bch[:, 0:128],
                    theta[:, 7:8], ALU.add, ALU.add))
                E("dv", lambda e: e.tensor_copy(theta[:, 7:8], bch[:, 639:640]))
                E("dv", lambda e: e.tensor_scalar_add(bch[:, 512:640],
                                                      bch[:, 512:640],
                                                      -1.0 - BIGF))
                E("dv", lambda e: e.tensor_mul(bch[:, 512:640], bch[:, 512:640],
                                               bch[:, 256:384]))
                E("dv", lambda e: e.tensor_scalar_add(bch[:, 640:768],
                                                      bch[:, 512:640], BIGF),
                  cDV)
                ev_dst = cDV.n
                W("pe", cDV, ev_dst)
                E("pe", lambda e, tc=tc: e.matmul(
                    psB[0:128, 1024 + (tc % 2) * 512:1024 + (tc % 2) * 512 + 8],
                    lhsT=bch[0:8, 640:768], rhs=I_f[0:8, 0:8], start=True,
                    stop=True), cPE)
                ev_t = cPE.n
                W("dv", cPE, ev_t)
                E("dv", lambda e, b=b, tc=tc: e.tensor_copy(
                    dstT[b][:, tc, :],
                    psB[:, 1024 + (tc % 2) * 512:1024 + (tc % 2) * 512 + 8]),
                  cDV)
                ev_last = cDV.n
            return ev_last

        ev_prep_xgT = {}
        ev_prep_mc = {}
        def prep_a(b, e, par, ev_dst_done):
            W("dv", cDV, ev_dst_done)
            W("dv", cDV, ev_phi[b])
            E("dv", lambda e_, b=b, e=e: e_.tensor_copy(
                pack_v[:, :, 2:3], phi_v[b][:, :, e:e + 1]))
            E("dv", lambda e_, b=b, e=e: e_.tensor_copy(
                pack_v[:, :, 3:4], plo_v[b][:, :, e:e + 1]), cDV)
            ev_pk = cDV.n
            W("pe", cDV, ev_pk)
            ev_round = [0] * 8
            for rnd in range(8):
                pp = rnd % 2
                if rnd >= 2:
                    W("dv", cPE, ev_round[rnd - 2])
                for i in range(2):
                    tc = rnd * 2 + i
                    E("dv", lambda e_, b=b, e=e, tc=tc, pp=pp, i=i:
                      e_.tensor_scalar(
                        Msel_v[:, pp * 2 + i, :], iota_s[:],
                        dstT[b][:, tc, e:e + 1], None, ALU.is_equal),
                      cDV if i == 1 else None)
                ev_ms = cDV.n
                W("pe", cDV, ev_ms)
                for i in range(2):
                    tc = rnd * 2 + i
                    E("pe", lambda e_, tc=tc, pp=pp, i=i: e_.matmul(
                        psB[0:4, 0:512], lhsT=pack_v[:, tc, :],
                        rhs=Msel_v[:, pp * 2 + i, :], start=(tc == 0),
                        stop=(tc == 15)), cPE if i == 1 else None)
                ev_round[rnd] = cPE.n
            ev_ext = cPE.n
            W("dv", cPE, ev_ext)
            E("dv", lambda e_: e_.tensor_copy(ext_sb[0:4, :], psB[0:4, 0:512]),
              cDV)
            ev_exts = cDV.n
            W("pe", cDV, ev_exts)
            for st in range(4):
                E("pe", lambda e_, st=st: e_.matmul(
                    psB[:, 512 + st * 4:512 + st * 4 + 4],
                    lhsT=ext_sb[0:4, st * 128:(st + 1) * 128], rhs=I_f[0:4, 0:4],
                    start=True, stop=True), cPE if st == 3 else None)
            ev_extT = cPE.n
            W("dv", cPE, ev_extT)
            E("dv", lambda e_: e_.tensor_copy(
                mg_v[:, :, :].rearrange("p s k -> p (s k)"), psB[:, 512:528]),
              cDV)
            for st in range(4):
                E("dv", lambda e_, st=st, b=b: e_.tensor_scalar(
                    scr1[:], mg_v[:, st, 0:1], 64.0, float(b * NT), ALU.mult,
                    ALU.add))
                E("dv", lambda e_, st=st: e_.tensor_add(scr1[:], scr1[:],
                                                        mg_v[:, st, 1:2]))
                E("dv", lambda e_, st=st, b=b, e=e: e_.tensor_copy(
                    mcol_i[(b * 8 + e) * 4 + st].ap()[:, :], scr1[:]))
                E("dv", lambda e_, st=st: e_.tensor_add(
                    scr2[:], mg_v[:, st, 2:3], mg_v[:, st, 3:4]))
                E("dv", lambda e_, st=st, par=par: e_.tensor_copy(
                    gco[:, 16 + par * 4 + st:17 + par * 4 + st], scr2[:]), cDV)
            ev_mcol = cDV.n
            ev_prep_mc[(b, e)] = ev_mcol
            W("gp", cDV, ev_mcol)
            gev = []
            for st in (0, 1):
                E("gp", lambda e_, st=st, b=b, e=e: e_.indirect_dma_start(
                    out=Xg_v[:, st % 2, :], out_offset=None,
                    in_=xn_s.rearrange("b n d -> (b n) d"),
                    in_offset=IndirectOffsetOnAxis(
                        mcol_i[(b * 8 + e) * 4 + st].ap()[:, :], 0)),
                  qGP, 16)
                gev.append(qGP.n)
            return gev, par

        def prep_b(b, e, gevpar):
            gev, par = gevpar
            xg_ev = [0, 0]
            for sh in range(2):
                if sh == 1:
                    W("gp", cPE, cPE.n)
                    for st in (2, 3):
                        E("gp", lambda e_, st=st, b=b, e=e: e_.indirect_dma_start(
                            out=Xg_v[:, st % 2, :], out_offset=None,
                            in_=xn_s.rearrange("b n d -> (b n) d"),
                            in_offset=IndirectOffsetOnAxis(
                                mcol_i[(b * 8 + e) * 4 + st].ap()[:, :], 0)),
                          qGP, 16)
                        gev.append(qGP.n)
                W("pe", qGP, gev[2 * sh + 1])
                for st in (2 * sh, 2 * sh + 1):
                    for dg in range(2):
                        par2 = dg % 2
                        if xg_ev[par2]:
                            W("pe", cDV, xg_ev[par2])
                        for dc4 in range(4):
                            dc = dg * 4 + dc4
                            E("pe", lambda e_, st=st, dc=dc, dc4=dc4, par2=par2:
                              e_.matmul(
                                psB[:, 512 + par2 * 512 + dc4 * 128:
                                    512 + par2 * 512 + dc4 * 128 + 128],
                                lhsT=Xg_v[:, st % 2, dc * 128:(dc + 1) * 128],
                                rhs=I_bf[:], start=True, stop=True),
                              cPE if dc4 == 3 else None)
                        evt = cPE.n
                        W("dv", cPE, evt)
                        E("dv", lambda e_, st=st, dg=dg, par2=par2:
                          e_.tensor_copy(
                            XgT_v[:, dg * 4:(dg + 1) * 4,
                                  st * 128:(st + 1) * 128],
                            psB[:, 512 + par2 * 512:1024 + par2 * 512]
                            .rearrange("p (c t) -> p c t", c=4)), cDV)
                        xg_ev[par2] = cDV.n
            ev_prep_xgT[(b, e)] = cDV.n

        ev_pc_box = [0]
        def ffn(b):
            W("pe", cDV, cDV.n)
            W("pe", cAC, cAC.n)
            ev_w1 = {}
            W("sy", cGP, ev_pc_box[0])
            W("sy", cPE, ev_attn_pe[1])
            W("sy", cDV, cDV.n)
            W("gp", cPE, ev_attn_pe[1])
            for hh in range(2):
                E("gp", lambda e, hh=hh: e.dma_start(
                    out=w1f[:, hh * 4:(hh + 1) * 4, :],
                    in_=w1_d[0].rearrange("(c p) f -> p c f", p=128)[
                        :, hh * 4:(hh + 1) * 4, 0:2048]), qGP, 16)
            ev_w1[(0, 0)] = qGP.n
            ev_scat = ev_wout_add[b]
            ev_gelu = {}
            ev_psA_free = [0, 0, 0, 0]
            ev_psB_free = [0, 0, 0, 0]
            ev_w2_rd = [0, 0]
            for e in range(NE):
                par = e % 2
                for fh in range(2):
                    W("pe", qGP, ev_w1[(e, fh)])
                    W("pe", cDV, ev_prep_xgT[(b, e)])
                    for ft in range(16):
                        sl = ft % 2
                        if (e, fh, ft - 2) in ev_gelu:
                            W("pe", cAC, ev_gelu[(e, fh, ft - 2)])
                        if ft < 2:
                            W("pe", cDV, ev_psA_free[sl])
                        for dc in range(DCC):
                            E("pe", lambda e_, ft=ft, dc=dc, sl=sl:
                              e_.matmul(
                                psA[:, sl * 512:(sl + 1) * 512],
                                lhsT=w1f[:, dc, ft * 128:(ft + 1) * 128],
                                rhs=XgT_v[:, dc, :], start=(dc == 0),
                                stop=(dc == DCC - 1)),
                              cPE if dc == DCC - 1 else None)
                        ev_h = cPE.n
                        W("ac", cPE, ev_h)
                        E("ac", lambda e_, ft=ft, sl=sl: e_.activation(
                            hT_v[:, ft, :], psA[:, sl * 512:(sl + 1) * 512],
                            AF.Gelu_apprx_tanh), cAC)
                        ev_gelu[(e, fh, ft)] = cAC.n
                    ev_w1_done = cPE.n
                    W("gp", cPE, ev_w1_done)
                    if fh == 0:
                        for hh in range(2):
                            E("gp", lambda e_, e=e, hh=hh: e_.dma_start(
                                out=w1f[:, hh * 4:(hh + 1) * 4, :],
                                in_=w1_d[e].rearrange(
                                    "(c p) f -> p c f", p=128)[
                                    :, hh * 4:(hh + 1) * 4, 2048:4096]),
                              qGP, 16)
                        ev_w1[(e, 1)] = qGP.n
                    elif e < NE - 1:
                        for hh in range(2):
                            E("gp", lambda e_, e=e, hh=hh: e_.dma_start(
                                out=w1f[:, hh * 4:(hh + 1) * 4, :],
                                in_=w1_d[e + 1].rearrange(
                                    "(c p) f -> p c f", p=128)[
                                    :, hh * 4:(hh + 1) * 4, 0:2048]), qGP, 16)
                        ev_w1[(e + 1, 0)] = qGP.n
                    if fh == 0 and e < NE - 1:
                        gev = prep_a(b, e + 1, (e + 1) % 2, 0)
                    if fh == 1 and e < NE - 1:
                        prep_b(b, e + 1, gev)
                        W("pe", cDV, ev_prep_xgT[(b, e + 1)])
                    W("pe", cDV, max(ev_psB_free))
                    W("pe", cAC, ev_gelu[(e, fh, 15)])
                    for g in range(4):
                        pr = g % 2
                        W("sy", cPE, ev_w2_rd[pr])
                        E("sy", lambda e_, e=e, fh=fh, g=g, pr=pr: e_.dma_start(
                            out=w2g_v[:, pr, :, :],
                            in_=w2b_s[e].rearrange("(c p) d -> p c d", p=128)[
                                :, fh * 16 + g * 4:fh * 16 + (g + 1) * 4, :]),
                          qSY, 16)
                        ev_w2 = qSY.n
                        W("pe", qSY, ev_w2)
                        for fc in range(4):
                            fcg = g * 4 + fc
                            W("pe", cAC, ev_gelu[(e, fh, fcg)])
                            for st in range(4):
                                for dh in range(2):
                                    ps = psA if dh == 0 else psB
                                    first = (g == 0 and fc == 0)
                                    last = (g == 3 and fc == 3)
                                    E("pe", lambda e_, fcg=fcg, st=st, dh=dh,
                                      pr=pr, fc=fc, first=first, last=last,
                                      ps=ps: e_.matmul(
                                        ps[:, st * 512:(st + 1) * 512],
                                        lhsT=hT_v[:, fcg,
                                                  st * 128:(st + 1) * 128],
                                        rhs=w2g_v[:, pr, fc,
                                                  dh * 512:(dh + 1) * 512],
                                        start=first, stop=last),
                                      cPE if (st == 3 and dh == 1) else None)
                        ev_w2_rd[pr] = cPE.n
                    ev_w2pe = cPE.n
                    W("dv", cPE, ev_w2pe)
                    for st in range(4):
                        for dh in range(2):
                            ps = psA if dh == 0 else psB
                            if fh == 0:
                                E("dv", lambda e_, st=st, dh=dh, ps=ps, par=par:
                                  e_.tensor_scalar(
                                    gOut[:, st, dh * 512:(dh + 1) * 512],
                                    ps[:, st * 512:(st + 1) * 512],
                                    gco[:, 16 + par * 4 + st:
                                        17 + par * 4 + st], None, ALU.mult),
                                  cDV)
                            else:
                                E("dv", lambda e_, st=st, dh=dh, ps=ps, par=par:
                                  e_.scalar_tensor_tensor(
                                    gOut[:, st, dh * 512:(dh + 1) * 512],
                                    ps[:, st * 512:(st + 1) * 512],
                                    gco[:, 16 + par * 4 + st:
                                        17 + par * 4 + st],
                                    gOut[:, st, dh * 512:(dh + 1) * 512],
                                    ALU.mult, ALU.add), cDV)
                            if dh == 0:
                                ev_psA_free[st] = cDV.n
                            else:
                                ev_psB_free[st] = cDV.n
                    ev_gout = cDV.n
                W("gp", cDV, ev_gout)
                W("gp", qGP, ev_scat)
                for st in range(4):
                    E("gp", lambda e_, st=st, b=b, e=e: e_.indirect_dma_start(
                        out=out_d.rearrange("b n d -> (b n) d"),
                        out_offset=IndirectOffsetOnAxis(
                            mcol_i[(b * 8 + e) * 4 + st].ap()[:, :], 0),
                        in_=gOut[:, st, :], in_offset=None, compute_op=ALU.add),
                      qGP, 16)
                ev_scat = qGP.n
            return ev_scat

        if stage >= 1:
            r0 = p1(0, 0)
        if stage >= 6:
            W("gp", cPE, cPE.n)
            for e in range(NE):
                E("gp", lambda e_, e=e: e_.dma_start(
                    out=w2b_s[e], in_=w2_d[e]), qGP, 16)
            W("gp", qGP, qGP.n)
            E("gp", lambda e_: e_.memset(zbias.ap()[:, :], 0.0), cGP)
            ev_pc_box[0] = cGP.n
        if stage >= 2:
            attn_core(0, *r0)
        if stage >= 3:
            r1 = p1(1, ev_attn_pe[0])
        if stage >= 4:
            FEED.extend(bisect())
        if stage >= 5:
            attn_core(1, *r1)
        if stage >= 4:
            while FEED:
                FEED.pop(0)()
            ev_bis = cDV.n
            ev_dst0 = build_dst(0, ev_bis)
            gev0 = prep_a(0, 0, 0, ev_dst0)
            prep_b(0, 0, gev0)
        if stage >= 6:
            ffn(0)
        if stage >= 7:
            ev_dst1 = build_dst(1, ev_bis)
            gev1 = prep_a(1, 0, 0, ev_dst1)
            prep_b(1, 0, gev1)
            ffn(1)

        W("sy", qGP, qGP.n)
        W("sy", qSY, qSY.n)

        blk = ctx.enter_context(nc.Block())
        blk.tensor(lambda t: [f(t) for f in P["pe"]])
        blk.vector(lambda v: [f(v) for f in P["dv"]])
        blk.scalar(lambda s: [f(s) for f in P["ac"]])
        blk.gpsimd(lambda g: [f(g) for f in P["gp"]])
        blk.sync(lambda s: [f(s) for f in P["sy"]])
    return nc


_NC_CACHE = [None]


def kernel_stage(stage, ins):
    nc = build_program(stage)
    return _run(nc, ins)


def _run(nc, kw):
    x = np.ascontiguousarray(np.asarray(kw["L0_attn_x"], dtype=np.float32))
    shared = {
        "gamma": np.ascontiguousarray(
            np.asarray(kw["L0_attn_gamma"], np.float32).reshape(1, D)),
        "Wq": np.ascontiguousarray(np.asarray(kw["L0_attn_Wq"], np.float32)),
        "Wkv": np.ascontiguousarray(np.asarray(kw["L0_attn_Wkv"], np.float32)),
        "qsc": np.ascontiguousarray(
            np.asarray(kw["L0_attn_q_scale"], np.float32).reshape(1, DH)),
        "ksc": np.ascontiguousarray(
            np.asarray(kw["L0_attn_k_scale"], np.float32).reshape(1, DH)),
        "nullkv": np.ascontiguousarray(
            np.asarray(kw["L0_attn_null_kv"], np.float32)),
        "Wout": np.ascontiguousarray(np.asarray(kw["L0_attn_Wout"], np.float32)),
        "gate_w": np.ascontiguousarray(
            np.asarray(kw["L0_ffn_gate_w"], np.float32)),
        "w1": np.ascontiguousarray(np.asarray(kw["L0_ffn_w1"], np.float32)),
        "w2": np.ascontiguousarray(np.asarray(kw["L0_ffn_w2"], np.float32)),
    }
    in_maps = []
    for c in range(8):
        m = dict(shared)
        m["x"] = x[c * NB:(c + 1) * NB]
        in_maps.append(m)
    res = run_bass_kernel_spmd(nc, in_maps, core_ids=list(range(8)))
    return np.concatenate([r["y"] for r in res.results], axis=0)


def kernel(L0_attn_x, L0_attn_gamma, L0_attn_Wq, L0_attn_Wkv, L0_attn_q_scale,
           L0_attn_k_scale, L0_attn_null_kv, L0_attn_Wout, L0_ffn_gate_w,
           L0_ffn_w1, L0_ffn_w2):
    if _NC_CACHE[0] is None:
        _NC_CACHE[0] = build_program()
    return _run(_NC_CACHE[0], dict(
        L0_attn_x=L0_attn_x, L0_attn_gamma=L0_attn_gamma, L0_attn_Wq=L0_attn_Wq,
        L0_attn_Wkv=L0_attn_Wkv, L0_attn_q_scale=L0_attn_q_scale,
        L0_attn_k_scale=L0_attn_k_scale, L0_attn_null_kv=L0_attn_null_kv,
        L0_attn_Wout=L0_attn_Wout, L0_ffn_gate_w=L0_ffn_gate_w,
        L0_ffn_w1=L0_ffn_w1, L0_ffn_w2=L0_ffn_w2))
